# version 13
# baseline (speedup 1.0000x reference)
import numpy as np
import concourse.bass as bass
import concourse.mybir as mybir
from concourse.bass_utils import run_bass_kernel_spmd
from contextlib import ExitStack

F32 = mybir.dt.float32
BF16 = mybir.dt.bfloat16
ALU = mybir.AluOpType
AFT = mybir.ActivationFunctionType
AX = mybir.AxisListType

M23 = 1.5 * 2.0**23
M52 = 1.5 * 2.0**52
M17 = 1.5 * 2.0**17

S, E = 4096, 1024
SEQ = 2048
INNER = 4096
NRC = 32
FSC = 256
NFC = SEQ // FSC
NOC = INNER // 128
NEC = E // 128


class Sem:
    __slots__ = ("h", "n")

    def __init__(self, es, nc, name):
        self.h = es.enter_context(nc.semaphore(name))
        self.n = 0

    def inc(self, instr, k=1):
        instr.then_inc(self.h, k)
        self.n += k
        return self.n

    def w(self, eng, mark=None):
        m = self.n if mark is None else mark
        if m > 0:
            eng.wait_ge(self.h, m)


class DSem:

    def __init__(self, es, nc, name, k):
        self.s = [Sem(es, nc, f"{name}_{i}") for i in range(k)]
        self.k = k

    def base(self):
        return [s.n for s in self.s]

    def inc(self, instr, idx, amt=16):
        return self.s[idx % self.k].inc(instr, amt)

    def w(self, eng, idx, base, per=16):
        sl = idx % self.k
        eng.wait_ge(self.s[sl].h, base[sl] + per * (idx // self.k + 1))


def build_program(dbg=False, stage='full'):
    nc = bass.Bass("TRN2", target_bir_lowering=False)

    x_full = nc.dram_tensor("x_full", [S, E], F32, kind="ExternalInput").ap()
    pw_h = nc.dram_tensor("pw", [E, E], F32, kind="ExternalInput").ap()
    awp_h = nc.dram_tensor("awp", [8 * 128, 64], F32, kind="ExternalInput").ap()
    vwp_h = nc.dram_tensor("vwp", [8 * 128, 64], F32, kind="ExternalInput").ap()
    owT_h = nc.dram_tensor("owT", [E, E], F32, kind="ExternalInput").ap()
    ob_h = nc.dram_tensor("ob", [1, E], F32, kind="ExternalInput").ap()
    wgT_h = nc.dram_tensor("wgT", [E, INNER], F32, kind="ExternalInput").ap()
    wuT_h = nc.dram_tensor("wuT", [E, INNER], F32, kind="ExternalInput").ap()
    wdT_h = nc.dram_tensor("wdT", [INNER, E], F32, kind="ExternalInput").ap()
    id32_h = nc.dram_tensor("id32", [128, 128], F32, kind="ExternalInput").ap()
    ones_h = nc.dram_tensor("onesb", [1, 128], BF16, kind="ExternalInput").ap()
    idbf_h = nc.dram_tensor("idbf", [128, 128], BF16, kind="ExternalInput").ap()
    y_h = nc.dram_tensor("y", [SEQ, E], F32, kind="ExternalOutput").ap()

    wgq_h = nc.dram_tensor("wgq", [128, 32, 8, 128], BF16, kind="Internal").ap()
    wuq_h = nc.dram_tensor("wuq", [128, 32, 8, 128], BF16, kind="Internal").ap()
    wdq_h = nc.dram_tensor("wdq", [128, 8, 32, 128], BF16, kind="Internal").ap()
    dbg_h = nc.dram_tensor("dbg", [128, 4096], F32,
                           kind="ExternalOutput").ap() if dbg else None
    dbg_aps = {}

    es = ExitStack()
    sb = lambda n, s, d: es.enter_context(nc.sbuf_tensor(n, s, d))
    sem = lambda n: Sem(es, nc, n)

    x_tm = sb("x_tm", [128, 16 * E], F32)
    id32 = sb("id32s", [128, 128], F32)
    idbf = sb("idbfs", [128, 128], BF16)
    onesb = sb("onesbs", [1, 128], BF16)
    eps_t = sb("eps_t", [128, 1], F32)
    g_t = sb("g_t", [128, 1], F32)
    h_t = sb("h_t", [128, 1], F32)
    c5217 = sb("c5217", [128, 1], F32)
    sc_t = sb("sc_t", [128, 8], F32)
    mkrow = sb("mkrow", [1, 32], F32)
    mh_row = sb("mh_row", [1, 512], F32)
    ones_row = sb("ones_row", [1, 128], F32)
    onesb_t = sb("onesb_t", [128, 128], BF16)
    mq_row = sb("mq_row", [1, 128], F32)
    wbar = sb("wbar", [1, 4], F32)
    obhi = sb("obhi", [1, E], BF16)
    oblo = sb("oblo", [1, E], BF16)

    s_dc = sem("s_dc"); s_dx = sem("s_dx"); s_g = sem("s_g"); s_ob = sem("s_ob")
    d_w = DSem(es, nc, "d_w", 2); v_wr = sem("v_wr"); t_wt = sem("t_wt"); v_ws = sem("v_ws")
    g_wb = sem("g_wb"); v_wq = sem("v_wq"); o_wq = DSem(es, nc, "o_wq", 2); s_gp = sem("s_gp")
    d_pw = DSem(es, nc, "d_pw", 2); v_pw = sem("v_pw")
    d_xa = DSem(es, nc, "d_xa", 2); a_sq = sem("a_sq"); v_st = sem("v_st"); a_rs = sem("a_rs")
    v_ln = sem("v_ln"); t_lt = sem("t_lt"); v_ls = sem("v_ls"); t_pj = sem("t_pj")
    v_he = sem("v_he"); v_mh = sem("v_mh"); t_mh = sem("t_mh"); g_mh = sem("g_mh")
    d_aw = sem("d_aw"); v_aw = sem("v_aw")
    t_v = sem("t_v"); v_ve = sem("v_ve")
    t_q = sem("t_q"); v_qe = sem("v_qe"); t_ct = sem("t_ct"); v_cd = sem("v_cd")
    g_cb = sem("g_cb")
    t_s = sem("t_s"); a_e = sem("a_e"); v_w = sem("v_w"); t_av = sem("t_av")
    t_cb = sem("t_cb"); v_as = sem("v_as"); d_pk = Sem(es, nc, "d_pk")
    v_nm = sem("v_nm"); t_at = sem("t_at"); v_af = sem("v_af"); t_op = sem("t_op")
    v_res = sem("v_res")
    a_f1 = sem("a_f1"); v_f1 = sem("v_f1"); a_f2 = sem("a_f2"); v_f2 = sem("v_f2")
    v_fq = sem("v_fq"); t_xt = sem("t_xt"); v_xe = sem("v_xe")
    t_fr = sem("t_fr"); v_fr = sem("v_fr"); t_bc = sem("t_bc"); v_bc = sem("v_bc")
    d_fw = DSem(es, nc, "d_fw", 2); t_fg = sem("t_fg"); v_gr = sem("v_gr"); a_si = sem("a_si")
    v_gu = sem("v_gu"); t_mx = sem("t_mx"); v_mx = sem("v_mx")
    t_sr = sem("t_sr"); v_sr = sem("v_sr"); t_sb = sem("t_sb"); v_sb2 = sem("v_sb2")
    t_fo = sem("t_fo"); v_fo = sem("v_fo")
    v_gq = sem("v_gq"); d_fd = DSem(es, nc, "d_fd", 2); t_dn = sem("t_dn"); v_de = sem("v_de")
    t_dt = sem("t_dt"); v_dt = sem("v_dt"); d_y = sem("d_y")

    blk = es.enter_context(nc.Block())

    @blk.sync
    def _(sy):
        s_dc.inc(sy.dma_start(out=id32[:], in_=id32_h), 16)
        s_dc.inc(sy.dma_start(out=onesb[:], in_=ones_h), 16)
        s_dc.inc(sy.dma_start(out=idbf[:], in_=idbf_h), 16)
        for t in range(16):
            s_dx.inc(sy.dma_start(out=x_tm[:, t * E:(t + 1) * E],
                                  in_=x_full[t * 128:(t + 1) * 128, :]), 16)

    @blk.vector
    def _(v):
        v.memset(eps_t[:], 1e-5)
        v.memset(ones_row[:], 1.0)
        v.memset(onesb_t[:], 1.0)
        v.memset(sc_t[:, 0:1], 0.75)
        v.drain()
        v.tensor_scalar(sc_t[:, 1:2], sc_t[:, 0:1], M23, -M23, ALU.add, ALU.add)
        v.drain()
        v.tensor_sub(sc_t[:, 2:3], sc_t[:, 1:2], sc_t[:, 0:1])
        v.drain()
        v.tensor_scalar(g_t[:], sc_t[:, 2:3], 4.0, -1.0, ALU.mult, ALU.add)
        v.drain()
        v.tensor_single_scalar(h_t[:], g_t[:], 1.0, ALU.add)
        v.memset(c5217[:], M52)
        v.drain()
        s_g.inc(v.tensor_single_scalar(c5217[:], c5217[:], -M17, ALU.add))

    with ExitStack() as wes:
        wsb = lambda n, s, d: wes.enter_context(nc.sbuf_tensor(n, s, d))
        wraw = wsb("wraw", [128, 2 * 4096], F32)
        wt = wsb("wt", [128, 4096], F32)
        wn = wsb("wn", [128, 4096], F32)
        wqt = wsb("wqt", [128, 2 * 4096], BF16)
        asum = wsb("asum", [128, 33], F32)
        swbc = wsb("swbc", [128, 1], F32)
        obf = wsb("obf", [1, E], F32)
        wrow = wsb("wrow", [1, 128], F32)
        ps_wq = wes.enter_context(nc.psum_tensor("ps_wq", [128, 128], F32))

        @blk.sync
        def _(sy):
            s_dc.inc(sy.dma_start(out=obf[:], in_=ob_h), 16)

        @blk.vector
        def _(v):
            s_dc.w(v)
            v.tensor_copy(obhi[:], obf[:])
            v.drain()
            s_ob.inc(v.tensor_tensor(oblo[:], obf[:], obhi[:], ALU.subtract))

        specs = [(wgT_h, wgq_h, 8, 4096, 0), (wuT_h, wuq_h, 8, 4096, 1),
                 (wdT_h, wdq_h, 32, 1024, 2)]
        if stage == 'setup0':
            specs = []
        elif stage == 'wq1':
            specs = specs[:1]

        for si, (src, dst, nt, wdt, wi) in enumerate(specs):
            bdw, bvr = d_w.base(), v_wr.n

            @blk.sync
            def _(sy, src=src, nt=nt, wdt=wdt, bvr=bvr):
                v_wq.w(sy)
                for t in range(nt):
                    if t >= 2:
                        v_wr.w(sy, bvr + t - 1)
                    d_w.inc(sy.dma_start(
                        out=wraw[:, (t % 2) * 4096:(t % 2) * 4096 + wdt],
                        in_=src[t * 128:(t + 1) * 128, :]), t)

            @blk.vector
            def _(v, nt=nt, wdt=wdt, bdw=bdw):
                for t in range(nt):
                    d_w.w(v, t, bdw)
                    v_wr.inc(v.reduce_sum(
                        asum[:, t:t + 1],
                        wraw[:, (t % 2) * 4096:(t % 2) * 4096 + wdt],
                        axis=AX.X, apply_absolute_value=True))
                v.drain()
                v.reduce_sum(asum[:, 32:33], asum[:, 0:nt], axis=AX.X)
                v_wr.inc(v.drain())

            @blk.tensor
            def _(te):
                v_wr.w(te)
                s_dc.w(te)
                t_wt.inc(te.transpose(ps_wq[0:1, 0:128], asum[:, 32:33], id32[:]))

            @blk.vector
            def _(v, nt=nt, wdt=wdt, wi=wi):
                t_wt.w(v)
                g_wb.w(v)
                v.reduce_sum(sc_t[0:1, 4:5], ps_wq[0:1, 0:128], axis=AX.X)
                v.drain()
                v.tensor_scalar(wbar[0:1, wi:wi + 1], sc_t[0:1, 4:5],
                                1.0 / (nt * 128 * wdt), 1e-5, ALU.mult, ALU.max)
                v.drain()
                v_ws.inc(v.reciprocal(sc_t[0:1, 5:6], wbar[0:1, wi:wi + 1]))

            @blk.vector
            def _(v):
                v_ws.w(v)
                v_ws.inc(v.tensor_single_scalar(wrow[0:1, 0:128], ones_row[0:1, :],
                                                sc_t[0:1, 5:6], ALU.mult))

            @blk.tensor
            def _(te):
                v_ws.w(te)
                t_wt.inc(te.transpose(ps_wq[:, 0:1], wrow[0:1, 0:128],
                                      id32[0:1, 0:1]))

            @blk.vector
            def _(v):
                t_wt.w(v)
                g_wb.inc(v.tensor_copy(swbc[:], ps_wq[:, 0:1]))

            bdw2, bvq, boq = d_w.base(), v_wq.n, o_wq.base()

            @blk.sync
            def _(sy, src=src, nt=nt, wdt=wdt, bvq=bvq):
                v_wr.w(sy)
                for t in range(nt):
                    if t >= 2:
                        v_wq.w(sy, bvq + t - 1)
                    d_w.inc(sy.dma_start(
                        out=wraw[:, (t % 2) * 4096:(t % 2) * 4096 + wdt],
                        in_=src[t * 128:(t + 1) * 128, :]), t)

            @blk.vector
            def _(v, nt=nt, wdt=wdt, bdw2=bdw2, boq=boq):
                g_wb.w(v)
                for i, s_ in enumerate(o_wq.s):
                    if boq[i] > 0:
                        v.wait_ge(s_.h, boq[i])
                for t in range(nt):
                    d_w.w(v, t, bdw2)
                    if t >= 2:
                        o_wq.w(v, t - 2, boq)
                    buf = wraw[:, (t % 2) * 4096:(t % 2) * 4096 + wdt]
                    qb = wqt[:, (t % 2) * 4096:(t % 2) * 4096 + wdt]
                    v.tensor_single_scalar(wt[:, 0:wdt], buf, swbc[:, 0:1], ALU.mult)
                    v.drain()
                    v.tensor_single_scalar(wn[:, 0:wdt], wt[:, 0:wdt], -0.5, ALU.is_le)
                    v.drain()
                    v_wq.inc(v.scalar_tensor_tensor(
                        qb, wt[:, 0:wdt], 0.5, wn[:, 0:wdt], ALU.is_ge, ALU.subtract))
                    v.drain()

            @blk.gpsimd
            def _(g, dst=dst, nt=nt, wdt=wdt, bvq=bvq):
                for t in range(nt):
                    v_wq.w(g, bvq + t + 1)
                    o_wq.inc(g.dma_start(
                        out=dst[:, :, t, :],
                        in_=wqt[:, (t % 2) * 4096:(t % 2) * 4096 + wdt]
                        .rearrange("p (o c) -> p o c", c=128)), t)

    def _early_out():
        @blk.gpsimd
        def _(gp):
            for s_ in o_wq.s:
                if s_.n > 0:
                    gp.wait_ge(s_.h, s_.n)
            for t in range(2):
                d_y.inc(gp.dma_start(out=y_h[t * 128:(t + 1) * 128, :],
                                     in_=x_tm[:, t * E:(t + 1) * E]), 16)
            gp.wait_ge(d_y.h, d_y.n)
        es.close()
        return nc

    if stage in ('wqonly', 'setup0', 'wq1'):
        return _early_out()

    with ExitStack() as aes:
        asb = lambda n, s, d: aes.enter_context(nc.sbuf_tensor(n, s, d))
        h_hi = asb("h_hi", [128, 4 * 4096], BF16)
        h_lo = asb("h_lo", [128, 4 * 4096], BF16)
        vA_hi = asb("vA_hi", [128, NRC * 65], BF16)
        vA_lo = asb("vA_lo", [128, NRC * 65], BF16)
        vB_hi = asb("vB_hi", [128, NRC * 65], BF16)
        vB_lo = asb("vB_lo", [128, NRC * 65], BF16)
        stat = asb("stat", [128, 16], F32)
        mh_t = asb("mh_t", [128, 4], F32)
        mh_r = asb("mh_r", [4, 2], F32)

        pb_q = aes.enter_context(nc.psum_tensor("pb_q", [128, 512], F32))
        pb_o = aes.enter_context(nc.psum_tensor("pb_o", [128, 512], F32))
        pb_s = [aes.enter_context(nc.psum_tensor(f"pb_s{i}", [128, 512], F32))
                for i in range(2)]
        pb_av = [aes.enter_context(nc.psum_tensor(f"pb_av{i}", [128, 512], F32))
                 for i in range(4)]

        def emit_hbuild(g, hes):
            hsb = lambda n, s, d: hes.enter_context(nc.sbuf_tensor(n + str(g), s, d))
            pw_hi = hsb("pw_hi", [128, 8 * 512], BF16)
            pw_lo = hsb("pw_lo", [128, 8 * 512], BF16)
            pw_f = hsb("pw_f", [128, 2 * 512], F32)
            xa = hsb("xa", [128, 2 * E], F32)
            xsq = hsb("xsq", [128, E], F32)
            lnt = hsb("lnt", [128, E], F32)
            ln_hi = hsb("ln_hi", [128, 4 * E], BF16)
            ln_lo = hsb("ln_lo", [128, 4 * E], BF16)
            bdpw, bvpw = d_pw.base(), v_pw.n
            bdxa, basq, bvst, bars, bvln = d_xa.base(), a_sq.n, v_st.n, a_rs.n, v_ln.n
            btlt, bvls, btpj, bvhe = t_lt.n, v_ls.n, t_pj.n, v_he.n
            prev_res = v_res.n
            prev_nm = v_nm.n
            prev_af_h = v_af.n

            @blk.sync
            def _(sy):
                v_res.w(sy, prev_res)
                for s_ in o_wq.s:
                    if s_.n > 0:
                        sy.wait_ge(s_.h, s_.n)
                v_wq.w(sy)
                for ic in range(8):
                    if ic >= 2:
                        v_pw.w(sy, bvpw + ic - 1)
                    d_pw.inc(sy.dma_start(
                        out=pw_f[:, (ic % 2) * 512:(ic % 2 + 1) * 512],
                        in_=pw_h[ic * 128:(ic + 1) * 128, 512 * g:512 * (g + 1)]), ic)
                for rc in range(NRC):
                    if rc >= 2:
                        v_ln.w(sy, bvln + rc - 1)
                    d_xa.inc(sy.dma_start(
                        out=xa[:, (rc % 2) * E:(rc % 2 + 1) * E],
                        in_=x_full[rc * 128:(rc + 1) * 128, :]), rc)

            @blk.vector
            def _(v):
                t_pj.w(v, btpj)
                for ic in range(8):
                    d_pw.w(v, ic, bdpw)
                    src = pw_f[:, (ic % 2) * 512:(ic % 2 + 1) * 512]
                    v.tensor_copy(pw_hi[:, ic * 512:(ic + 1) * 512], src)
                    v.drain()
                    v_pw.inc(v.tensor_tensor(
                        pw_lo[:, ic * 512:(ic + 1) * 512], src,
                        pw_hi[:, ic * 512:(ic + 1) * 512], ALU.subtract))

            def emit_sq(sc, rc):
                c = (rc % 2) * 8
                d_xa.w(sc, rc, bdxa)
                if rc >= 2:
                    v_st.w(sc, bvst + rc - 1)
                a_sq.inc(sc.activation(
                    xsq[:, 0:E],
                    xa[:, (rc % 2) * E:(rc % 2 + 1) * E], AFT.Square,
                    accum_out=stat[:, c + 0:c + 1]))
                sc.drain()

            def emit_sqrt(sc, rc):
                c = (rc % 2) * 8
                v_st.w(sc, bvst + rc + 1)
                if rc >= 2:
                    v_ln.w(sc, bvln + rc - 1)
                a_rs.inc(sc.activation(
                    stat[:, c + 5:c + 6], stat[:, c + 4:c + 5], AFT.Sqrt,
                    bias=eps_t[:, 0:1], scale=1.0))

            @blk.scalar
            def _(sc):
                s_g.w(sc, 1)
                v_res.w(sc, prev_res)
                for s_ in o_wq.s:
                    if s_.n > 0:
                        sc.wait_ge(s_.h, s_.n)
                v_wq.w(sc)
                for rc in range(NRC):
                    emit_sq(sc, rc)
                    if rc >= 1:
                        emit_sqrt(sc, rc - 1)
                emit_sqrt(sc, NRC - 1)

            def emit_stats2(v, rc):
                c = (rc % 2) * 8
                a_sq.w(v, basq + rc + 1)
                if rc >= 2:
                    a_rs.w(v, bars + rc - 1)
                if rc >= 2:
                    v_ln.w(v, bvln + rc - 1)
                v.reduce_sum(stat[:, c + 1:c + 2],
                             xa[:, (rc % 2) * E:(rc % 2 + 1) * E], axis=AX.X)
                v.drain()
                v.tensor_single_scalar(stat[:, c + 2:c + 3], stat[:, c + 1:c + 2],
                                       1.0 / E, ALU.mult)
                v.drain()
                v.tensor_tensor(stat[:, c + 3:c + 4], stat[:, c + 2:c + 3],
                                stat[:, c + 2:c + 3], ALU.mult)
                v.drain()
                v_st.inc(v.scalar_tensor_tensor(
                    stat[:, c + 4:c + 5], stat[:, c + 0:c + 1], 1.0 / E,
                    stat[:, c + 3:c + 4], ALU.mult, ALU.subtract))
                v.drain()

            def emit_ln(v, rc):
                c = (rc % 2) * 8
                a_rs.w(v, bars + rc + 1)
                if rc >= 1:
                    t_lt.w(v, btlt + 8 * rc)
                v.reciprocal(stat[:, c + 6:c + 7], stat[:, c + 5:c + 6])
                v.drain()
                v_ln.inc(v.tensor_scalar(
                    lnt[:], xa[:, (rc % 2) * E:(rc % 2 + 1) * E],
                    stat[:, c + 2:c + 3], stat[:, c + 6:c + 7],
                    ALU.subtract, ALU.mult))
                v.drain()



            ltp = [pb_q, pb_o, pb_s[0], pb_s[1]]

            def emit_lt(te, rc):
                v_ln.w(te, bvln + rc + 1)
                for ib in range(8):
                    tid = rc * 8 + ib
                    if tid >= 4:
                        v_ls.w(te, bvls + tid - 3)
                    t_lt.inc(te.transpose(
                        ltp[ib % 4][:, 0:128],
                        lnt[:, ib * 128:(ib + 1) * 128], id32[:]))

            def emit_pj(te, rcq):
                v_ls.w(te, bvls + 32 * rcq + 32)
                for lc in range(4):
                    pid = rcq * 4 + lc
                    pp = pb_av[lc]
                    if pid >= 4:
                        v_he.w(te, bvhe + pid - 3)
                    first = True
                    last = None
                    for (wop, rop) in ((pw_hi, ln_hi), (pw_hi, ln_lo), (pw_lo, ln_hi)):
                        for ic in range(8):
                            last = te.matmul(
                                pp[:, 0:512],
                                wop[:, ic * 512 + lc * 128:ic * 512 + (lc + 1) * 128],
                                rop[:, ic * 512:(ic + 1) * 512],
                                start=first, stop=(wop is pw_lo and ic == 7))
                            first = False
                    t_pj.inc(last)

            def emit_ls(v, rc):
                for ib in range(8):
                    tid = rc * 8 + ib
                    t_lt.w(v, btlt + tid + 1)
                    if ib == 0 and rc >= 4:
                        t_pj.w(v, btpj + 4 * (rc // 4))
                    src = ltp[ib % 4][:, 0:128]
                    dst = ib * 512 + (rc % 4) * 128
                    v.tensor_copy(ln_hi[:, dst:dst + 128], src)
                    v.drain()
                    v_ls.inc(v.tensor_tensor(
                        ln_lo[:, dst:dst + 128], src,
                        ln_hi[:, dst:dst + 128], ALU.subtract))
                    v.drain()

            def emit_he(v, rcq):
                for lc in range(4):
                    pid = rcq * 4 + lc
                    t_pj.w(v, btpj + pid + 1)
                    pp = pb_av[lc]
                    dst = lc * 4096 + rcq * 512
                    v.tensor_copy(h_hi[:, dst:dst + 512], pp[:, 0:512])
                    v.drain()
                    v_he.inc(v.tensor_tensor(
                        h_lo[:, dst:dst + 512], pp[:, 0:512],
                        h_hi[:, dst:dst + 512], ALU.subtract))
                    v.drain()

            @blk.tensor
            def _(te):
                s_dc.w(te)
                v_res.w(te, prev_res)
                v_nm.w(te, prev_nm)
                v_mh.w(te)
                g_cb.w(te)
                v_pw.w(te, bvpw + 8)
                v_ve.w(te)
                t_av.w(te)
                v_af.w(te, prev_af_h)
                for rc in range(NRC):
                    if rc >= 4 and rc % 4 == 0:
                        emit_pj(te, rc // 4 - 1)
                    emit_lt(te, rc)
                emit_pj(te, NRC // 4 - 1)

            @blk.vector
            def _(v):
                for rc in range(NRC + 3):
                    if rc < NRC:
                        emit_stats2(v, rc)
                    if 1 <= rc < NRC + 1:
                        emit_ln(v, rc - 1)
                    if 1 <= rc < NRC + 1:
                        emit_ls(v, rc - 1)
                    if rc >= 6 and (rc - 6) % 4 == 0:
                        emit_he(v, (rc - 6) // 4)

            @blk.vector
            def _(v):
                v_he.w(v, bvhe + (NRC // 4) * 4)
                for lc in range(4):
                    v.reduce_max(mh_t[:, lc:lc + 1], h_hi[:, lc * 4096:(lc + 1) * 4096],
                                 axis=AX.X, apply_absolute_value=True)
                v.drain()
                v_mh.inc(v.tensor_single_scalar(mh_t[:], mh_t[:], 1.01, ALU.mult))

            @blk.gpsimd
            def _(gp, g=g):
                v_mh.w(gp)
                if g >= 1:
                    v_mh.w(gp)
                g_mh.inc(gp.dma_start(out=mh_row[0:1, 0:512], in_=mh_t[:, 0:4]), 16)

            @blk.vector
            def _(v, g=g):
                g_mh.w(v)
                mv = mh_row.ap().rearrange("o (p l) -> o l p", l=4)
                for lc in range(4):
                    for j in range(2):
                        p = 4 * g + lc
                        v.reduce_max(
                            mkrow[0:1, 2 * p + j:2 * p + j + 1],
                            mv[0:1, lc:lc + 1, 64 * j:64 * (j + 1)], axis=AX.X)
                v_mh.inc(v.drain())

        def emit_attention(g, lp, T):
            (aw_f, vw_f, ow_f, aw_hi, aw_lo, vw_hi, vw_lo, ow_hi, ow_lo,
             q_hi, q_lo, w32, w_hi, w_lo, mq_t, cbias, csc, crow,
             attn_tm, attn_hi, attn_lo, rr_t, rhib, fbA_t, fbB_t, attnB_s) = T
            p = 4 * g + lp
            hofs = lp * 4096
            bdaw, bvaw = d_aw.n, v_aw.n
            btv, bvve = t_v.n, v_ve.n
            hdone = v_he.n
            prev_nm = v_nm.n
            prev_res = v_res.n
            prev_res2 = v_res.n
            prev_op = t_op.n
            prev_at = t_at.n
            prev_af = v_af.n
            prev_s = t_s.n
            prev_e = a_e.n

            @blk.sync
            def _(sy):
                t_pj.w(sy)
                v_aw.w(sy, bvaw)
                d_aw.inc(sy.dma_start(out=aw_f[:], in_=awp_h[p * 128:(p + 1) * 128, :]), 16)
                d_aw.inc(sy.dma_start(out=vw_f[:], in_=vwp_h[p * 128:(p + 1) * 128, :]), 16)
                d_aw.inc(sy.dma_start(out=ow_f[:], in_=owT_h[p * 128:(p + 1) * 128, :]), 16)

            @blk.vector
            def _(v):
                d_aw.w(v, bdaw + 48)
                t_q.w(v)
                t_v.w(v, btv)
                t_op.w(v, prev_op)
                t_av.w(v)
                v.tensor_copy(aw_hi[:], aw_f[:])
                v.tensor_copy(vw_hi[:], vw_f[:])
                v.tensor_copy(ow_hi[:], ow_f[:])
                v.drain()
                v.tensor_tensor(aw_lo[:], aw_f[:], aw_hi[:], ALU.subtract)
                v.tensor_tensor(vw_lo[:], vw_f[:], vw_hi[:], ALU.subtract)
                v.tensor_tensor(ow_lo[:], ow_f[:], ow_hi[:], ALU.subtract)
                v_aw.inc(v.drain())

            @blk.tensor
            def _(te):
                v_aw.w(te, bvaw + 1)
                v_he.w(te, hdone)
                v_af.w(te, prev_af)
                v_res.w(te, prev_res2)
                t_pj.w(te)
                for rc in range(NRC):
                    if rc >= 2:
                        v_ve.w(te, bvve + rc - 1)
                    last = None
                    for (tp, hp, pav) in (((0, 0), 0, pb_av[rc % 2]),
                                          ((64, 0), 64, pb_av[2 + rc % 2])):
                        first = True
                        for (hop, vop) in ((h_hi, vw_hi), (h_hi, vw_lo), (h_lo, vw_hi)):
                            last = te.matmul(
                                pav[:, 0:64],
                                hop[hp:hp + 64, hofs + rc * 128:hofs + (rc + 1) * 128],
                                vop[hp:hp + 64, :], start=first,
                                stop=(hop is h_lo), tile_position=tp)
                            first = False
                    t_v.inc(last)

            @blk.vector
            def _(v):
                for rc in range(NRC):
                    t_v.w(v, btv + rc + 1)
                    o = rc * 65
                    for (hp, vh, vl) in ((0, vA_hi, vA_lo), (64, vB_hi, vB_lo)):
                        pav = pb_av[(0 if hp == 0 else 2) + rc % 2]
                        v.tensor_copy(vh[:, o:o + 64], pav[:, 0:64])
                        v.memset(vh[:, o + 64:o + 65], 1.0)
                        v.drain()
                        v.tensor_tensor(vl[:, o:o + 64], pav[:, 0:64],
                                        vh[:, o:o + 64], ALU.subtract)
                        v.memset(vl[:, o + 64:o + 65], 0.0)
                    v_ve.inc(v.drain())

            def emit_resid(v, P):
                s_dx.w(v)
                for st in range(4):
                    for eh in range(2):
                        oid = 2 * st + eh
                        t_op.w(v, P['btop0'] + oid + 1)
                        pp = [pb_av[0], pb_av[1]][eh]
                        base = (P['scof'] // 128 + st) * E + eh * 512
                        xd = x_tm[:, base:base + 512]
                        v_res.inc(v.tensor_tensor(xd, xd, pp[:, 0:512], ALU.add))
                        v.drain()

            def emit_outproj(te, P):
                s_ob.w(te, 1)
                v_as.w(te, P['vas_mark'])
                v_af.w(te, P['vaf_mark'])
                for st in range(4):
                    for eh in range(2):
                        oid = 2 * st + eh
                        if oid >= 2:
                            v_res.w(te, P['bvres_r'] + oid - 1)
                        pp = [pb_av[0], pb_av[1]][eh]
                        ops = [(attn_hi, ow_hi, False), (attn_hi, ow_lo, False),
                               (attn_lo, ow_hi, p != 7)]
                        first = True
                        last = None
                        for (a, o, stop) in ops:
                            last = te.matmul(pp[:, 0:512],
                                             a[:, st * 128:(st + 1) * 128],
                                             o[:, eh * 512:(eh + 1) * 512],
                                             start=first, stop=stop)
                            first = False
                        if p == 7:
                            te.matmul(pp[:, 0:512], onesb[0:1, :],
                                      obhi[0:1, eh * 512:(eh + 1) * 512],
                                      start=False, stop=False)
                            last = te.matmul(pp[:, 0:512], onesb[0:1, :],
                                             oblo[0:1, eh * 512:(eh + 1) * 512],
                                             start=False, stop=True)
                        t_op.inc(last)

            def emit_qhead_te(scof_n, gcb_mark):
                hm = dict(btq=t_q.n, bvqe=v_qe.n, btct=t_ct.n, bvcd=v_cd.n,
                          scof=scof_n)

                @blk.tensor
                def _(te):
                    g_cb.w(te, gcb_mark)
                    v_ls.w(te)
                    v_aw.w(te, bvaw + 1)
                    v_he.w(te, hdone)
                    last = None
                    for (tp, hp) in (((0, 0), 0), ((64, 64), 64)):
                        first = True
                        for (a, hh) in ((aw_hi, h_hi), (aw_hi, h_lo), (aw_lo, h_hi)):
                            last = te.matmul(
                                pb_q[hp:hp + 64, 0:512],
                                a[hp:hp + 64, :],
                                hh[hp:hp + 64, hofs + scof_n:hofs + scof_n + 512],
                                start=first, stop=(a is aw_lo), tile_position=tp)
                            first = False
                    t_q.inc(last)
                return hm

            def emit_qhead_dve(hm, bts_gate):
                qo = ((hm['scof'] // 512) % 2) * 512

                @blk.vector
                def _(v):
                    t_q.w(v, hm['btq'] + 1)
                    t_s.w(v, bts_gate)
                    t_ct.w(v)
                    v.tensor_copy(q_hi[:, qo:qo + 512], pb_q[:, 0:512])
                    v.drain()
                    v.tensor_tensor(q_lo[:, qo:qo + 512], pb_q[:, 0:512],
                                    q_hi[:, qo:qo + 512], ALU.subtract)
                    v.reduce_max(mq_t[:], pb_q[:, 0:512], axis=AX.X,
                                 apply_absolute_value=True)
                    v_qe.inc(v.drain())

                @blk.gpsimd
                def _(gp):
                    v_qe.w(gp, hm['bvqe'] + 1)
                    v_cd.w(gp, hm['bvcd'])
                    t_ct.inc(gp.dma_start(out=mq_row[0:1, 0:128],
                                          in_=mq_t[:, 0:1]), 16)

                @blk.vector
                def _(v):
                    t_ct.w(v, hm['btct'] + 16)
                    g_mh.w(v)
                    t_cb.w(v)
                    v.reduce_max(csc[0:1, 0:1], mq_row[0:1, 0:64], axis=AX.X)
                    v.reduce_max(csc[0:1, 1:2], mq_row[0:1, 64:128], axis=AX.X)
                    v.drain()
                    v.tensor_tensor(csc[0:1, 2:3], csc[0:1, 0:1],
                                    mkrow[0:1, 2 * p:2 * p + 1], ALU.mult)
                    v.tensor_tensor(csc[0:1, 3:4], csc[0:1, 1:2],
                                    mkrow[0:1, 2 * p + 1:2 * p + 2], ALU.mult)
                    v.drain()
                    v.tensor_single_scalar(csc[0:1, 4:5], csc[0:1, 2:3], -1.0, ALU.mult)
                    v.tensor_single_scalar(csc[0:1, 5:6], csc[0:1, 3:4], -1.0, ALU.mult)
                    v.drain()
                    v.tensor_single_scalar(crow[0:1, 0:128], ones_row[0:1, :],
                                           csc[0:1, 4:5], ALU.mult)
                    v.tensor_single_scalar(crow[0:1, 128:256], ones_row[0:1, :],
                                           csc[0:1, 5:6], ALU.mult)
                    v_cd.inc(v.drain())

            pend = None
            curhead = emit_qhead_te(0, g_cb.n)
            emit_qhead_dve(curhead, t_s.n)
            for sc_i in range(4):
                scof = sc_i * 512
                scpar = sc_i % 2
                qo = scpar * 512
                hm = curhead
                bts, bae, btav, bvw = t_s.n, a_e.n, t_av.n, v_w.n
                bvnm, btat, bvaf = v_nm.n, t_at.n, v_af.n
                bgcb = g_cb.n

                @blk.tensor
                def _(te):
                    def emit_av(rc):
                        for hidx in range(2):
                            v_w.w(te, bvw + 2 * rc + hidx + 1)
                            if rc == 0:
                                v_af.w(te, bvaf)
                            wofs = (2 * (rc % 2) + hidx) * 512
                            vh, vl = [(vA_hi, vA_lo), (vB_hi, vB_lo)][hidx]
                            o = rc * 65
                            for k, (vv, ww) in enumerate(
                                    ((vh, w_hi), (vh, w_lo), (vl, w_hi))):
                                last = te.matmul(
                                    pb_av[2 + hidx][0:65, 0:512],
                                    vv[:, o:o + 65],
                                    ww[:, wofs:wofs + 512],
                                    start=(rc == 0 and k == 0),
                                    stop=(rc == NRC - 1 and k == 2))
                        t_av.inc(last)

                    v_ve.w(te, bvve + NRC)
                    v_qe.w(te, hm['bvqe'] + 1)
                    a_e.w(te, bae)
                    v_ls.w(te)
                    v_mh.w(te)
                    for rc in range(NRC):
                        for hidx in range(2):
                            if rc >= 1:
                                a_e.w(te, bae + 2 * (rc - 1) + hidx + 1)
                            first = True
                            last = None
                            for (kop, qop) in ((h_hi, q_hi), (h_hi, q_lo), (h_lo, q_hi)):
                                last = te.matmul(
                                    pb_s[hidx][:, 0:512],
                                    kop[hidx * 64:hidx * 64 + 64,
                                        hofs + rc * 128:hofs + (rc + 1) * 128],
                                    qop[hidx * 64:hidx * 64 + 64, qo:qo + 512],
                                    start=first, stop=(kop is h_lo),
                                    tile_position=(hidx * 64, 0))
                                first = False
                            t_s.inc(last)
                        if rc == 0:
                            v_cd.w(te, hm['bvcd'] + 1)
                            g_cb.w(te, bgcb)
                            t_cb.inc(te.transpose(pb_o[:, 0:1], crow[0:1, 0:128],
                                                  id32[0:1, 0:1]))
                            t_cb.inc(te.transpose(pb_q[:, 0:1], crow[0:1, 128:256],
                                                  id32[0:1, 0:1]))
                        if rc == 1 and pend is not None:
                            emit_outproj(te, pend)
                        if rc >= 1:
                            emit_av(rc - 1)
                    emit_av(NRC - 1)

                @blk.vector
                def _(v):
                    t_cb.w(v)
                    a_e.w(v, bae)
                    v.tensor_copy(cbias[:, 2 * scpar:2 * scpar + 1], pb_o[:, 0:1])
                    g_cb.inc(v.tensor_copy(cbias[:, 2 * scpar + 1:2 * scpar + 2],
                                           pb_q[:, 0:1]))

                @blk.scalar
                def _(sc):
                    g_cb.w(sc, bgcb + 1)
                    for rc in range(NRC):
                        for hidx in range(2):
                            t_s.w(sc, bts + 2 * rc + hidx + 1)
                            if rc >= 2:
                                v_w.w(sc, bvw + 2 * rc - 2)
                            elif bvw > 0:
                                v_w.w(sc, bvw + 2 * rc - 2)
                            slot = (2 * (rc % 2) + hidx) * 512
                            a_e.inc(sc.activation(
                                w32[:, slot:slot + 512], pb_s[hidx][:, 0:512], AFT.Exp,
                                bias=cbias[:, 2 * scpar + hidx:2 * scpar + hidx + 1],
                                scale=1.0 / 64))

                if sc_i < 3:
                    nexthead = emit_qhead_te(scof + 512, g_cb.n)

                if pend is not None:
                    @blk.vector
                    def _(v, P=pend):
                        emit_resid(v, P)

                @blk.vector
                def _(v):
                    for rc in range(NRC):
                        for hidx in range(2):
                            a_e.w(v, bae + 2 * rc + hidx + 1)
                            slot = (2 * (rc % 2) + hidx) * 512
                            v.tensor_copy(w_hi[:, slot:slot + 512],
                                          w32[:, slot:slot + 512])
                            v.drain()
                            v.tensor_tensor(w_lo[:, slot:slot + 512],
                                            w32[:, slot:slot + 512],
                                            w_hi[:, slot:slot + 512], ALU.subtract)
                            v_w.inc(v.drain())

                @blk.vector
                def _(v):
                    t_av.w(v, btav + NRC)
                    t_at.w(v, btat)
                    for hidx in range(2):
                        v.reciprocal(rr_t[64:65, hidx * 512:(hidx + 1) * 512],
                                     pb_av[2 + hidx][64:65, 0:512])
                    v.drain()
                    for hidx in range(2):
                        v.tensor_copy(rhib[64:65, hidx * 1024:hidx * 1024 + 512],
                                      rr_t[64:65, hidx * 512:(hidx + 1) * 512])
                    v.drain()
                    for hidx in range(2):
                        v.tensor_tensor(rhib[64:65, hidx * 1024 + 512:hidx * 1024 + 1024],
                                        rr_t[64:65, hidx * 512:(hidx + 1) * 512],
                                        rhib[64:65, hidx * 1024:hidx * 1024 + 512],
                                        ALU.subtract)
                    v_nm.inc(v.drain())

                @blk.tensor
                def _(te):
                    v_nm.w(te, bvnm + 1)
                    v_res.w(te)
                    for hidx in range(2):
                        te.matmul(pb_av[hidx][:, 0:512],
                                  onesb_t[64:65, 0:128],
                                  rhib[64:65, hidx * 1024:hidx * 1024 + 512],
                                  start=True, stop=False, tile_position=(64, 0))
                        t_at.inc(te.matmul(
                            pb_av[hidx][:, 0:512],
                            onesb_t[64:65, 0:128],
                            rhib[64:65, hidx * 1024 + 512:hidx * 1024 + 1024],
                            start=False, stop=True, tile_position=(64, 0)))

                @blk.vector
                def _(v):
                    t_at.w(v, btat + 1)
                    v.tensor_copy(fbA_t[:], pb_av[0][:, 0:512])
                    v_af.inc(v.drain())
                    t_at.w(v, btat + 2)
                    v.tensor_copy(fbB_t[:], pb_av[1][:, 0:512])
                    v_af.inc(v.drain())
                    v.tensor_tensor(attn_tm[0:64, 0:512], pb_av[2][0:64, 0:512],
                                    fbA_t[0:64, 0:512], ALU.mult)
                    v_af.inc(v.drain())
                    v.tensor_tensor(attnB_s[0:64, 0:512], pb_av[3][0:64, 0:512],
                                    fbB_t[0:64, 0:512], ALU.mult)
                    v_af.inc(v.drain())

                if sc_i < 3:
                    emit_qhead_dve(nexthead, bts)
                    curhead = nexthead

                @blk.gpsimd
                def _(gp):
                    v_af.w(gp, bvaf + 4)
                    d_pk.inc(gp.dma_start(out=attn_tm[64:128, 0:512],
                                          in_=attnB_s[0:64, 0:512]), 16)

                @blk.vector
                def _(v):
                    d_pk.w(v)
                    t_op.w(v)
                    v.tensor_copy(attn_hi[:], attn_tm[:])
                    v.drain()
                    v.tensor_tensor(attn_lo[:], attn_tm[:], attn_hi[:], ALU.subtract)
                    v_as.inc(v.drain())

                pend = dict(scof=scof, btop0=t_op.n, vas_mark=v_as.n,
                            vaf_mark=v_af.n - 2, bvres_r=v_res.n)

            if pend is not None:
                @blk.vector
                def _(v, P=pend):
                    emit_resid(v, P)

                @blk.tensor
                def _(te, P=pend):
                    emit_outproj(te, P)
                pend = None

        ngroups = 1 if stage in ('h0', 'a0') else 2
        for g in range(ngroups):
            with ExitStack() as hes:
                emit_hbuild(g, hes)
            if stage == 'h0':
                break
            with ExitStack() as bes:
                bsb = lambda n, s, d: bes.enter_context(
                    nc.sbuf_tensor(n + str(g), s, d))
                T = (bsb("aw_f", [128, 64], F32), bsb("vw_f", [128, 64], F32),
                     bsb("ow_f", [128, E], F32),
                     bsb("aw_hi", [128, 64], BF16), bsb("aw_lo", [128, 64], BF16),
                     bsb("vw_hi", [128, 64], BF16), bsb("vw_lo", [128, 64], BF16),
                     bsb("ow_hi", [128, E], BF16), bsb("ow_lo", [128, E], BF16),
                     bsb("q_hi", [128, 1024], BF16), bsb("q_lo", [128, 1024], BF16),
                     bsb("w32", [128, 4 * 512], F32),
                     bsb("w_hi", [128, 4 * 512], BF16),
                     bsb("w_lo", [128, 4 * 512], BF16),
                     bsb("mq_t", [128, 1], F32), bsb("cbias", [128, 4], F32),
                     bsb("csc", [1, 8], F32), bsb("crow", [1, 256], F32),
                     bsb("attn_tm", [128, 512], F32),
                     bsb("attn_hi", [128, 512], BF16),
                     bsb("attn_lo", [128, 512], BF16),
                     bsb("rr_t", [128, 1024], F32),
                     bsb("rhib", [128, 2048], BF16),
                     bsb("fbA_t", [128, 512], F32), bsb("fbB_t", [128, 512], F32),
                     bsb("attnB_s", [128, 512], F32))
                for lp in range(4 if stage != 'a0' else 1):
                    emit_attention(g, lp, T)

    if stage in ('h0', 'a0'):
        with ExitStack() as _tes:
            _dummy = _tes.enter_context(nc.sbuf_tensor("dmy", [128, 8], F32))

            @blk.gpsimd
            def _(gp):
                if stage == 'h0':
                    v_he.w(gp)
                else:
                    v_res.w(gp)
                for t in range(2):
                    d_y.inc(gp.dma_start(out=y_h[t * 128:(t + 1) * 128, :],
                                         in_=x_tm[:, t * E:(t + 1) * E]), 16)
                gp.wait_ge(d_y.h, d_y.n)
        es.close()
        return nc

    if stage == 'attn':
        all_res_a = v_res.n

        @blk.gpsimd
        def _(gp):
            v_res.w(gp, all_res_a)
            for t in range(16):
                d_y.inc(gp.dma_start(out=y_h[t * 128:(t + 1) * 128, :],
                                     in_=x_tm[:, t * E:(t + 1) * E]), 16)
            gp.wait_ge(d_y.h, d_y.n)
        es.close()
        return nc

    with ExitStack() as fes:
        fsb = lambda n, s, d: fes.enter_context(nc.sbuf_tensor(n, s, d))
        f_ln = fsb("f_ln", [128, 2 * E], F32)
        f_xq = fsb("f_xq", [128, 2 * E], F32)
        f_t = fsb("f_t", [128, E], F32)
        f_u = fsb("f_u", [128, E], F32)
        f_v = fsb("f_v", [128, E], F32)
        f_xqf = fsb("f_xqf", [128, 2 * 8 * FSC], BF16)
        f_gu = fsb("f_gu", [128, NOC * FSC], F32)
        f_guq = fsb("f_guq", [128, NOC * FSC], BF16)
        f_wg = fsb("f_wg", [128, 2 * 8 * 128], BF16)
        f_wu = fsb("f_wu", [128, 2 * 8 * 128], BF16)
        f_wd = fsb("f_wd", [128, 2 * 32 * 128], BF16)
        f_do = fsb("f_do", [128, NEC * FSC], F32)
        f_gr = fsb("f_gr", [128, FSC], F32)
        f_si = fsb("f_si", [128, FSC], F32)
        f_fb = fsb("f_fb", [128, FSC], F32)
        f_sb = fsb("f_sb", [128, FSC], F32)
        f_mx = fsb("f_mx", [128, FSC], F32)
        f_mx2 = fsb("f_mx2", [128, FSC], F32)
        f_sc = fsb("f_sc", [128, 16], F32)
        f_st = fsb("f_st", [128, 20], F32)
        f_row = fsb("f_row", [1, 2 * FSC], F32)
        f_rhi = fsb("f_rhi", [1, FSC], BF16)
        f_rlo = fsb("f_rlo", [1, FSC], BF16)
        f_m2r = fsb("f_m2r", [1, 512], F32)
        f_rhi2 = fsb("f_rhi2", [1, FSC], BF16)
        f_rlo2 = fsb("f_rlo2", [1, FSC], BF16)
        f_mgr = fsb("f_mgr", [1, 256], F32)
        f_for = fsb("f_for", [1, 256], F32)
        f_trw = fsb("f_trw", [1, 256], F32)

        pf_g = fes.enter_context(nc.psum_tensor("pf_g", [128, FSC], F32))
        pf_u = fes.enter_context(nc.psum_tensor("pf_u", [128, FSC], F32))
        pf_d0 = fes.enter_context(nc.psum_tensor("pf_d0", [128, FSC], F32))
        pf_d1 = fes.enter_context(nc.psum_tensor("pf_d1", [128, FSC], F32))
        pf_t0 = fes.enter_context(nc.psum_tensor("pf_t0", [128, 128], F32))
        pf_t1 = fes.enter_context(nc.psum_tensor("pf_t1", [128, 128], F32))
        pf_b = fes.enter_context(nc.psum_tensor("pf_b", [128, FSC], F32))
        pf_r = fes.enter_context(nc.psum_tensor("pf_r", [128, 128], F32))

        all_res = v_res.n
        wq_done = o_wq.base()

        def wq_wait(eng):
            for i, s in enumerate(o_wq.s):
                if wq_done[i] > 0:
                    eng.wait_ge(s.h, wq_done[i])

        if stage == 'ffndbg':
            for nm, sh, dt in [("dxq", [128, 2048], F32), ("dsc", [128, 16], F32),
                               ("dst", [128, 2], F32), ("dgu", [128, 8192], F32),
                               ("dguq", [128, 8192], BF16), ("dfb", [128, 256], F32),
                               ("dsb", [128, 256], F32), ("ddo", [128, 2048], F32),
                               ("dxqf", [128, 2048], BF16), ("dln", [128, 2048], F32)]:
                dbg_aps[nm] = nc.dram_tensor(nm, sh, dt, kind="ExternalOutput").ap()

        def snapA():
            return dict(basq=a_f1.n, bvst=v_f1.n, bars=a_f2.n, bvfq=v_fq.n,
                        btxt=t_xt.n, bvxe=v_xe.n, btfr=t_fr.n, bvfr=v_fr.n)

        def A_stats_dve(fcn, fan, fac):
            @blk.vector
            def _(v):
                for st2 in range(2):
                    gst = fcn * 2 + st2
                    cs = st2 * 8
                    a_f1.w(v, fan['basq'] + st2 + 1)
                    if fac is not None:
                        a_f2.w(v, fac['bars'] + 2)
                    v.reduce_sum(f_st[:, cs + 1:cs + 2],
                                 x_tm[:, gst * E:(gst + 1) * E], axis=AX.X)
                    v.drain()
                    v.tensor_single_scalar(f_st[:, cs + 2:cs + 3],
                                           f_st[:, cs + 1:cs + 2], 1.0 / E, ALU.mult)
                    v.drain()
                    v.tensor_tensor(f_st[:, cs + 3:cs + 4], f_st[:, cs + 2:cs + 3],
                                    f_st[:, cs + 2:cs + 3], ALU.mult)
                    v.drain()
                    v_f1.inc(v.scalar_tensor_tensor(
                        f_st[:, cs + 4:cs + 5], f_st[:, cs + 0:cs + 1], 1.0 / E,
                        f_st[:, cs + 3:cs + 4], ALU.mult, ALU.subtract))
                    v.drain()

        def A_quant_dve(fcn, fan, fac):
            @blk.vector
            def _(v):
                s_g.w(v, 1)
                a_f2.w(v, fan['bars'] + 2)
                a_f1.w(v, fan['basq'] + 2)
                if fac is not None:
                    t_fr.w(v, fac['btfr'] + 2)
                    t_xt.w(v, fac['btxt'] + 16)
                for st2 in range(2):
                    gst = fcn * 2 + st2
                    cs = st2 * 8
                    c0 = st2 * 8
                    v.reciprocal(f_st[:, cs + 6:cs + 7], f_st[:, cs + 5:cs + 6])
                    v.drain()
                    v.tensor_scalar(f_ln[:, st2 * E:(st2 + 1) * E],
                                    x_tm[:, gst * E:(gst + 1) * E],
                                    f_st[:, cs + 2:cs + 3], f_st[:, cs + 6:cs + 7],
                                    ALU.subtract, ALU.mult)
                    v_f2.inc(v.drain())
                    v.reduce_max(f_sc[:, c0:c0 + 1], f_ln[:, st2 * E:(st2 + 1) * E],
                                 axis=AX.X, apply_absolute_value=True)
                    v.drain()
                    v.tensor_scalar(f_sc[:, c0 + 1:c0 + 2], f_sc[:, c0:c0 + 1],
                                    1.0, 1e-5, ALU.mult, ALU.max)
                    v.drain()
                    v.reciprocal(f_sc[:, c0 + 2:c0 + 3], f_sc[:, c0 + 1:c0 + 2])
                    v.drain()
                    v.tensor_single_scalar(f_sc[:, c0 + 3:c0 + 4],
                                           f_sc[:, c0 + 2:c0 + 3], 127.0, ALU.mult)
                    v.drain()
                    v.tensor_single_scalar(f_t[:], f_ln[:, st2 * E:(st2 + 1) * E],
                                           f_sc[:, c0 + 3:c0 + 4], ALU.mult)
                    v.drain()
                    v.tensor_scalar(f_u[:], f_t[:], M23, -M23, ALU.add, ALU.add)
                    v.tensor_scalar(f_v[:], f_t[:], M52, -M52, ALU.add, ALU.add)
                    v.drain()
                    v.scalar_tensor_tensor(f_xq[:, st2 * E:(st2 + 1) * E],
                                           f_u[:], h_t[:, 0:1], f_v[:],
                                           ALU.mult, ALU.add)
                    v_fq.inc(v.drain())

        def A_xqT_te(fcn, fan):
            @blk.tensor
            def _(te):
                v_fq.w(te, fan['bvfq'] + 2)
                t_fg.w(te)
                v_dt.w(te)
                for st2 in range(2):
                    for ib in range(8):
                        tid = st2 * 8 + ib
                        if tid >= 2:
                            v_xe.w(te, fan['bvxe'] + tid - 1)
                        t_xt.inc(te.transpose(
                            [pf_t0, pf_t1][tid % 2][:, 0:128],
                            f_xq[:, st2 * E + ib * 128:st2 * E + (ib + 1) * 128],
                            id32[:]))

        def A_xqT_dve(fcn, fan):
            xp = (fcn % 2) * 8 * FSC
            @blk.vector
            def _(v):
                for st2 in range(2):
                    for ib in range(8):
                        tid = st2 * 8 + ib
                        t_xt.w(v, fan['btxt'] + tid + 1)
                        v_xe.inc(v.tensor_copy(
                            f_xqf[:, xp + ib * FSC + st2 * 128:
                                  xp + ib * FSC + (st2 + 1) * 128],
                            [pf_t0, pf_t1][tid % 2][:, 0:128]))

        def A_frT_te(fcn, fan):
            @blk.tensor
            def _(te):
                for st2 in range(2):
                    tid = 16 + st2
                    v_xe.w(te, fan['bvxe'] + tid - 1)
                    t_fr.inc(te.transpose(
                        [pf_t0, pf_t1][tid % 2][0:1, 0:128],
                        f_sc[:, st2 * 8 + 1:st2 * 8 + 2], id32[:]))

        def A_fr_dve(fcn, fan, btbc_cur):
            mp = (fcn % 2) * 256
            @blk.vector
            def _(v):
                if btbc_cur is not None:
                    t_bc.w(v, btbc_cur + 1)
                for st2 in range(2):
                    tid = 16 + st2
                    t_fr.w(v, fan['btfr'] + st2 + 1)
                    v_fr.inc(v.tensor_copy(
                        f_m2r[0:1, mp + st2 * 128:mp + (st2 + 1) * 128],
                        [pf_t0, pf_t1][tid % 2][0:1, 0:128]))
                v.drain()
                v.tensor_scalar(f_row[0:1, 0:FSC], f_m2r[0:1, mp:mp + FSC],
                                wbar[0:1, 0:1], 1.0 / 127, ALU.mult, ALU.mult)
                v.drain()
                v.tensor_copy(f_rhi2[:], f_row[0:1, 0:FSC])
                v.drain()
                v_fr.inc(v.tensor_tensor(f_rlo2[:], f_row[0:1, 0:FSC], f_rhi2[:],
                                         ALU.subtract))

        fC = snapA()

        @blk.scalar
        def _(sc):
            v_res.w(sc, all_res)
            wq_wait(sc)
            for st2 in range(2):
                a_f1.inc(sc.activation(
                    f_ln[:, st2 * E:(st2 + 1) * E],
                    x_tm[:, st2 * E:(st2 + 1) * E], AFT.Square,
                    accum_out=f_st[:, st2 * 8:st2 * 8 + 1]))
                sc.drain()

        A_stats_dve(0, fC, None)

        @blk.scalar
        def _(sc):
            for st2 in range(2):
                cs = st2 * 8
                v_f1.w(sc, fC['bvst'] + st2 + 1)
                a_f2.inc(sc.activation(
                    f_st[:, cs + 5:cs + 6], f_st[:, cs + 4:cs + 5], AFT.Sqrt,
                    bias=eps_t[:, 0:1], scale=1.0))
                sc.drain()

        A_quant_dve(0, fC, None)
        A_xqT_te(0, fC)
        A_xqT_dve(0, fC)
        A_frT_te(0, fC)
        A_fr_dve(0, fC, None)

        for fc in range(NFC if stage != 'ffndbg' else 1):
            btbc, bvbc = t_bc.n, v_bc.n
            bdfw, btfg, bvgr, basi, bvgu = d_fw.base(), t_fg.n, v_gr.n, a_si.n, v_gu.n
            btmx, bvmx = t_mx.n, v_mx.n
            btsr, bvsr, btsb, bvsb = t_sr.n, v_sr.n, t_sb.n, v_sb2.n
            btfo, bvfo = t_fo.n, v_fo.n
            bvgq, bdfd, btdn, bvde = v_gq.n, d_fd.base(), t_dn.n, v_de.n
            btdt, bvdt = t_dt.n, v_dt.n
            prev_dt = v_dt.n
            xp = (fc % 2) * 8 * FSC
            mp = (fc % 2) * 256
            last_fc = (fc == NFC - 1) or stage == 'ffndbg'
            fN = None if last_fc else snapA()

            @blk.tensor
            def _(te):
                v_fr.w(te, fC['bvfr'] + 3)
                v_sb2.w(te)
                if fc >= 1:
                    v_bc.w(te, bvbc)
                te.matmul(pf_b[:, 0:FSC], onesb[0:1, :], f_rhi2[0:1, :],
                          start=True, stop=False)
                t_bc.inc(te.matmul(pf_b[:, 0:FSC], onesb[0:1, :], f_rlo2[0:1, :],
                                   start=False, stop=True))

            @blk.vector
            def _(v):
                t_bc.w(v, btbc + 1)
                v_bc.inc(v.tensor_copy(f_fb[:], pf_b[:, 0:FSC]))
                t_mx.w(v, btmx)
                v.memset(f_mx[:], 0.0)
                v.memset(f_mx2[:], 0.0)
                v_bc.inc(v.drain())

            @blk.sync
            def _(sy):
                wq_wait(sy)
                v_res.w(sy, all_res)
                for oc in range(NOC):
                    if oc >= 2:
                        t_fg.w(sy, btfg + oc - 1)
                    buf = (oc % 2) * 1024
                    d_fw.inc(sy.dma_start(
                        out=f_wg[:, buf:buf + 1024].rearrange("p (a c) -> p a c", c=128),
                        in_=wgq_h[:, oc, :, :]), oc)
                    d_fw.inc(sy.dma_start(
                        out=f_wu[:, buf:buf + 1024].rearrange("p (a c) -> p a c", c=128),
                        in_=wuq_h[:, oc, :, :]), oc)

            @blk.scalar
            def _(sc):
                if fN is not None:
                    v_res.w(sc, all_res)
                    v_f1.w(sc, fC['bvst'] + 2)
                    v_fq.w(sc, fC['bvfq'] + 2)
                    for st2 in range(2):
                        gst = (fc + 1) * 2 + st2
                        a_f1.inc(sc.activation(
                            f_ln[:, st2 * E:(st2 + 1) * E],
                            x_tm[:, gst * E:(gst + 1) * E], AFT.Square,
                            accum_out=f_st[:, st2 * 8:st2 * 8 + 1]))
                        sc.drain()
                for oc in range(NOC):
                    v_gr.w(sc, bvgr + oc + 1)
                    if oc >= 1:
                        v_gu.w(sc, bvgu + oc)
                    a_si.inc(sc.activation(f_si[:], f_gr[:], AFT.Silu))
                    if oc == 24 and fN is not None:
                        for st2 in range(2):
                            cs = st2 * 8
                            v_f1.w(sc, fN['bvst'] + st2 + 1)
                            a_f2.inc(sc.activation(
                                f_st[:, cs + 5:cs + 6], f_st[:, cs + 4:cs + 5],
                                AFT.Sqrt, bias=eps_t[:, 0:1], scale=1.0))
                            sc.drain()

            @blk.tensor
            def _(te):
                v_xe.w(te, fC['bvxe'] + 16)
                if fc >= 1:
                    v_gu.w(te, bvgu)
                for oc in range(NOC):
                    d_fw.w(te, oc, bdfw, per=32)
                    if oc >= 2:
                        v_gr.w(te, bvgr + oc - 1)
                        v_gu.w(te, bvgu + oc - 1)
                    buf = (oc % 2) * 1024
                    pg = [pf_g, pf_d0][oc % 2]
                    pu = [pf_u, pf_d1][oc % 2]
                    for ic in range(8):
                        te.matmul(pg[:, 0:FSC],
                                  f_wg[:, buf + ic * 128:buf + (ic + 1) * 128],
                                  f_xqf[:, xp + ic * FSC:xp + (ic + 1) * FSC],
                                  start=(ic == 0), stop=(ic == 7))
                    last = None
                    for ic in range(8):
                        last = te.matmul(pu[:, 0:FSC],
                                         f_wu[:, buf + ic * 128:buf + (ic + 1) * 128],
                                         f_xqf[:, xp + ic * FSC:xp + (ic + 1) * FSC],
                                         start=(ic == 0), stop=(ic == 7))
                    t_fg.inc(last)

            def fold_iter(v, oc):
                if oc < NOC:
                    t_fg.w(v, btfg + oc + 1)
                    v_bc.w(v, bvbc + 2)
                    if oc >= 1:
                        a_si.w(v, basi + oc)
                    pg = [pf_g, pf_d0][oc % 2]
                    v_gr.inc(v.tensor_tensor(f_gr[:], pg[:, 0:FSC], f_fb[:], ALU.mult))
                    v.drain()
                if oc >= 1:
                    a_si.w(v, basi + oc)
                    pu = [pf_u, pf_d1][(oc - 1) % 2]
                    v.tensor_tensor(f_gu[:, (oc - 1) * FSC:oc * FSC], f_si[:],
                                    pu[:, 0:FSC], ALU.mult)
                    v.drain()
                    v.tensor_tensor(f_mx[:], f_mx[:],
                                    f_gu[:, (oc - 1) * FSC:oc * FSC], ALU.max)
                    v.scalar_tensor_tensor(f_mx2[:],
                                           f_gu[:, (oc - 1) * FSC:oc * FSC],
                                           -1.0, f_mx2[:], ALU.mult, ALU.max)
                    v_gu.inc(v.drain())

            @blk.vector
            def _(v):
                for oc in range(0, 17):
                    fold_iter(v, oc)

            if fN is not None:
                A_stats_dve(fc + 1, fN, fC)

            @blk.vector
            def _(v):
                for oc in range(17, NOC + 1):
                    fold_iter(v, oc)
                v.tensor_tensor(f_mx[:], f_mx[:], f_mx2[:], ALU.max)
                v_gu.inc(v.drain())

            if fN is not None:
                A_quant_dve(fc + 1, fN, fC)

            @blk.tensor
            def _(te):
                v_gu.w(te, bvgu + NOC + 1)
                v_fr.w(te, fC['bvfr'] + 3)
                v_dt.w(te, prev_dt)
                for st2 in range(2):
                    t_mx.inc(te.transpose([pf_t0, pf_t1][st2][:, 0:128],
                                          f_mx[:, st2 * 128:(st2 + 1) * 128], id32[:]))

            @blk.vector
            def _(v):
                for st2 in range(2):
                    t_mx.w(v, btmx + st2 + 1)
                    c0 = st2 * 8
                    v.reduce_max(f_sc[:, c0 + 6:c0 + 7], [pf_t0, pf_t1][st2][:, 0:128],
                                 axis=AX.X)
                v_mx.inc(v.drain())

            @blk.tensor
            def _(te):
                v_mx.w(te, bvmx + 1)
                if fc >= 1:
                    v_fo.w(te, bvfo)
                for st2 in range(2):
                    if st2 >= 1:
                        v_sr.w(te, bvsr + 1)
                    t_sr.inc(te.transpose(pf_r[0:1, 0:128],
                                          f_sc[:, st2 * 8 + 6:st2 * 8 + 7], id32[:]))

            @blk.vector
            def _(v):
                v_dt.w(v, prev_dt)
                for st2 in range(2):
                    t_sr.w(v, btsr + st2 + 1)
                    v_sr.inc(v.tensor_copy(f_mgr[0:1, st2 * 128:(st2 + 1) * 128],
                                           pf_r[0:1, 0:128]))
                v.drain()
                v.tensor_tensor(f_trw[0:1, 0:FSC], f_mgr[0:1, 0:FSC],
                                f_m2r[0:1, mp:mp + FSC], ALU.mult)
                v.drain()
                v.tensor_scalar(f_trw[0:1, 0:FSC], f_trw[0:1, 0:FSC],
                                wbar[0:1, 1:2], 1.0 / 127, ALU.mult, ALU.mult)
                v.drain()
                v.tensor_scalar(f_trw[0:1, 0:FSC], f_trw[0:1, 0:FSC],
                                1.0, 1e-5, ALU.mult, ALU.max)
                v.drain()
                v.tensor_scalar(f_for[0:1, 0:FSC], f_trw[0:1, 0:FSC],
                                wbar[0:1, 2:3], 1.0 / 127, ALU.mult, ALU.mult)
                v.drain()
                v.reciprocal(f_trw[0:1, 0:FSC], f_trw[0:1, 0:FSC])
                v.drain()
                v.tensor_tensor(f_row[0:1, FSC:2 * FSC], f_m2r[0:1, mp:mp + FSC],
                                f_trw[0:1, 0:FSC], ALU.mult)
                v.drain()
                v.tensor_single_scalar(f_row[0:1, FSC:2 * FSC],
                                       f_row[0:1, FSC:2 * FSC],
                                       wbar[0:1, 1:2], ALU.mult)
                v.drain()
                v.tensor_copy(f_rhi[:], f_row[0:1, FSC:2 * FSC])
                v.drain()
                v_sr.inc(v.tensor_tensor(f_rlo[:], f_row[0:1, FSC:2 * FSC], f_rhi[:],
                                         ALU.subtract))

            @blk.tensor
            def _(te):
                v_sr.w(te, bvsr + 3)
                for st2 in range(2):
                    if st2 >= 1:
                        v_fo.w(te, bvfo + 1)
                    t_fo.inc(te.transpose(pf_r[:, 0:1],
                                          f_for[0:1, st2 * 128:(st2 + 1) * 128],
                                          id32[0:1, 0:1]))

            @blk.vector
            def _(v):
                for st2 in range(2):
                    t_fo.w(v, btfo + st2 + 1)
                    v_fo.inc(v.tensor_copy(f_st[:, 18 + st2:19 + st2],
                                           pf_r[:, 0:1]))

            @blk.tensor
            def _(te):
                v_sr.w(te, bvsr + 3)
                v_bc.w(te)
                te.matmul(pf_b[:, 0:FSC], onesb[0:1, :], f_rhi[0:1, :],
                          start=True, stop=False)
                t_sb.inc(te.matmul(pf_b[:, 0:FSC], onesb[0:1, :], f_rlo[0:1, :],
                                   start=False, stop=True))

            @blk.vector
            def _(v):
                t_sb.w(v, btsb + 1)
                v_sb2.inc(v.tensor_copy(f_sb[:], pf_b[:, 0:FSC]))

            @blk.vector
            def _(v):
                v_sb2.w(v, bvsb + 1)
                for oc in range(NOC):
                    if fc >= 1 and oc == 0:
                        t_dn.w(v, btdn)
                    v.tensor_tensor(f_gr[:], f_gu[:, oc * FSC:(oc + 1) * FSC],
                                    f_sb[:], ALU.mult)
                    v.drain()
                    v.tensor_scalar(f_si[:], f_gr[:], M23, -M23, ALU.add, ALU.add)
                    v.tensor_scalar(f_fb[:], f_gr[:], M17, c5217[:, 0:1],
                                    ALU.add, ALU.add)
                    v.drain()
                    v.tensor_single_scalar(f_fb[:], f_fb[:], -M52, ALU.add)
                    v.drain()
                    v_gq.inc(v.scalar_tensor_tensor(
                        f_guq[:, oc * FSC:(oc + 1) * FSC], f_si[:], h_t[:, 0:1],
                        f_fb[:], ALU.mult, ALU.add))
                    v.drain()

            @blk.sync
            def _(sy):
                v_res.w(sy, all_res)
                for ec in range(NEC):
                    if ec >= 2:
                        t_dn.w(sy, btdn + ec - 1)
                    d_fd.inc(sy.dma_start(
                        out=f_wd[:, (ec % 2) * 4096:(ec % 2 + 1) * 4096].rearrange(
                            "p (a c) -> p a c", c=128),
                        in_=wdq_h[:, ec, :, :]), ec)

            @blk.tensor
            def _(te):
                v_gq.w(te, bvgq + NOC)
                for ec in range(NEC):
                    d_fd.w(te, ec, bdfd)
                    if ec >= 2:
                        v_de.w(te, bvde + ec - 1)
                    pd = [pf_d0, pf_d1][ec % 2]
                    last = None
                    for ic in range(NOC):
                        last = te.matmul(
                            pd[:, 0:FSC],
                            f_wd[:, (ec % 2) * 4096 + ic * 128:
                                 (ec % 2) * 4096 + (ic + 1) * 128],
                            f_guq[:, ic * FSC:(ic + 1) * FSC],
                            start=(ic == 0), stop=(ic == NOC - 1))
                    t_dn.inc(last)

            @blk.vector
            def _(v):
                for ec in range(NEC):
                    t_dn.w(v, btdn + ec + 1)
                    pd = [pf_d0, pf_d1][ec % 2]
                    v_de.inc(v.tensor_copy(f_do[:, ec * FSC:(ec + 1) * FSC],
                                           pd[:, 0:FSC]))

            @blk.tensor
            def _(te):
                v_de.w(te, bvde + NEC)
                v_mx.w(te, bvmx + 1)
                for ec in range(NEC):
                    for st2 in range(2):
                        tid = ec * 2 + st2
                        if tid >= 2:
                            v_dt.w(te, bvdt + tid - 1)
                        t_dt.inc(te.transpose(
                            [pf_t0, pf_t1][tid % 2][:, 0:128],
                            f_do[:, ec * FSC + st2 * 128:ec * FSC + (st2 + 1) * 128],
                            id32[:]))

            @blk.vector
            def _(v):
                v_fo.w(v, bvfo + 2)
                for ec in range(NEC):
                    for st2 in range(2):
                        tid = ec * 2 + st2
                        t_dt.w(v, btdt + tid + 1)
                        gst = fc * 2 + st2
                        xd = x_tm[:, gst * E + ec * 128:gst * E + (ec + 1) * 128]
                        v_dt.inc(v.scalar_tensor_tensor(
                            xd, [pf_t0, pf_t1][tid % 2][:, 0:128],
                            f_st[:, 18 + st2:19 + st2], xd, ALU.mult, ALU.add))

            @blk.gpsimd
            def _(gp):
                v_dt.w(gp, bvdt + 16)
                for st2 in range(2):
                    gst = fc * 2 + st2
                    d_y.inc(gp.dma_start(out=y_h[gst * 128:(gst + 1) * 128, :],
                                         in_=x_tm[:, gst * E:(gst + 1) * E]), 16)

            if fN is not None:
                A_xqT_te(fc + 1, fN)
                A_xqT_dve(fc + 1, fN)
                A_frT_te(fc + 1, fN)
                A_fr_dve(fc + 1, fN, btbc)
                fC = fN

        if stage == 'ffndbg':
            @blk.gpsimd
            def _(gp):
                gp.wait_ge(d_y.h, d_y.n)
                for (nm, buf, w) in [("dxq", f_xq, 2048),
                                     ("dst", f_st[:, 18:20], 2), ("dgu", f_gu, 8192),
                                     ("dguq", f_guq, 8192), ("dfb", f_fb, 256),
                                     ("dsb", f_sb, 256), ("ddo", f_do, 2048),
                                     ("dxqf", f_xqf, 2048), ("dln", f_ln, 2048)]:
                    src_ap = buf if not hasattr(buf, "shape") or not hasattr(buf, "ap") else buf
                    try:
                        src_ap = buf[:, 0:w]
                    except Exception:
                        src_ap = buf
                    d_y.inc(gp.dma_start(out=dbg_aps[nm], in_=src_ap), 16)
                gp.wait_ge(d_y.h, d_y.n)

        @blk.gpsimd
        def _(gp):
            gp.wait_ge(d_y.h, d_y.n)

        if dbg:
            @blk.gpsimd
            def _(gp):
                gp.dma_start(out=dbg_h[0:128, 0:4096], in_=x_tm[:, 0:4096]).then_inc(d_y.h, 16)
                gp.wait_ge(d_y.h, d_y.n + 16)

    es.close()
    return nc


def kernel(L0_attn_x, L0_attn_proj_weight, L0_attn_attn_weight,
           L0_attn_value_weight, L0_attn_out_w, L0_attn_out_b,
           L0_ffn_w_gate, L0_ffn_w_up, L0_ffn_w_down):
    import ml_dtypes
    x = np.asarray(L0_attn_x, np.float32)
    pw = np.ascontiguousarray(np.asarray(L0_attn_proj_weight, np.float32)
                              .reshape(E, E))
    aw = np.ascontiguousarray(np.asarray(L0_attn_attn_weight, np.float32)
                              .reshape(8 * 128, 64))
    vw = np.ascontiguousarray(np.asarray(L0_attn_value_weight, np.float32)
                              .reshape(8 * 128, 64))
    owT = np.ascontiguousarray(np.asarray(L0_attn_out_w, np.float32).T)
    ob = np.ascontiguousarray(np.asarray(L0_attn_out_b, np.float32)
                              .reshape(1, E))
    wgT = np.ascontiguousarray(np.asarray(L0_ffn_w_gate, np.float32).T)
    wuT = np.ascontiguousarray(np.asarray(L0_ffn_w_up, np.float32).T)
    wdT = np.ascontiguousarray(np.asarray(L0_ffn_w_down, np.float32).T)
    id32 = np.eye(128, dtype=np.float32)
    idbf = np.eye(128, dtype=ml_dtypes.bfloat16)
    onesb = np.ones((1, 128), dtype=ml_dtypes.bfloat16)

    nc = build_program()
    in_maps = []
    for c in range(8):
        b, sh = c // 2, c % 2
        xr = np.ascontiguousarray(np.roll(x[b], -2048 * sh, axis=0))
        in_maps.append(dict(x_full=xr, pw=pw, awp=aw, vwp=vw, owT=owT, ob=ob,
                            wgT=wgT, wuT=wuT, wdT=wdT, id32=id32, idbf=idbf,
                            onesb=onesb))
    res = run_bass_kernel_spmd(nc, in_maps, core_ids=list(range(8)))
    y = np.empty((4, S, E), np.float32)
    for c in range(8):
        b, sh = c // 2, c % 2
        y[b, 2048 * sh:2048 * (sh + 1), :] = res.results[c]["y"]
    return y


# version 14
# speedup vs baseline: 1.0286x; 1.0286x over previous
import numpy as np
import concourse.bass as bass
import concourse.mybir as mybir
from concourse.bass_utils import run_bass_kernel_spmd
from contextlib import ExitStack

F32 = mybir.dt.float32
BF16 = mybir.dt.bfloat16
ALU = mybir.AluOpType
AFT = mybir.ActivationFunctionType
AX = mybir.AxisListType

M23 = 1.5 * 2.0**23
M52 = 1.5 * 2.0**52
M17 = 1.5 * 2.0**17

S, E = 4096, 1024
SEQ = 2048
INNER = 4096
NRC = 32
FSC = 256
NFC = SEQ // FSC
NOC = INNER // 128
NEC = E // 128


class Sem:
    __slots__ = ("h", "n")

    def __init__(self, es, nc, name):
        self.h = es.enter_context(nc.semaphore(name))
        self.n = 0

    def inc(self, instr, k=1):
        instr.then_inc(self.h, k)
        self.n += k
        return self.n

    def w(self, eng, mark=None):
        m = self.n if mark is None else mark
        if m > 0:
            eng.wait_ge(self.h, m)


class DSem:

    def __init__(self, es, nc, name, k):
        self.s = [Sem(es, nc, f"{name}_{i}") for i in range(k)]
        self.k = k

    def base(self):
        return [s.n for s in self.s]

    def inc(self, instr, idx, amt=16):
        return self.s[idx % self.k].inc(instr, amt)

    def w(self, eng, idx, base, per=16):
        sl = idx % self.k
        eng.wait_ge(self.s[sl].h, base[sl] + per * (idx // self.k + 1))


def build_program(dbg=False, stage='full'):
    nc = bass.Bass("TRN2", target_bir_lowering=False)

    x_full = nc.dram_tensor("x_full", [S, E], F32, kind="ExternalInput").ap()
    pw_h = nc.dram_tensor("pw", [E, E], F32, kind="ExternalInput").ap()
    awp_h = nc.dram_tensor("awp", [8 * 128, 64], F32, kind="ExternalInput").ap()
    vwp_h = nc.dram_tensor("vwp", [8 * 128, 64], F32, kind="ExternalInput").ap()
    owT_h = nc.dram_tensor("owT", [E, E], F32, kind="ExternalInput").ap()
    ob_h = nc.dram_tensor("ob", [1, E], F32, kind="ExternalInput").ap()
    wgT_h = nc.dram_tensor("wgT", [E, INNER], F32, kind="ExternalInput").ap()
    wuT_h = nc.dram_tensor("wuT", [E, INNER], F32, kind="ExternalInput").ap()
    wdT_h = nc.dram_tensor("wdT", [INNER, E], F32, kind="ExternalInput").ap()
    id32_h = nc.dram_tensor("id32", [128, 128], F32, kind="ExternalInput").ap()
    ones_h = nc.dram_tensor("onesb", [1, 128], BF16, kind="ExternalInput").ap()
    idbf_h = nc.dram_tensor("idbf", [128, 128], BF16, kind="ExternalInput").ap()
    y_h = nc.dram_tensor("y", [SEQ, E], F32, kind="ExternalOutput").ap()

    wgq_h = nc.dram_tensor("wgq", [128, 32, 8, 128], BF16, kind="Internal").ap()
    wuq_h = nc.dram_tensor("wuq", [128, 32, 8, 128], BF16, kind="Internal").ap()
    wdq_h = nc.dram_tensor("wdq", [128, 8, 32, 128], BF16, kind="Internal").ap()
    dbg_h = nc.dram_tensor("dbg", [128, 4096], F32,
                           kind="ExternalOutput").ap() if dbg else None
    dbg_aps = {}

    es = ExitStack()
    sb = lambda n, s, d: es.enter_context(nc.sbuf_tensor(n, s, d))
    sem = lambda n: Sem(es, nc, n)

    x_tm = sb("x_tm", [128, 16 * E], F32)
    id32 = sb("id32s", [128, 128], F32)
    idbf = sb("idbfs", [128, 128], BF16)
    onesb = sb("onesbs", [1, 128], BF16)
    eps_t = sb("eps_t", [128, 1], F32)
    g_t = sb("g_t", [128, 1], F32)
    h_t = sb("h_t", [128, 1], F32)
    c5217 = sb("c5217", [128, 1], F32)
    sc_t = sb("sc_t", [128, 8], F32)
    mkrow = sb("mkrow", [1, 32], F32)
    mh_row = sb("mh_row", [1, 512], F32)
    ones_row = sb("ones_row", [1, 128], F32)
    onesb_t = sb("onesb_t", [128, 128], BF16)
    mq_row = sb("mq_row", [1, 128], F32)
    wbar = sb("wbar", [1, 4], F32)
    obhi = sb("obhi", [1, E], BF16)
    oblo = sb("oblo", [1, E], BF16)

    s_dc = sem("s_dc"); s_dx = sem("s_dx"); s_g = sem("s_g"); s_ob = sem("s_ob")
    d_w = DSem(es, nc, "d_w", 2); v_wr = sem("v_wr"); t_wt = sem("t_wt"); v_ws = sem("v_ws")
    g_wb = sem("g_wb"); v_wq = sem("v_wq"); o_wq = DSem(es, nc, "o_wq", 2); s_gp = sem("s_gp")
    d_pw = DSem(es, nc, "d_pw", 2); v_pw = sem("v_pw")
    d_xa = DSem(es, nc, "d_xa", 2); a_sq = sem("a_sq"); v_st = sem("v_st"); a_rs = sem("a_rs")
    v_ln = sem("v_ln"); t_lt = sem("t_lt"); v_ls = sem("v_ls"); t_pj = sem("t_pj")
    v_he = sem("v_he"); v_mh = sem("v_mh"); t_mh = sem("t_mh"); g_mh = sem("g_mh")
    d_aw = sem("d_aw"); v_aw = sem("v_aw")
    t_v = sem("t_v"); v_ve = sem("v_ve")
    t_q = sem("t_q"); v_qe = sem("v_qe"); t_ct = sem("t_ct"); v_cd = sem("v_cd")
    g_cb = sem("g_cb")
    t_s = sem("t_s"); a_e = sem("a_e"); v_w = sem("v_w"); t_av = sem("t_av")
    t_cb = sem("t_cb"); v_as = sem("v_as"); d_pk = Sem(es, nc, "d_pk")
    v_nm = sem("v_nm"); t_at = sem("t_at"); v_af = sem("v_af"); t_op = sem("t_op")
    v_res = sem("v_res")
    a_f1 = sem("a_f1"); v_f1 = sem("v_f1"); a_f2 = sem("a_f2"); v_f2 = sem("v_f2")
    v_fq = sem("v_fq"); t_xt = sem("t_xt"); v_xe = sem("v_xe")
    t_fr = sem("t_fr"); v_fr = sem("v_fr"); t_bc = sem("t_bc"); v_bc = sem("v_bc")
    d_fw = DSem(es, nc, "d_fw", 2); t_fg = sem("t_fg"); v_gr = sem("v_gr"); a_si = sem("a_si")
    v_gu = sem("v_gu"); t_mx = sem("t_mx"); v_mx = sem("v_mx")
    t_sr = sem("t_sr"); v_sr = sem("v_sr"); t_sb = sem("t_sb"); v_sb2 = sem("v_sb2")
    t_fo = sem("t_fo"); v_fo = sem("v_fo")
    v_gq = sem("v_gq"); d_fd = DSem(es, nc, "d_fd", 2); t_dn = sem("t_dn"); v_de = sem("v_de")
    t_dt = sem("t_dt"); v_dt = sem("v_dt"); d_y = sem("d_y")

    blk = es.enter_context(nc.Block())

    @blk.sync
    def _(sy):
        s_dc.inc(sy.dma_start(out=id32[:], in_=id32_h), 16)
        s_dc.inc(sy.dma_start(out=onesb[:], in_=ones_h), 16)
        s_dc.inc(sy.dma_start(out=idbf[:], in_=idbf_h), 16)
        for t in range(16):
            s_dx.inc(sy.dma_start(out=x_tm[:, t * E:(t + 1) * E],
                                  in_=x_full[t * 128:(t + 1) * 128, :]), 16)

    @blk.vector
    def _(v):
        v.memset(eps_t[:], 1e-5)
        v.memset(ones_row[:], 1.0)
        v.memset(onesb_t[:], 1.0)
        v.memset(sc_t[:, 0:1], 0.75)
        v.drain()
        v.tensor_scalar(sc_t[:, 1:2], sc_t[:, 0:1], M23, -M23, ALU.add, ALU.add)
        v.drain()
        v.tensor_sub(sc_t[:, 2:3], sc_t[:, 1:2], sc_t[:, 0:1])
        v.drain()
        v.tensor_scalar(g_t[:], sc_t[:, 2:3], 4.0, -1.0, ALU.mult, ALU.add)
        v.drain()
        v.tensor_single_scalar(h_t[:], g_t[:], 1.0, ALU.add)
        v.memset(c5217[:], M52)
        v.drain()
        s_g.inc(v.tensor_single_scalar(c5217[:], c5217[:], -M17, ALU.add))

    with ExitStack() as wes:
        wsb = lambda n, s, d: wes.enter_context(nc.sbuf_tensor(n, s, d))
        wraw = wsb("wraw", [128, 2 * 4096], F32)
        wt = wsb("wt", [128, 4096], F32)
        wn = wsb("wn", [128, 4096], F32)
        wqt = wsb("wqt", [128, 2 * 4096], BF16)
        asum = wsb("asum", [128, 33], F32)
        swbc = wsb("swbc", [128, 1], F32)
        obf = wsb("obf", [1, E], F32)
        wrow = wsb("wrow", [1, 128], F32)
        ps_wq = wes.enter_context(nc.psum_tensor("ps_wq", [128, 128], F32))

        @blk.sync
        def _(sy):
            s_dc.inc(sy.dma_start(out=obf[:], in_=ob_h), 16)

        @blk.vector
        def _(v):
            s_dc.w(v)
            v.tensor_copy(obhi[:], obf[:])
            v.drain()
            s_ob.inc(v.tensor_tensor(oblo[:], obf[:], obhi[:], ALU.subtract))

        specs = [(wgT_h, wgq_h, 8, 4096, 0), (wuT_h, wuq_h, 8, 4096, 1),
                 (wdT_h, wdq_h, 32, 1024, 2)]
        if stage == 'setup0':
            specs = []
        elif stage == 'wq1':
            specs = specs[:1]

        for si, (src, dst, nt, wdt, wi) in enumerate(specs):
            bdw, bvr = d_w.base(), v_wr.n

            @blk.sync
            def _(sy, src=src, nt=nt, wdt=wdt, bvr=bvr):
                v_wq.w(sy)
                for t in range(nt):
                    if t >= 2:
                        v_wr.w(sy, bvr + t - 1)
                    d_w.inc(sy.dma_start(
                        out=wraw[:, (t % 2) * 4096:(t % 2) * 4096 + wdt],
                        in_=src[t * 128:(t + 1) * 128, :]), t)

            @blk.vector
            def _(v, nt=nt, wdt=wdt, bdw=bdw):
                for t in range(nt):
                    d_w.w(v, t, bdw)
                    v_wr.inc(v.reduce_sum(
                        asum[:, t:t + 1],
                        wraw[:, (t % 2) * 4096:(t % 2) * 4096 + wdt],
                        axis=AX.X, apply_absolute_value=True))
                v.drain()
                v.reduce_sum(asum[:, 32:33], asum[:, 0:nt], axis=AX.X)
                v_wr.inc(v.drain())

            @blk.tensor
            def _(te):
                v_wr.w(te)
                s_dc.w(te)
                t_wt.inc(te.transpose(ps_wq[0:1, 0:128], asum[:, 32:33], id32[:]))

            @blk.vector
            def _(v, nt=nt, wdt=wdt, wi=wi):
                t_wt.w(v)
                g_wb.w(v)
                v.reduce_sum(sc_t[0:1, 4:5], ps_wq[0:1, 0:128], axis=AX.X)
                v.drain()
                v.tensor_scalar(wbar[0:1, wi:wi + 1], sc_t[0:1, 4:5],
                                1.0 / (nt * 128 * wdt), 1e-5, ALU.mult, ALU.max)
                v.drain()
                v_ws.inc(v.reciprocal(sc_t[0:1, 5:6], wbar[0:1, wi:wi + 1]))

            @blk.vector
            def _(v):
                v_ws.w(v)
                v_ws.inc(v.tensor_single_scalar(wrow[0:1, 0:128], ones_row[0:1, :],
                                                sc_t[0:1, 5:6], ALU.mult))

            @blk.tensor
            def _(te):
                v_ws.w(te)
                t_wt.inc(te.transpose(ps_wq[:, 0:1], wrow[0:1, 0:128],
                                      id32[0:1, 0:1]))

            @blk.vector
            def _(v):
                t_wt.w(v)
                g_wb.inc(v.tensor_copy(swbc[:], ps_wq[:, 0:1]))

            bdw2, bvq, boq = d_w.base(), v_wq.n, o_wq.base()

            @blk.sync
            def _(sy, src=src, nt=nt, wdt=wdt, bvq=bvq):
                v_wr.w(sy)
                for t in range(nt):
                    if t >= 2:
                        v_wq.w(sy, bvq + t - 1)
                    d_w.inc(sy.dma_start(
                        out=wraw[:, (t % 2) * 4096:(t % 2) * 4096 + wdt],
                        in_=src[t * 128:(t + 1) * 128, :]), t)

            @blk.vector
            def _(v, nt=nt, wdt=wdt, bdw2=bdw2, boq=boq):
                g_wb.w(v)
                for i, s_ in enumerate(o_wq.s):
                    if boq[i] > 0:
                        v.wait_ge(s_.h, boq[i])
                for t in range(nt):
                    d_w.w(v, t, bdw2)
                    if t >= 2:
                        o_wq.w(v, t - 2, boq)
                    buf = wraw[:, (t % 2) * 4096:(t % 2) * 4096 + wdt]
                    qb = wqt[:, (t % 2) * 4096:(t % 2) * 4096 + wdt]
                    v.tensor_single_scalar(wt[:, 0:wdt], buf, swbc[:, 0:1], ALU.mult)
                    v.drain()
                    v.tensor_single_scalar(wn[:, 0:wdt], wt[:, 0:wdt], -0.5, ALU.is_le)
                    v.drain()
                    v_wq.inc(v.scalar_tensor_tensor(
                        qb, wt[:, 0:wdt], 0.5, wn[:, 0:wdt], ALU.is_ge, ALU.subtract))
                    v.drain()

            @blk.gpsimd
            def _(g, dst=dst, nt=nt, wdt=wdt, bvq=bvq):
                for t in range(nt):
                    v_wq.w(g, bvq + t + 1)
                    o_wq.inc(g.dma_start(
                        out=dst[:, :, t, :],
                        in_=wqt[:, (t % 2) * 4096:(t % 2) * 4096 + wdt]
                        .rearrange("p (o c) -> p o c", c=128)), t)

    def _early_out():
        @blk.gpsimd
        def _(gp):
            for s_ in o_wq.s:
                if s_.n > 0:
                    gp.wait_ge(s_.h, s_.n)
            for t in range(2):
                d_y.inc(gp.dma_start(out=y_h[t * 128:(t + 1) * 128, :],
                                     in_=x_tm[:, t * E:(t + 1) * E]), 16)
            gp.wait_ge(d_y.h, d_y.n)
        es.close()
        return nc

    if stage in ('wqonly', 'setup0', 'wq1'):
        return _early_out()

    with ExitStack() as aes:
        asb = lambda n, s, d: aes.enter_context(nc.sbuf_tensor(n, s, d))
        h_hi = asb("h_hi", [128, 4 * 4096], BF16)
        h_lo = asb("h_lo", [128, 4 * 4096], BF16)
        vA = asb("vA", [128, NRC * 65], F32)
        vB = asb("vB", [128, NRC * 65], F32)
        stat = asb("stat", [128, 16], F32)
        mh_t = asb("mh_t", [128, 4], F32)
        mh_r = asb("mh_r", [4, 2], F32)

        pb_q = aes.enter_context(nc.psum_tensor("pb_q", [128, 512], F32))
        pb_o = aes.enter_context(nc.psum_tensor("pb_o", [128, 512], F32))
        pb_s = [aes.enter_context(nc.psum_tensor(f"pb_s{i}", [128, 512], F32))
                for i in range(2)]
        pb_av = [aes.enter_context(nc.psum_tensor(f"pb_av{i}", [128, 512], F32))
                 for i in range(4)]

        def emit_hbuild(g, hes):
            hsb = lambda n, s, d: hes.enter_context(nc.sbuf_tensor(n + str(g), s, d))
            pw_hi = hsb("pw_hi", [128, 8 * 512], BF16)
            pw_lo = hsb("pw_lo", [128, 8 * 512], BF16)
            pw_f = hsb("pw_f", [128, 2 * 512], F32)
            xa = hsb("xa", [128, 2 * E], F32)
            xsq = hsb("xsq", [128, E], F32)
            lnt = hsb("lnt", [128, E], F32)
            ln_hi = hsb("ln_hi", [128, 4 * E], BF16)
            ln_lo = hsb("ln_lo", [128, 4 * E], BF16)
            bdpw, bvpw = d_pw.base(), v_pw.n
            bdxa, basq, bvst, bars, bvln = d_xa.base(), a_sq.n, v_st.n, a_rs.n, v_ln.n
            btlt, bvls, btpj, bvhe = t_lt.n, v_ls.n, t_pj.n, v_he.n
            prev_res = v_res.n
            prev_nm = v_nm.n
            prev_af_h = v_af.n

            @blk.sync
            def _(sy):
                v_res.w(sy, prev_res)
                for s_ in o_wq.s:
                    if s_.n > 0:
                        sy.wait_ge(s_.h, s_.n)
                v_wq.w(sy)
                for ic in range(8):
                    if ic >= 2:
                        v_pw.w(sy, bvpw + ic - 1)
                    d_pw.inc(sy.dma_start(
                        out=pw_f[:, (ic % 2) * 512:(ic % 2 + 1) * 512],
                        in_=pw_h[ic * 128:(ic + 1) * 128, 512 * g:512 * (g + 1)]), ic)
                for rc in range(NRC):
                    if rc >= 2:
                        v_ln.w(sy, bvln + rc - 1)
                    d_xa.inc(sy.dma_start(
                        out=xa[:, (rc % 2) * E:(rc % 2 + 1) * E],
                        in_=x_full[rc * 128:(rc + 1) * 128, :]), rc)

            @blk.vector
            def _(v):
                t_pj.w(v, btpj)
                for ic in range(8):
                    d_pw.w(v, ic, bdpw)
                    src = pw_f[:, (ic % 2) * 512:(ic % 2 + 1) * 512]
                    v.tensor_copy(pw_hi[:, ic * 512:(ic + 1) * 512], src)
                    v.drain()
                    v_pw.inc(v.tensor_tensor(
                        pw_lo[:, ic * 512:(ic + 1) * 512], src,
                        pw_hi[:, ic * 512:(ic + 1) * 512], ALU.subtract))

            def emit_sq(sc, rc):
                c = (rc % 2) * 8
                d_xa.w(sc, rc, bdxa)
                if rc >= 2:
                    v_st.w(sc, bvst + rc - 1)
                a_sq.inc(sc.activation(
                    xsq[:, 0:E],
                    xa[:, (rc % 2) * E:(rc % 2 + 1) * E], AFT.Square,
                    accum_out=stat[:, c + 0:c + 1]))
                sc.drain()

            def emit_sqrt(sc, rc):
                c = (rc % 2) * 8
                v_st.w(sc, bvst + rc + 1)
                if rc >= 2:
                    v_ln.w(sc, bvln + rc - 1)
                a_rs.inc(sc.activation(
                    stat[:, c + 5:c + 6], stat[:, c + 4:c + 5], AFT.Sqrt,
                    bias=eps_t[:, 0:1], scale=1.0))

            @blk.scalar
            def _(sc):
                s_g.w(sc, 1)
                v_res.w(sc, prev_res)
                for s_ in o_wq.s:
                    if s_.n > 0:
                        sc.wait_ge(s_.h, s_.n)
                v_wq.w(sc)
                for rc in range(NRC):
                    emit_sq(sc, rc)
                    if rc >= 1:
                        emit_sqrt(sc, rc - 1)
                emit_sqrt(sc, NRC - 1)

            def emit_stats2(v, rc):
                c = (rc % 2) * 8
                a_sq.w(v, basq + rc + 1)
                if rc >= 2:
                    a_rs.w(v, bars + rc - 1)
                if rc >= 2:
                    v_ln.w(v, bvln + rc - 1)
                v.reduce_sum(stat[:, c + 1:c + 2],
                             xa[:, (rc % 2) * E:(rc % 2 + 1) * E], axis=AX.X)
                v.drain()
                v.tensor_single_scalar(stat[:, c + 2:c + 3], stat[:, c + 1:c + 2],
                                       1.0 / E, ALU.mult)
                v.drain()
                v.tensor_tensor(stat[:, c + 3:c + 4], stat[:, c + 2:c + 3],
                                stat[:, c + 2:c + 3], ALU.mult)
                v.drain()
                v_st.inc(v.scalar_tensor_tensor(
                    stat[:, c + 4:c + 5], stat[:, c + 0:c + 1], 1.0 / E,
                    stat[:, c + 3:c + 4], ALU.mult, ALU.subtract))
                v.drain()

            def emit_ln(v, rc):
                c = (rc % 2) * 8
                a_rs.w(v, bars + rc + 1)
                if rc >= 1:
                    t_lt.w(v, btlt + 8 * rc)
                v.reciprocal(stat[:, c + 6:c + 7], stat[:, c + 5:c + 6])
                v.drain()
                v_ln.inc(v.tensor_scalar(
                    lnt[:], xa[:, (rc % 2) * E:(rc % 2 + 1) * E],
                    stat[:, c + 2:c + 3], stat[:, c + 6:c + 7],
                    ALU.subtract, ALU.mult))
                v.drain()



            ltp = [pb_q, pb_o, pb_s[0], pb_s[1]]

            def emit_lt(te, rc):
                v_ln.w(te, bvln + rc + 1)
                for ib in range(8):
                    tid = rc * 8 + ib
                    if tid >= 4:
                        v_ls.w(te, bvls + tid - 3)
                    t_lt.inc(te.transpose(
                        ltp[ib % 4][:, 0:128],
                        lnt[:, ib * 128:(ib + 1) * 128], id32[:]))

            def emit_pj(te, rcq):
                v_ls.w(te, bvls + 32 * rcq + 32)
                for lc in range(4):
                    pid = rcq * 4 + lc
                    pp = pb_av[lc]
                    if pid >= 4:
                        v_he.w(te, bvhe + pid - 3)
                    first = True
                    last = None
                    for (wop, rop) in ((pw_hi, ln_hi), (pw_hi, ln_lo), (pw_lo, ln_hi)):
                        for ic in range(8):
                            last = te.matmul(
                                pp[:, 0:512],
                                wop[:, ic * 512 + lc * 128:ic * 512 + (lc + 1) * 128],
                                rop[:, ic * 512:(ic + 1) * 512],
                                start=first, stop=(wop is pw_lo and ic == 7))
                            first = False
                    t_pj.inc(last)

            def emit_ls(v, rc):
                for ib in range(8):
                    tid = rc * 8 + ib
                    t_lt.w(v, btlt + tid + 1)
                    if ib == 0 and rc >= 4:
                        t_pj.w(v, btpj + 4 * (rc // 4))
                    src = ltp[ib % 4][:, 0:128]
                    dst = ib * 512 + (rc % 4) * 128
                    v.tensor_copy(ln_hi[:, dst:dst + 128], src)
                    v.drain()
                    v_ls.inc(v.tensor_tensor(
                        ln_lo[:, dst:dst + 128], src,
                        ln_hi[:, dst:dst + 128], ALU.subtract))
                    v.drain()

            def emit_he(v, rcq):
                for lc in range(4):
                    pid = rcq * 4 + lc
                    t_pj.w(v, btpj + pid + 1)
                    pp = pb_av[lc]
                    dst = lc * 4096 + rcq * 512
                    v.tensor_copy(h_hi[:, dst:dst + 512], pp[:, 0:512])
                    v.drain()
                    v_he.inc(v.tensor_tensor(
                        h_lo[:, dst:dst + 512], pp[:, 0:512],
                        h_hi[:, dst:dst + 512], ALU.subtract))
                    v.drain()

            @blk.tensor
            def _(te):
                s_dc.w(te)
                v_res.w(te, prev_res)
                v_nm.w(te, prev_nm)
                v_mh.w(te)
                g_cb.w(te)
                v_pw.w(te, bvpw + 8)
                v_ve.w(te)
                t_av.w(te)
                v_af.w(te, prev_af_h)
                for rc in range(NRC):
                    if rc >= 4 and rc % 4 == 0:
                        emit_pj(te, rc // 4 - 1)
                    emit_lt(te, rc)
                emit_pj(te, NRC // 4 - 1)

            @blk.vector
            def _(v):
                for rc in range(NRC + 3):
                    if rc < NRC:
                        emit_stats2(v, rc)
                    if 1 <= rc < NRC + 1:
                        emit_ln(v, rc - 1)
                    if 1 <= rc < NRC + 1:
                        emit_ls(v, rc - 1)
                    if rc >= 6 and (rc - 6) % 4 == 0:
                        emit_he(v, (rc - 6) // 4)

            @blk.vector
            def _(v):
                v_he.w(v, bvhe + (NRC // 4) * 4)
                for lc in range(4):
                    v.reduce_max(mh_t[:, lc:lc + 1], h_hi[:, lc * 4096:(lc + 1) * 4096],
                                 axis=AX.X, apply_absolute_value=True)
                v.drain()
                v_mh.inc(v.tensor_single_scalar(mh_t[:], mh_t[:], 1.01, ALU.mult))

            @blk.gpsimd
            def _(gp, g=g):
                v_mh.w(gp)
                if g >= 1:
                    v_mh.w(gp)
                g_mh.inc(gp.dma_start(out=mh_row[0:1, 0:512], in_=mh_t[:, 0:4]), 16)

            @blk.vector
            def _(v, g=g):
                g_mh.w(v)
                mv = mh_row.ap().rearrange("o (p l) -> o l p", l=4)
                for lc in range(4):
                    for j in range(2):
                        p = 4 * g + lc
                        v.reduce_max(
                            mkrow[0:1, 2 * p + j:2 * p + j + 1],
                            mv[0:1, lc:lc + 1, 64 * j:64 * (j + 1)], axis=AX.X)
                v_mh.inc(v.drain())

        def emit_attention(g, lp, T):
            (aw_f, vw_f, ow_f, aw_hi, aw_lo, vw_hi, vw_lo, ow_hi, ow_lo,
             q_hi, q_lo, w32, mq_t, cbias, csc, crow,
             attn_tm, attn_hi, attn_lo, rr_t, rhib, fbA_t, fbB_t, attnB_s) = T
            p = 4 * g + lp
            hofs = lp * 4096
            bdaw, bvaw = d_aw.n, v_aw.n
            btv, bvve = t_v.n, v_ve.n
            hdone = v_he.n
            prev_nm = v_nm.n
            prev_res = v_res.n
            prev_res2 = v_res.n
            prev_op = t_op.n
            prev_at = t_at.n
            prev_af = v_af.n
            prev_s = t_s.n
            prev_e = a_e.n

            @blk.sync
            def _(sy):
                t_pj.w(sy)
                v_aw.w(sy, bvaw)
                d_aw.inc(sy.dma_start(out=aw_f[:], in_=awp_h[p * 128:(p + 1) * 128, :]), 16)
                d_aw.inc(sy.dma_start(out=vw_f[:], in_=vwp_h[p * 128:(p + 1) * 128, :]), 16)
                d_aw.inc(sy.dma_start(out=ow_f[:], in_=owT_h[p * 128:(p + 1) * 128, :]), 16)

            @blk.vector
            def _(v):
                d_aw.w(v, bdaw + 48)
                t_q.w(v)
                t_v.w(v, btv)
                t_op.w(v, prev_op)
                t_av.w(v)
                v.tensor_copy(aw_hi[:], aw_f[:])
                v.tensor_copy(vw_hi[:], vw_f[:])
                v.tensor_copy(ow_hi[:], ow_f[:])
                v.drain()
                v.tensor_tensor(aw_lo[:], aw_f[:], aw_hi[:], ALU.subtract)
                v.tensor_tensor(vw_lo[:], vw_f[:], vw_hi[:], ALU.subtract)
                v.tensor_tensor(ow_lo[:], ow_f[:], ow_hi[:], ALU.subtract)
                v_aw.inc(v.drain())

            @blk.tensor
            def _(te):
                v_aw.w(te, bvaw + 1)
                v_he.w(te, hdone)
                v_af.w(te, prev_af)
                v_res.w(te, prev_res2)
                t_pj.w(te)
                for rc in range(NRC):
                    if rc >= 2:
                        v_ve.w(te, bvve + rc - 1)
                    last = None
                    for (tp, hp, pav) in (((0, 0), 0, pb_av[rc % 2]),
                                          ((64, 0), 64, pb_av[2 + rc % 2])):
                        first = True
                        for (hop, vop) in ((h_hi, vw_hi), (h_hi, vw_lo), (h_lo, vw_hi)):
                            last = te.matmul(
                                pav[:, 0:64],
                                hop[hp:hp + 64, hofs + rc * 128:hofs + (rc + 1) * 128],
                                vop[hp:hp + 64, :], start=first,
                                stop=(hop is h_lo), tile_position=tp)
                            first = False
                    t_v.inc(last)

            @blk.vector
            def _(v):
                for rc in range(NRC):
                    t_v.w(v, btv + rc + 1)
                    o = rc * 65
                    for (hp, vb) in ((0, vA), (64, vB)):
                        pav = pb_av[(0 if hp == 0 else 2) + rc % 2]
                        v.tensor_copy(vb[:, o:o + 64], pav[:, 0:64])
                        v.memset(vb[:, o + 64:o + 65], 1.0)
                    v_ve.inc(v.drain())

            def emit_resid(v, P):
                s_dx.w(v)
                for st in range(4):
                    for eh in range(2):
                        oid = 2 * st + eh
                        t_op.w(v, P['btop0'] + oid + 1)
                        pp = [pb_av[0], pb_av[1]][eh]
                        base = (P['scof'] // 128 + st) * E + eh * 512
                        xd = x_tm[:, base:base + 512]
                        v_res.inc(v.tensor_tensor(xd, xd, pp[:, 0:512], ALU.add))
                        v.drain()

            def emit_outproj(te, P):
                s_ob.w(te, 1)
                v_as.w(te, P['vas_mark'])
                v_af.w(te, P['vaf_mark'])
                for st in range(4):
                    for eh in range(2):
                        oid = 2 * st + eh
                        if oid >= 2:
                            v_res.w(te, P['bvres_r'] + oid - 1)
                        pp = [pb_av[0], pb_av[1]][eh]
                        ops = [(attn_hi, ow_hi, False), (attn_hi, ow_lo, False),
                               (attn_lo, ow_hi, p != 7)]
                        first = True
                        last = None
                        for (a, o, stop) in ops:
                            last = te.matmul(pp[:, 0:512],
                                             a[:, st * 128:(st + 1) * 128],
                                             o[:, eh * 512:(eh + 1) * 512],
                                             start=first, stop=stop)
                            first = False
                        if p == 7:
                            te.matmul(pp[:, 0:512], onesb[0:1, :],
                                      obhi[0:1, eh * 512:(eh + 1) * 512],
                                      start=False, stop=False)
                            last = te.matmul(pp[:, 0:512], onesb[0:1, :],
                                             oblo[0:1, eh * 512:(eh + 1) * 512],
                                             start=False, stop=True)
                        t_op.inc(last)

            def emit_qhead_te(scof_n, gcb_mark):
                hm = dict(btq=t_q.n, bvqe=v_qe.n, btct=t_ct.n, bvcd=v_cd.n,
                          scof=scof_n)

                @blk.tensor
                def _(te):
                    g_cb.w(te, gcb_mark)
                    v_ls.w(te)
                    v_aw.w(te, bvaw + 1)
                    v_he.w(te, hdone)
                    last = None
                    for (tp, hp) in (((0, 0), 0), ((64, 64), 64)):
                        first = True
                        for (a, hh) in ((aw_hi, h_hi), (aw_hi, h_lo), (aw_lo, h_hi)):
                            last = te.matmul(
                                pb_q[hp:hp + 64, 0:512],
                                a[hp:hp + 64, :],
                                hh[hp:hp + 64, hofs + scof_n:hofs + scof_n + 512],
                                start=first, stop=(a is aw_lo), tile_position=tp)
                            first = False
                    t_q.inc(last)
                return hm

            def emit_qhead_dve(hm, bts_gate):
                qo = ((hm['scof'] // 512) % 2) * 512

                @blk.vector
                def _(v):
                    t_q.w(v, hm['btq'] + 1)
                    t_s.w(v, bts_gate)
                    t_ct.w(v)
                    v.tensor_copy(q_hi[:, qo:qo + 512], pb_q[:, 0:512])
                    v.drain()
                    v.tensor_tensor(q_lo[:, qo:qo + 512], pb_q[:, 0:512],
                                    q_hi[:, qo:qo + 512], ALU.subtract)
                    v.reduce_max(mq_t[:], pb_q[:, 0:512], axis=AX.X,
                                 apply_absolute_value=True)
                    v_qe.inc(v.drain())

                @blk.gpsimd
                def _(gp):
                    v_qe.w(gp, hm['bvqe'] + 1)
                    v_cd.w(gp, hm['bvcd'])
                    t_ct.inc(gp.dma_start(out=mq_row[0:1, 0:128],
                                          in_=mq_t[:, 0:1]), 16)

                @blk.vector
                def _(v):
                    t_ct.w(v, hm['btct'] + 16)
                    g_mh.w(v)
                    t_cb.w(v)
                    v.reduce_max(csc[0:1, 0:1], mq_row[0:1, 0:64], axis=AX.X)
                    v.reduce_max(csc[0:1, 1:2], mq_row[0:1, 64:128], axis=AX.X)
                    v.drain()
                    v.tensor_tensor(csc[0:1, 2:3], csc[0:1, 0:1],
                                    mkrow[0:1, 2 * p:2 * p + 1], ALU.mult)
                    v.tensor_tensor(csc[0:1, 3:4], csc[0:1, 1:2],
                                    mkrow[0:1, 2 * p + 1:2 * p + 2], ALU.mult)
                    v.drain()
                    v.tensor_single_scalar(csc[0:1, 4:5], csc[0:1, 2:3], -1.0, ALU.mult)
                    v.tensor_single_scalar(csc[0:1, 5:6], csc[0:1, 3:4], -1.0, ALU.mult)
                    v.drain()
                    v.tensor_single_scalar(crow[0:1, 0:128], ones_row[0:1, :],
                                           csc[0:1, 4:5], ALU.mult)
                    v.tensor_single_scalar(crow[0:1, 128:256], ones_row[0:1, :],
                                           csc[0:1, 5:6], ALU.mult)
                    v_cd.inc(v.drain())

            pend = None
            curhead = emit_qhead_te(0, g_cb.n)
            emit_qhead_dve(curhead, t_s.n)
            for sc_i in range(4):
                scof = sc_i * 512
                scpar = sc_i % 2
                qo = scpar * 512
                hm = curhead
                bts, bae, btav, bvw = t_s.n, a_e.n, t_av.n, v_w.n
                bvnm, btat, bvaf = v_nm.n, t_at.n, v_af.n
                bgcb = g_cb.n

                @blk.tensor
                def _(te):
                    def emit_av(rc):
                        for hidx in range(2):
                            a_e.w(te, bae + 2 * rc + hidx + 1)
                            if rc == 0:
                                v_af.w(te, bvaf)
                            wofs = (2 * (rc % 2) + hidx) * 512
                            vb = [vA, vB][hidx]
                            o = rc * 65
                            last = te.matmul(
                                pb_av[2 + hidx][0:65, 0:512],
                                vb[:, o:o + 65],
                                w32[:, wofs:wofs + 512],
                                start=(rc == 0), stop=(rc == NRC - 1))
                        t_av.inc(last)

                    v_ve.w(te, bvve + NRC)
                    v_qe.w(te, hm['bvqe'] + 1)
                    a_e.w(te, bae)
                    v_ls.w(te)
                    v_mh.w(te)
                    for rc in range(NRC):
                        for hidx in range(2):
                            if rc >= 1:
                                a_e.w(te, bae + 2 * (rc - 1) + hidx + 1)
                            first = True
                            last = None
                            for (kop, qop) in ((h_hi, q_hi), (h_hi, q_lo), (h_lo, q_hi)):
                                last = te.matmul(
                                    pb_s[hidx][:, 0:512],
                                    kop[hidx * 64:hidx * 64 + 64,
                                        hofs + rc * 128:hofs + (rc + 1) * 128],
                                    qop[hidx * 64:hidx * 64 + 64, qo:qo + 512],
                                    start=first, stop=(kop is h_lo),
                                    tile_position=(hidx * 64, 0))
                                first = False
                            t_s.inc(last)
                        if rc == 0:
                            v_cd.w(te, hm['bvcd'] + 1)
                            g_cb.w(te, bgcb)
                            t_cb.inc(te.transpose(pb_o[:, 0:1], crow[0:1, 0:128],
                                                  id32[0:1, 0:1]))
                            t_cb.inc(te.transpose(pb_q[:, 0:1], crow[0:1, 128:256],
                                                  id32[0:1, 0:1]))
                        if rc == 1 and pend is not None:
                            emit_outproj(te, pend)
                        if rc >= 1:
                            emit_av(rc - 1)
                    emit_av(NRC - 1)

                @blk.vector
                def _(v):
                    t_cb.w(v)
                    a_e.w(v, bae)
                    v.tensor_copy(cbias[:, 2 * scpar:2 * scpar + 1], pb_o[:, 0:1])
                    g_cb.inc(v.tensor_copy(cbias[:, 2 * scpar + 1:2 * scpar + 2],
                                           pb_q[:, 0:1]))

                @blk.scalar
                def _(sc):
                    g_cb.w(sc, bgcb + 1)
                    for rc in range(NRC):
                        for hidx in range(2):
                            t_s.w(sc, bts + 2 * rc + hidx + 1)
                            if rc >= 2:
                                t_av.w(sc, btav + rc - 1)
                            elif btav > 0:
                                t_av.w(sc, btav + rc - 1)
                            slot = (2 * (rc % 2) + hidx) * 512
                            a_e.inc(sc.activation(
                                w32[:, slot:slot + 512], pb_s[hidx][:, 0:512], AFT.Exp,
                                bias=cbias[:, 2 * scpar + hidx:2 * scpar + hidx + 1],
                                scale=1.0 / 64))

                if sc_i < 3:
                    nexthead = emit_qhead_te(scof + 512, g_cb.n)

                if pend is not None:
                    @blk.vector
                    def _(v, P=pend):
                        emit_resid(v, P)

                @blk.vector
                def _(v):
                    t_av.w(v, btav + NRC)
                    t_at.w(v, btat)
                    for hidx in range(2):
                        v.reciprocal(rr_t[64:65, hidx * 512:(hidx + 1) * 512],
                                     pb_av[2 + hidx][64:65, 0:512])
                    v.drain()
                    for hidx in range(2):
                        v.tensor_copy(rhib[64:65, hidx * 1024:hidx * 1024 + 512],
                                      rr_t[64:65, hidx * 512:(hidx + 1) * 512])
                    v.drain()
                    for hidx in range(2):
                        v.tensor_tensor(rhib[64:65, hidx * 1024 + 512:hidx * 1024 + 1024],
                                        rr_t[64:65, hidx * 512:(hidx + 1) * 512],
                                        rhib[64:65, hidx * 1024:hidx * 1024 + 512],
                                        ALU.subtract)
                    v_nm.inc(v.drain())

                @blk.tensor
                def _(te):
                    v_nm.w(te, bvnm + 1)
                    v_res.w(te)
                    for hidx in range(2):
                        te.matmul(pb_av[hidx][:, 0:512],
                                  onesb_t[64:65, 0:128],
                                  rhib[64:65, hidx * 1024:hidx * 1024 + 512],
                                  start=True, stop=False, tile_position=(64, 0))
                        t_at.inc(te.matmul(
                            pb_av[hidx][:, 0:512],
                            onesb_t[64:65, 0:128],
                            rhib[64:65, hidx * 1024 + 512:hidx * 1024 + 1024],
                            start=False, stop=True, tile_position=(64, 0)))

                @blk.vector
                def _(v):
                    t_at.w(v, btat + 1)
                    v.tensor_copy(fbA_t[:], pb_av[0][:, 0:512])
                    v_af.inc(v.drain())
                    t_at.w(v, btat + 2)
                    v.tensor_copy(fbB_t[:], pb_av[1][:, 0:512])
                    v_af.inc(v.drain())
                    v.tensor_tensor(attn_tm[0:64, 0:512], pb_av[2][0:64, 0:512],
                                    fbA_t[0:64, 0:512], ALU.mult)
                    v_af.inc(v.drain())
                    v.tensor_tensor(attnB_s[0:64, 0:512], pb_av[3][0:64, 0:512],
                                    fbB_t[0:64, 0:512], ALU.mult)
                    v_af.inc(v.drain())

                if sc_i < 3:
                    emit_qhead_dve(nexthead, bts)
                    curhead = nexthead

                @blk.gpsimd
                def _(gp):
                    v_af.w(gp, bvaf + 4)
                    d_pk.inc(gp.dma_start(out=attn_tm[64:128, 0:512],
                                          in_=attnB_s[0:64, 0:512]), 16)

                @blk.vector
                def _(v):
                    d_pk.w(v)
                    t_op.w(v)
                    v.tensor_copy(attn_hi[:], attn_tm[:])
                    v.drain()
                    v.tensor_tensor(attn_lo[:], attn_tm[:], attn_hi[:], ALU.subtract)
                    v_as.inc(v.drain())

                pend = dict(scof=scof, btop0=t_op.n, vas_mark=v_as.n,
                            vaf_mark=v_af.n - 2, bvres_r=v_res.n)

            if pend is not None:
                @blk.vector
                def _(v, P=pend):
                    emit_resid(v, P)

                @blk.tensor
                def _(te, P=pend):
                    emit_outproj(te, P)
                pend = None

        ngroups = 1 if stage in ('h0', 'a0') else 2
        for g in range(ngroups):
            with ExitStack() as hes:
                emit_hbuild(g, hes)
            if stage == 'h0':
                break
            with ExitStack() as bes:
                bsb = lambda n, s, d: bes.enter_context(
                    nc.sbuf_tensor(n + str(g), s, d))
                T = (bsb("aw_f", [128, 64], F32), bsb("vw_f", [128, 64], F32),
                     bsb("ow_f", [128, E], F32),
                     bsb("aw_hi", [128, 64], BF16), bsb("aw_lo", [128, 64], BF16),
                     bsb("vw_hi", [128, 64], BF16), bsb("vw_lo", [128, 64], BF16),
                     bsb("ow_hi", [128, E], BF16), bsb("ow_lo", [128, E], BF16),
                     bsb("q_hi", [128, 1024], BF16), bsb("q_lo", [128, 1024], BF16),
                     bsb("w32", [128, 4 * 512], F32),
                     bsb("mq_t", [128, 1], F32), bsb("cbias", [128, 4], F32),
                     bsb("csc", [1, 8], F32), bsb("crow", [1, 256], F32),
                     bsb("attn_tm", [128, 512], F32),
                     bsb("attn_hi", [128, 512], BF16),
                     bsb("attn_lo", [128, 512], BF16),
                     bsb("rr_t", [128, 1024], F32),
                     bsb("rhib", [128, 2048], BF16),
                     bsb("fbA_t", [128, 512], F32), bsb("fbB_t", [128, 512], F32),
                     bsb("attnB_s", [128, 512], F32))
                for lp in range(4 if stage != 'a0' else 1):
                    emit_attention(g, lp, T)

    if stage in ('h0', 'a0'):
        with ExitStack() as _tes:
            _dummy = _tes.enter_context(nc.sbuf_tensor("dmy", [128, 8], F32))

            @blk.gpsimd
            def _(gp):
                if stage == 'h0':
                    v_he.w(gp)
                else:
                    v_res.w(gp)
                for t in range(2):
                    d_y.inc(gp.dma_start(out=y_h[t * 128:(t + 1) * 128, :],
                                         in_=x_tm[:, t * E:(t + 1) * E]), 16)
                gp.wait_ge(d_y.h, d_y.n)
        es.close()
        return nc

    if stage == 'attn':
        all_res_a = v_res.n

        @blk.gpsimd
        def _(gp):
            v_res.w(gp, all_res_a)
            for t in range(16):
                d_y.inc(gp.dma_start(out=y_h[t * 128:(t + 1) * 128, :],
                                     in_=x_tm[:, t * E:(t + 1) * E]), 16)
            gp.wait_ge(d_y.h, d_y.n)
        es.close()
        return nc

    with ExitStack() as fes:
        fsb = lambda n, s, d: fes.enter_context(nc.sbuf_tensor(n, s, d))
        f_ln = fsb("f_ln", [128, 2 * E], F32)
        f_xq = fsb("f_xq", [128, 2 * E], F32)
        f_t = fsb("f_t", [128, E], F32)
        f_u = fsb("f_u", [128, E], F32)
        f_v = fsb("f_v", [128, E], F32)
        f_xqf = fsb("f_xqf", [128, 2 * 8 * FSC], BF16)
        f_gu = fsb("f_gu", [128, NOC * FSC], F32)
        f_guq = fsb("f_guq", [128, NOC * FSC], BF16)
        f_wg = fsb("f_wg", [128, 2 * 8 * 128], BF16)
        f_wu = fsb("f_wu", [128, 2 * 8 * 128], BF16)
        f_wd = fsb("f_wd", [128, 2 * 32 * 128], BF16)
        f_do = fsb("f_do", [128, NEC * FSC], F32)
        f_gr = fsb("f_gr", [128, FSC], F32)
        f_si = fsb("f_si", [128, FSC], F32)
        f_fb = fsb("f_fb", [128, FSC], F32)
        f_sb = fsb("f_sb", [128, FSC], F32)
        f_mx = fsb("f_mx", [128, FSC], F32)
        f_mx2 = fsb("f_mx2", [128, FSC], F32)
        f_sc = fsb("f_sc", [128, 16], F32)
        f_st = fsb("f_st", [128, 20], F32)
        f_row = fsb("f_row", [1, 2 * FSC], F32)
        f_rhi = fsb("f_rhi", [1, FSC], BF16)
        f_rlo = fsb("f_rlo", [1, FSC], BF16)
        f_m2r = fsb("f_m2r", [1, 512], F32)
        f_rhi2 = fsb("f_rhi2", [1, FSC], BF16)
        f_rlo2 = fsb("f_rlo2", [1, FSC], BF16)
        f_mgr = fsb("f_mgr", [1, 256], F32)
        f_for = fsb("f_for", [1, 256], F32)
        f_trw = fsb("f_trw", [1, 256], F32)

        pf_g = fes.enter_context(nc.psum_tensor("pf_g", [128, FSC], F32))
        pf_u = fes.enter_context(nc.psum_tensor("pf_u", [128, FSC], F32))
        pf_d0 = fes.enter_context(nc.psum_tensor("pf_d0", [128, FSC], F32))
        pf_d1 = fes.enter_context(nc.psum_tensor("pf_d1", [128, FSC], F32))
        pf_t0 = fes.enter_context(nc.psum_tensor("pf_t0", [128, 128], F32))
        pf_t1 = fes.enter_context(nc.psum_tensor("pf_t1", [128, 128], F32))
        pf_b = fes.enter_context(nc.psum_tensor("pf_b", [128, FSC], F32))
        pf_r = fes.enter_context(nc.psum_tensor("pf_r", [128, 128], F32))

        all_res = v_res.n
        wq_done = o_wq.base()

        def wq_wait(eng):
            for i, s in enumerate(o_wq.s):
                if wq_done[i] > 0:
                    eng.wait_ge(s.h, wq_done[i])

        if stage == 'ffndbg':
            for nm, sh, dt in [("dxq", [128, 2048], F32), ("dsc", [128, 16], F32),
                               ("dst", [128, 2], F32), ("dgu", [128, 8192], F32),
                               ("dguq", [128, 8192], BF16), ("dfb", [128, 256], F32),
                               ("dsb", [128, 256], F32), ("ddo", [128, 2048], F32),
                               ("dxqf", [128, 2048], BF16), ("dln", [128, 2048], F32)]:
                dbg_aps[nm] = nc.dram_tensor(nm, sh, dt, kind="ExternalOutput").ap()

        def snapA():
            return dict(basq=a_f1.n, bvst=v_f1.n, bars=a_f2.n, bvfq=v_fq.n,
                        btxt=t_xt.n, bvxe=v_xe.n, btfr=t_fr.n, bvfr=v_fr.n)

        def A_stats_dve(fcn, fan, fac):
            @blk.vector
            def _(v):
                for st2 in range(2):
                    gst = fcn * 2 + st2
                    cs = st2 * 8
                    a_f1.w(v, fan['basq'] + st2 + 1)
                    if fac is not None:
                        a_f2.w(v, fac['bars'] + 2)
                    v.reduce_sum(f_st[:, cs + 1:cs + 2],
                                 x_tm[:, gst * E:(gst + 1) * E], axis=AX.X)
                    v.drain()
                    v.tensor_single_scalar(f_st[:, cs + 2:cs + 3],
                                           f_st[:, cs + 1:cs + 2], 1.0 / E, ALU.mult)
                    v.drain()
                    v.tensor_tensor(f_st[:, cs + 3:cs + 4], f_st[:, cs + 2:cs + 3],
                                    f_st[:, cs + 2:cs + 3], ALU.mult)
                    v.drain()
                    v_f1.inc(v.scalar_tensor_tensor(
                        f_st[:, cs + 4:cs + 5], f_st[:, cs + 0:cs + 1], 1.0 / E,
                        f_st[:, cs + 3:cs + 4], ALU.mult, ALU.subtract))
                    v.drain()

        def A_quant_dve(fcn, fan, fac):
            @blk.vector
            def _(v):
                s_g.w(v, 1)
                a_f2.w(v, fan['bars'] + 2)
                a_f1.w(v, fan['basq'] + 2)
                if fac is not None:
                    t_fr.w(v, fac['btfr'] + 2)
                    t_xt.w(v, fac['btxt'] + 16)
                for st2 in range(2):
                    gst = fcn * 2 + st2
                    cs = st2 * 8
                    c0 = st2 * 8
                    v.reciprocal(f_st[:, cs + 6:cs + 7], f_st[:, cs + 5:cs + 6])
                    v.drain()
                    v.tensor_scalar(f_ln[:, st2 * E:(st2 + 1) * E],
                                    x_tm[:, gst * E:(gst + 1) * E],
                                    f_st[:, cs + 2:cs + 3], f_st[:, cs + 6:cs + 7],
                                    ALU.subtract, ALU.mult)
                    v_f2.inc(v.drain())
                    v.reduce_max(f_sc[:, c0:c0 + 1], f_ln[:, st2 * E:(st2 + 1) * E],
                                 axis=AX.X, apply_absolute_value=True)
                    v.drain()
                    v.tensor_scalar(f_sc[:, c0 + 1:c0 + 2], f_sc[:, c0:c0 + 1],
                                    1.0, 1e-5, ALU.mult, ALU.max)
                    v.drain()
                    v.reciprocal(f_sc[:, c0 + 2:c0 + 3], f_sc[:, c0 + 1:c0 + 2])
                    v.drain()
                    v.tensor_single_scalar(f_sc[:, c0 + 3:c0 + 4],
                                           f_sc[:, c0 + 2:c0 + 3], 127.0, ALU.mult)
                    v.drain()
                    v.tensor_single_scalar(f_t[:], f_ln[:, st2 * E:(st2 + 1) * E],
                                           f_sc[:, c0 + 3:c0 + 4], ALU.mult)
                    v.drain()
                    v.tensor_scalar(f_u[:], f_t[:], M23, -M23, ALU.add, ALU.add)
                    v.tensor_scalar(f_v[:], f_t[:], M52, -M52, ALU.add, ALU.add)
                    v.drain()
                    v.scalar_tensor_tensor(f_xq[:, st2 * E:(st2 + 1) * E],
                                           f_u[:], h_t[:, 0:1], f_v[:],
                                           ALU.mult, ALU.add)
                    v_fq.inc(v.drain())

        def A_xqT_te(fcn, fan):
            @blk.tensor
            def _(te):
                v_fq.w(te, fan['bvfq'] + 2)
                t_fg.w(te)
                v_dt.w(te)
                for st2 in range(2):
                    for ib in range(8):
                        tid = st2 * 8 + ib
                        if tid >= 2:
                            v_xe.w(te, fan['bvxe'] + tid - 1)
                        t_xt.inc(te.transpose(
                            [pf_t0, pf_t1][tid % 2][:, 0:128],
                            f_xq[:, st2 * E + ib * 128:st2 * E + (ib + 1) * 128],
                            id32[:]))

        def A_xqT_dve(fcn, fan):
            xp = (fcn % 2) * 8 * FSC
            @blk.vector
            def _(v):
                for st2 in range(2):
                    for ib in range(8):
                        tid = st2 * 8 + ib
                        t_xt.w(v, fan['btxt'] + tid + 1)
                        v_xe.inc(v.tensor_copy(
                            f_xqf[:, xp + ib * FSC + st2 * 128:
                                  xp + ib * FSC + (st2 + 1) * 128],
                            [pf_t0, pf_t1][tid % 2][:, 0:128]))

        def A_frT_te(fcn, fan):
            @blk.tensor
            def _(te):
                for st2 in range(2):
                    tid = 16 + st2
                    v_xe.w(te, fan['bvxe'] + tid - 1)
                    t_fr.inc(te.transpose(
                        [pf_t0, pf_t1][tid % 2][0:1, 0:128],
                        f_sc[:, st2 * 8 + 1:st2 * 8 + 2], id32[:]))

        def A_fr_dve(fcn, fan, btbc_cur):
            mp = (fcn % 2) * 256
            @blk.vector
            def _(v):
                if btbc_cur is not None:
                    t_bc.w(v, btbc_cur + 1)
                for st2 in range(2):
                    tid = 16 + st2
                    t_fr.w(v, fan['btfr'] + st2 + 1)
                    v_fr.inc(v.tensor_copy(
                        f_m2r[0:1, mp + st2 * 128:mp + (st2 + 1) * 128],
                        [pf_t0, pf_t1][tid % 2][0:1, 0:128]))
                v.drain()
                v.tensor_scalar(f_row[0:1, 0:FSC], f_m2r[0:1, mp:mp + FSC],
                                wbar[0:1, 0:1], 1.0 / 127, ALU.mult, ALU.mult)
                v.drain()
                v.tensor_copy(f_rhi2[:], f_row[0:1, 0:FSC])
                v.drain()
                v_fr.inc(v.tensor_tensor(f_rlo2[:], f_row[0:1, 0:FSC], f_rhi2[:],
                                         ALU.subtract))

        fC = snapA()

        @blk.scalar
        def _(sc):
            v_res.w(sc, all_res)
            wq_wait(sc)
            for st2 in range(2):
                a_f1.inc(sc.activation(
                    f_ln[:, st2 * E:(st2 + 1) * E],
                    x_tm[:, st2 * E:(st2 + 1) * E], AFT.Square,
                    accum_out=f_st[:, st2 * 8:st2 * 8 + 1]))
                sc.drain()

        A_stats_dve(0, fC, None)

        @blk.scalar
        def _(sc):
            for st2 in range(2):
                cs = st2 * 8
                v_f1.w(sc, fC['bvst'] + st2 + 1)
                a_f2.inc(sc.activation(
                    f_st[:, cs + 5:cs + 6], f_st[:, cs + 4:cs + 5], AFT.Sqrt,
                    bias=eps_t[:, 0:1], scale=1.0))
                sc.drain()

        A_quant_dve(0, fC, None)
        A_xqT_te(0, fC)
        A_xqT_dve(0, fC)
        A_frT_te(0, fC)
        A_fr_dve(0, fC, None)

        for fc in range(NFC if stage != 'ffndbg' else 1):
            btbc, bvbc = t_bc.n, v_bc.n
            bdfw, btfg, bvgr, basi, bvgu = d_fw.base(), t_fg.n, v_gr.n, a_si.n, v_gu.n
            btmx, bvmx = t_mx.n, v_mx.n
            btsr, bvsr, btsb, bvsb = t_sr.n, v_sr.n, t_sb.n, v_sb2.n
            btfo, bvfo = t_fo.n, v_fo.n
            bvgq, bdfd, btdn, bvde = v_gq.n, d_fd.base(), t_dn.n, v_de.n
            btdt, bvdt = t_dt.n, v_dt.n
            prev_dt = v_dt.n
            xp = (fc % 2) * 8 * FSC
            mp = (fc % 2) * 256
            last_fc = (fc == NFC - 1) or stage == 'ffndbg'
            fN = None if last_fc else snapA()

            @blk.tensor
            def _(te):
                v_fr.w(te, fC['bvfr'] + 3)
                v_sb2.w(te)
                if fc >= 1:
                    v_bc.w(te, bvbc)
                te.matmul(pf_b[:, 0:FSC], onesb[0:1, :], f_rhi2[0:1, :],
                          start=True, stop=False)
                t_bc.inc(te.matmul(pf_b[:, 0:FSC], onesb[0:1, :], f_rlo2[0:1, :],
                                   start=False, stop=True))

            @blk.vector
            def _(v):
                t_bc.w(v, btbc + 1)
                v_bc.inc(v.tensor_copy(f_fb[:], pf_b[:, 0:FSC]))
                t_mx.w(v, btmx)
                v.memset(f_mx[:], 0.0)
                v.memset(f_mx2[:], 0.0)
                v_bc.inc(v.drain())

            @blk.sync
            def _(sy):
                wq_wait(sy)
                v_res.w(sy, all_res)
                for oc in range(NOC):
                    if oc >= 2:
                        t_fg.w(sy, btfg + oc - 1)
                    buf = (oc % 2) * 1024
                    d_fw.inc(sy.dma_start(
                        out=f_wg[:, buf:buf + 1024].rearrange("p (a c) -> p a c", c=128),
                        in_=wgq_h[:, oc, :, :]), oc)
                    d_fw.inc(sy.dma_start(
                        out=f_wu[:, buf:buf + 1024].rearrange("p (a c) -> p a c", c=128),
                        in_=wuq_h[:, oc, :, :]), oc)

            @blk.scalar
            def _(sc):
                if fN is not None:
                    v_res.w(sc, all_res)
                    v_f1.w(sc, fC['bvst'] + 2)
                    v_fq.w(sc, fC['bvfq'] + 2)
                    for st2 in range(2):
                        gst = (fc + 1) * 2 + st2
                        a_f1.inc(sc.activation(
                            f_ln[:, st2 * E:(st2 + 1) * E],
                            x_tm[:, gst * E:(gst + 1) * E], AFT.Square,
                            accum_out=f_st[:, st2 * 8:st2 * 8 + 1]))
                        sc.drain()
                for oc in range(NOC):
                    v_gr.w(sc, bvgr + oc + 1)
                    if oc >= 1:
                        v_gu.w(sc, bvgu + oc)
                    a_si.inc(sc.activation(f_si[:], f_gr[:], AFT.Silu))
                    if oc == 24 and fN is not None:
                        for st2 in range(2):
                            cs = st2 * 8
                            v_f1.w(sc, fN['bvst'] + st2 + 1)
                            a_f2.inc(sc.activation(
                                f_st[:, cs + 5:cs + 6], f_st[:, cs + 4:cs + 5],
                                AFT.Sqrt, bias=eps_t[:, 0:1], scale=1.0))
                            sc.drain()

            @blk.tensor
            def _(te):
                v_xe.w(te, fC['bvxe'] + 16)
                if fc >= 1:
                    v_gu.w(te, bvgu)
                for oc in range(NOC):
                    d_fw.w(te, oc, bdfw, per=32)
                    if oc >= 2:
                        v_gr.w(te, bvgr + oc - 1)
                        v_gu.w(te, bvgu + oc - 1)
                    buf = (oc % 2) * 1024
                    pg = [pf_g, pf_d0][oc % 2]
                    pu = [pf_u, pf_d1][oc % 2]
                    for ic in range(8):
                        te.matmul(pg[:, 0:FSC],
                                  f_wg[:, buf + ic * 128:buf + (ic + 1) * 128],
                                  f_xqf[:, xp + ic * FSC:xp + (ic + 1) * FSC],
                                  start=(ic == 0), stop=(ic == 7))
                    last = None
                    for ic in range(8):
                        last = te.matmul(pu[:, 0:FSC],
                                         f_wu[:, buf + ic * 128:buf + (ic + 1) * 128],
                                         f_xqf[:, xp + ic * FSC:xp + (ic + 1) * FSC],
                                         start=(ic == 0), stop=(ic == 7))
                    t_fg.inc(last)

            def fold_iter(v, oc):
                if oc < NOC:
                    t_fg.w(v, btfg + oc + 1)
                    v_bc.w(v, bvbc + 2)
                    if oc >= 1:
                        a_si.w(v, basi + oc)
                    pg = [pf_g, pf_d0][oc % 2]
                    v_gr.inc(v.tensor_tensor(f_gr[:], pg[:, 0:FSC], f_fb[:], ALU.mult))
                    v.drain()
                if oc >= 1:
                    a_si.w(v, basi + oc)
                    pu = [pf_u, pf_d1][(oc - 1) % 2]
                    v.tensor_tensor(f_gu[:, (oc - 1) * FSC:oc * FSC], f_si[:],
                                    pu[:, 0:FSC], ALU.mult)
                    v.drain()
                    v.tensor_tensor(f_mx[:], f_mx[:],
                                    f_gu[:, (oc - 1) * FSC:oc * FSC], ALU.max)
                    v.scalar_tensor_tensor(f_mx2[:],
                                           f_gu[:, (oc - 1) * FSC:oc * FSC],
                                           -1.0, f_mx2[:], ALU.mult, ALU.max)
                    v_gu.inc(v.drain())

            @blk.vector
            def _(v):
                for oc in range(0, 17):
                    fold_iter(v, oc)

            if fN is not None:
                A_stats_dve(fc + 1, fN, fC)

            @blk.vector
            def _(v):
                for oc in range(17, NOC + 1):
                    fold_iter(v, oc)
                v.tensor_tensor(f_mx[:], f_mx[:], f_mx2[:], ALU.max)
                v_gu.inc(v.drain())

            if fN is not None:
                A_quant_dve(fc + 1, fN, fC)

            @blk.tensor
            def _(te):
                v_gu.w(te, bvgu + NOC + 1)
                v_fr.w(te, fC['bvfr'] + 3)
                v_dt.w(te, prev_dt)
                for st2 in range(2):
                    t_mx.inc(te.transpose([pf_t0, pf_t1][st2][:, 0:128],
                                          f_mx[:, st2 * 128:(st2 + 1) * 128], id32[:]))

            @blk.vector
            def _(v):
                for st2 in range(2):
                    t_mx.w(v, btmx + st2 + 1)
                    c0 = st2 * 8
                    v.reduce_max(f_sc[:, c0 + 6:c0 + 7], [pf_t0, pf_t1][st2][:, 0:128],
                                 axis=AX.X)
                v_mx.inc(v.drain())

            @blk.tensor
            def _(te):
                v_mx.w(te, bvmx + 1)
                if fc >= 1:
                    v_fo.w(te, bvfo)
                for st2 in range(2):
                    if st2 >= 1:
                        v_sr.w(te, bvsr + 1)
                    t_sr.inc(te.transpose(pf_r[0:1, 0:128],
                                          f_sc[:, st2 * 8 + 6:st2 * 8 + 7], id32[:]))

            @blk.vector
            def _(v):
                v_dt.w(v, prev_dt)
                for st2 in range(2):
                    t_sr.w(v, btsr + st2 + 1)
                    v_sr.inc(v.tensor_copy(f_mgr[0:1, st2 * 128:(st2 + 1) * 128],
                                           pf_r[0:1, 0:128]))
                v.drain()
                v.tensor_tensor(f_trw[0:1, 0:FSC], f_mgr[0:1, 0:FSC],
                                f_m2r[0:1, mp:mp + FSC], ALU.mult)
                v.drain()
                v.tensor_scalar(f_trw[0:1, 0:FSC], f_trw[0:1, 0:FSC],
                                wbar[0:1, 1:2], 1.0 / 127, ALU.mult, ALU.mult)
                v.drain()
                v.tensor_scalar(f_trw[0:1, 0:FSC], f_trw[0:1, 0:FSC],
                                1.0, 1e-5, ALU.mult, ALU.max)
                v.drain()
                v.tensor_scalar(f_for[0:1, 0:FSC], f_trw[0:1, 0:FSC],
                                wbar[0:1, 2:3], 1.0 / 127, ALU.mult, ALU.mult)
                v.drain()
                v.reciprocal(f_trw[0:1, 0:FSC], f_trw[0:1, 0:FSC])
                v.drain()
                v.tensor_tensor(f_row[0:1, FSC:2 * FSC], f_m2r[0:1, mp:mp + FSC],
                                f_trw[0:1, 0:FSC], ALU.mult)
                v.drain()
                v.tensor_single_scalar(f_row[0:1, FSC:2 * FSC],
                                       f_row[0:1, FSC:2 * FSC],
                                       wbar[0:1, 1:2], ALU.mult)
                v.drain()
                v.tensor_copy(f_rhi[:], f_row[0:1, FSC:2 * FSC])
                v.drain()
                v_sr.inc(v.tensor_tensor(f_rlo[:], f_row[0:1, FSC:2 * FSC], f_rhi[:],
                                         ALU.subtract))

            @blk.tensor
            def _(te):
                v_sr.w(te, bvsr + 3)
                for st2 in range(2):
                    if st2 >= 1:
                        v_fo.w(te, bvfo + 1)
                    t_fo.inc(te.transpose(pf_r[:, 0:1],
                                          f_for[0:1, st2 * 128:(st2 + 1) * 128],
                                          id32[0:1, 0:1]))

            @blk.vector
            def _(v):
                for st2 in range(2):
                    t_fo.w(v, btfo + st2 + 1)
                    v_fo.inc(v.tensor_copy(f_st[:, 18 + st2:19 + st2],
                                           pf_r[:, 0:1]))

            @blk.tensor
            def _(te):
                v_sr.w(te, bvsr + 3)
                v_bc.w(te)
                te.matmul(pf_b[:, 0:FSC], onesb[0:1, :], f_rhi[0:1, :],
                          start=True, stop=False)
                t_sb.inc(te.matmul(pf_b[:, 0:FSC], onesb[0:1, :], f_rlo[0:1, :],
                                   start=False, stop=True))

            @blk.vector
            def _(v):
                t_sb.w(v, btsb + 1)
                v_sb2.inc(v.tensor_copy(f_sb[:], pf_b[:, 0:FSC]))

            @blk.vector
            def _(v):
                v_sb2.w(v, bvsb + 1)
                for oc in range(NOC):
                    if fc >= 1 and oc == 0:
                        t_dn.w(v, btdn)
                    v.tensor_tensor(f_gr[:], f_gu[:, oc * FSC:(oc + 1) * FSC],
                                    f_sb[:], ALU.mult)
                    v.drain()
                    v.tensor_scalar(f_si[:], f_gr[:], M23, -M23, ALU.add, ALU.add)
                    v.tensor_scalar(f_fb[:], f_gr[:], M17, c5217[:, 0:1],
                                    ALU.add, ALU.add)
                    v.drain()
                    v.tensor_single_scalar(f_fb[:], f_fb[:], -M52, ALU.add)
                    v.drain()
                    v_gq.inc(v.scalar_tensor_tensor(
                        f_guq[:, oc * FSC:(oc + 1) * FSC], f_si[:], h_t[:, 0:1],
                        f_fb[:], ALU.mult, ALU.add))
                    v.drain()

            @blk.sync
            def _(sy):
                v_res.w(sy, all_res)
                for ec in range(NEC):
                    if ec >= 2:
                        t_dn.w(sy, btdn + ec - 1)
                    d_fd.inc(sy.dma_start(
                        out=f_wd[:, (ec % 2) * 4096:(ec % 2 + 1) * 4096].rearrange(
                            "p (a c) -> p a c", c=128),
                        in_=wdq_h[:, ec, :, :]), ec)

            @blk.tensor
            def _(te):
                v_gq.w(te, bvgq + NOC)
                for ec in range(NEC):
                    d_fd.w(te, ec, bdfd)
                    if ec >= 2:
                        v_de.w(te, bvde + ec - 1)
                    pd = [pf_d0, pf_d1][ec % 2]
                    last = None
                    for ic in range(NOC):
                        last = te.matmul(
                            pd[:, 0:FSC],
                            f_wd[:, (ec % 2) * 4096 + ic * 128:
                                 (ec % 2) * 4096 + (ic + 1) * 128],
                            f_guq[:, ic * FSC:(ic + 1) * FSC],
                            start=(ic == 0), stop=(ic == NOC - 1))
                    t_dn.inc(last)

            @blk.vector
            def _(v):
                for ec in range(NEC):
                    t_dn.w(v, btdn + ec + 1)
                    pd = [pf_d0, pf_d1][ec % 2]
                    v_de.inc(v.tensor_copy(f_do[:, ec * FSC:(ec + 1) * FSC],
                                           pd[:, 0:FSC]))

            @blk.tensor
            def _(te):
                v_de.w(te, bvde + NEC)
                v_mx.w(te, bvmx + 1)
                for ec in range(NEC):
                    for st2 in range(2):
                        tid = ec * 2 + st2
                        if tid >= 2:
                            v_dt.w(te, bvdt + tid - 1)
                        t_dt.inc(te.transpose(
                            [pf_t0, pf_t1][tid % 2][:, 0:128],
                            f_do[:, ec * FSC + st2 * 128:ec * FSC + (st2 + 1) * 128],
                            id32[:]))

            @blk.vector
            def _(v):
                v_fo.w(v, bvfo + 2)
                for ec in range(NEC):
                    for st2 in range(2):
                        tid = ec * 2 + st2
                        t_dt.w(v, btdt + tid + 1)
                        gst = fc * 2 + st2
                        xd = x_tm[:, gst * E + ec * 128:gst * E + (ec + 1) * 128]
                        v_dt.inc(v.scalar_tensor_tensor(
                            xd, [pf_t0, pf_t1][tid % 2][:, 0:128],
                            f_st[:, 18 + st2:19 + st2], xd, ALU.mult, ALU.add))

            @blk.gpsimd
            def _(gp):
                v_dt.w(gp, bvdt + 16)
                for st2 in range(2):
                    gst = fc * 2 + st2
                    d_y.inc(gp.dma_start(out=y_h[gst * 128:(gst + 1) * 128, :],
                                         in_=x_tm[:, gst * E:(gst + 1) * E]), 16)

            if fN is not None:
                A_xqT_te(fc + 1, fN)
                A_xqT_dve(fc + 1, fN)
                A_frT_te(fc + 1, fN)
                A_fr_dve(fc + 1, fN, btbc)
                fC = fN

        if stage == 'ffndbg':
            @blk.gpsimd
            def _(gp):
                gp.wait_ge(d_y.h, d_y.n)
                for (nm, buf, w) in [("dxq", f_xq, 2048),
                                     ("dst", f_st[:, 18:20], 2), ("dgu", f_gu, 8192),
                                     ("dguq", f_guq, 8192), ("dfb", f_fb, 256),
                                     ("dsb", f_sb, 256), ("ddo", f_do, 2048),
                                     ("dxqf", f_xqf, 2048), ("dln", f_ln, 2048)]:
                    src_ap = buf if not hasattr(buf, "shape") or not hasattr(buf, "ap") else buf
                    try:
                        src_ap = buf[:, 0:w]
                    except Exception:
                        src_ap = buf
                    d_y.inc(gp.dma_start(out=dbg_aps[nm], in_=src_ap), 16)
                gp.wait_ge(d_y.h, d_y.n)

        @blk.gpsimd
        def _(gp):
            gp.wait_ge(d_y.h, d_y.n)

        if dbg:
            @blk.gpsimd
            def _(gp):
                gp.dma_start(out=dbg_h[0:128, 0:4096], in_=x_tm[:, 0:4096]).then_inc(d_y.h, 16)
                gp.wait_ge(d_y.h, d_y.n + 16)

    es.close()
    return nc


def kernel(L0_attn_x, L0_attn_proj_weight, L0_attn_attn_weight,
           L0_attn_value_weight, L0_attn_out_w, L0_attn_out_b,
           L0_ffn_w_gate, L0_ffn_w_up, L0_ffn_w_down):
    import ml_dtypes
    x = np.asarray(L0_attn_x, np.float32)
    pw = np.ascontiguousarray(np.asarray(L0_attn_proj_weight, np.float32)
                              .reshape(E, E))
    aw = np.ascontiguousarray(np.asarray(L0_attn_attn_weight, np.float32)
                              .reshape(8 * 128, 64))
    vw = np.ascontiguousarray(np.asarray(L0_attn_value_weight, np.float32)
                              .reshape(8 * 128, 64))
    owT = np.ascontiguousarray(np.asarray(L0_attn_out_w, np.float32).T)
    ob = np.ascontiguousarray(np.asarray(L0_attn_out_b, np.float32)
                              .reshape(1, E))
    wgT = np.ascontiguousarray(np.asarray(L0_ffn_w_gate, np.float32).T)
    wuT = np.ascontiguousarray(np.asarray(L0_ffn_w_up, np.float32).T)
    wdT = np.ascontiguousarray(np.asarray(L0_ffn_w_down, np.float32).T)
    id32 = np.eye(128, dtype=np.float32)
    idbf = np.eye(128, dtype=ml_dtypes.bfloat16)
    onesb = np.ones((1, 128), dtype=ml_dtypes.bfloat16)

    nc = build_program()
    in_maps = []
    for c in range(8):
        b, sh = c // 2, c % 2
        xr = np.ascontiguousarray(np.roll(x[b], -2048 * sh, axis=0))
        in_maps.append(dict(x_full=xr, pw=pw, awp=aw, vwp=vw, owT=owT, ob=ob,
                            wgT=wgT, wuT=wuT, wdT=wdT, id32=id32, idbf=idbf,
                            onesb=onesb))
    res = run_bass_kernel_spmd(nc, in_maps, core_ids=list(range(8)))
    y = np.empty((4, S, E), np.float32)
    for c in range(8):
        b, sh = c // 2, c % 2
        y[b, 2048 * sh:2048 * (sh + 1), :] = res.results[c]["y"]
    return y


# version 15
# speedup vs baseline: 1.0525x; 1.0232x over previous
import numpy as np
import concourse.bass as bass
import concourse.mybir as mybir
from concourse.bass_utils import run_bass_kernel_spmd
from contextlib import ExitStack

F32 = mybir.dt.float32
BF16 = mybir.dt.bfloat16
ALU = mybir.AluOpType
AFT = mybir.ActivationFunctionType
AX = mybir.AxisListType

M23 = 1.5 * 2.0**23
M52 = 1.5 * 2.0**52
M17 = 1.5 * 2.0**17

S, E = 4096, 1024
SEQ = 2048
INNER = 4096
NRC = 32
FSC = 256
NFC = SEQ // FSC
NOC = INNER // 128
NEC = E // 128


class Sem:
    __slots__ = ("h", "n")

    def __init__(self, es, nc, name):
        self.h = es.enter_context(nc.semaphore(name))
        self.n = 0

    def inc(self, instr, k=1):
        instr.then_inc(self.h, k)
        self.n += k
        return self.n

    def w(self, eng, mark=None):
        m = self.n if mark is None else mark
        if m > 0:
            eng.wait_ge(self.h, m)


class DSem:

    def __init__(self, es, nc, name, k):
        self.s = [Sem(es, nc, f"{name}_{i}") for i in range(k)]
        self.k = k

    def base(self):
        return [s.n for s in self.s]

    def inc(self, instr, idx, amt=16):
        return self.s[idx % self.k].inc(instr, amt)

    def w(self, eng, idx, base, per=16):
        sl = idx % self.k
        eng.wait_ge(self.s[sl].h, base[sl] + per * (idx // self.k + 1))


def build_program(dbg=False, stage='full'):
    nc = bass.Bass("TRN2", target_bir_lowering=False)

    x_full = nc.dram_tensor("x_full", [S, E], F32, kind="ExternalInput").ap()
    pw_h = nc.dram_tensor("pw", [E, E], F32, kind="ExternalInput").ap()
    awp_h = nc.dram_tensor("awp", [8 * 128, 64], F32, kind="ExternalInput").ap()
    vwp_h = nc.dram_tensor("vwp", [8 * 128, 64], F32, kind="ExternalInput").ap()
    owT_h = nc.dram_tensor("owT", [E, E], F32, kind="ExternalInput").ap()
    ob_h = nc.dram_tensor("ob", [1, E], F32, kind="ExternalInput").ap()
    wgT_h = nc.dram_tensor("wgT", [E, INNER], F32, kind="ExternalInput").ap()
    wuT_h = nc.dram_tensor("wuT", [E, INNER], F32, kind="ExternalInput").ap()
    wdT_h = nc.dram_tensor("wdT", [INNER, E], F32, kind="ExternalInput").ap()
    id32_h = nc.dram_tensor("id32", [128, 128], F32, kind="ExternalInput").ap()
    ones_h = nc.dram_tensor("onesb", [1, 128], BF16, kind="ExternalInput").ap()
    idbf_h = nc.dram_tensor("idbf", [128, 128], BF16, kind="ExternalInput").ap()
    y_h = nc.dram_tensor("y", [SEQ, E], F32, kind="ExternalOutput").ap()

    wgq_h = nc.dram_tensor("wgq", [E, INNER], BF16, kind="Internal").ap()
    wuq_h = nc.dram_tensor("wuq", [E, INNER], BF16, kind="Internal").ap()
    wdq_h = nc.dram_tensor("wdq", [INNER, E], BF16, kind="Internal").ap()
    dbg_h = nc.dram_tensor("dbg", [128, 4096], F32,
                           kind="ExternalOutput").ap() if dbg else None
    dbg_aps = {}

    es = ExitStack()
    sb = lambda n, s, d: es.enter_context(nc.sbuf_tensor(n, s, d))
    sem = lambda n: Sem(es, nc, n)

    x_tm = sb("x_tm", [128, 16 * E], F32)
    id32 = sb("id32s", [128, 128], F32)
    idbf = sb("idbfs", [128, 128], BF16)
    onesb = sb("onesbs", [1, 128], BF16)
    eps_t = sb("eps_t", [128, 1], F32)
    g_t = sb("g_t", [128, 1], F32)
    h_t = sb("h_t", [128, 1], F32)
    c5217 = sb("c5217", [128, 1], F32)
    sc_t = sb("sc_t", [128, 8], F32)
    mkrow = sb("mkrow", [1, 32], F32)
    mh_row = sb("mh_row", [1, 512], F32)
    ones_row = sb("ones_row", [1, 128], F32)
    onesb_t = sb("onesb_t", [128, 128], BF16)
    mq_row = sb("mq_row", [1, 128], F32)
    wbar = sb("wbar", [1, 4], F32)
    obhi = sb("obhi", [1, E], BF16)
    oblo = sb("oblo", [1, E], BF16)

    s_dc = sem("s_dc"); s_dx = sem("s_dx"); s_g = sem("s_g"); s_ob = sem("s_ob")
    d_w = DSem(es, nc, "d_w", 2); v_wr = sem("v_wr"); t_wt = sem("t_wt"); v_ws = sem("v_ws")
    g_wb = sem("g_wb"); v_wq = sem("v_wq"); o_wq = DSem(es, nc, "o_wq", 2); s_gp = sem("s_gp")
    d_pw = DSem(es, nc, "d_pw", 2); v_pw = sem("v_pw")
    d_xa = DSem(es, nc, "d_xa", 2); a_sq = sem("a_sq"); v_st = sem("v_st"); a_rs = sem("a_rs")
    v_ln = sem("v_ln"); t_lt = sem("t_lt"); v_ls = sem("v_ls"); t_pj = sem("t_pj")
    v_he = sem("v_he"); v_mh = sem("v_mh"); t_mh = sem("t_mh"); g_mh = sem("g_mh")
    d_aw = sem("d_aw"); v_aw = sem("v_aw")
    t_v = sem("t_v"); v_ve = sem("v_ve")
    t_q = sem("t_q"); v_qe = sem("v_qe"); t_ct = sem("t_ct"); v_cd = sem("v_cd")
    g_cb = sem("g_cb")
    t_s = sem("t_s"); a_e = sem("a_e"); v_w = sem("v_w"); t_av = sem("t_av")
    t_cb = sem("t_cb"); v_as = sem("v_as"); d_pk = Sem(es, nc, "d_pk")
    v_nm = sem("v_nm"); t_at = sem("t_at"); v_af = sem("v_af"); t_op = sem("t_op")
    v_res = sem("v_res")
    a_f1 = sem("a_f1"); v_f1 = sem("v_f1"); a_f2 = sem("a_f2"); v_f2 = sem("v_f2")
    v_fq = sem("v_fq"); t_xt = sem("t_xt"); v_xe = sem("v_xe")
    t_fr = sem("t_fr"); v_fr = sem("v_fr"); t_bc = sem("t_bc"); v_bc = sem("v_bc")
    d_fw = DSem(es, nc, "d_fw", 3); t_fg = sem("t_fg"); v_gr = sem("v_gr"); a_si = sem("a_si")
    v_gu = sem("v_gu"); t_mx = sem("t_mx"); v_mx = sem("v_mx")
    t_sr = sem("t_sr"); v_sr = sem("v_sr"); t_sb = sem("t_sb"); v_sb2 = sem("v_sb2")
    t_fo = sem("t_fo"); v_fo = sem("v_fo")
    v_gq = sem("v_gq"); d_fd = DSem(es, nc, "d_fd", 2); t_dn = sem("t_dn"); v_de = sem("v_de")
    t_dt = sem("t_dt"); v_dt = sem("v_dt"); d_y = sem("d_y")

    blk = es.enter_context(nc.Block())

    @blk.sync
    def _(sy):
        s_dc.inc(sy.dma_start(out=id32[:], in_=id32_h), 16)
        s_dc.inc(sy.dma_start(out=onesb[:], in_=ones_h), 16)
        s_dc.inc(sy.dma_start(out=idbf[:], in_=idbf_h), 16)
        for t in range(16):
            s_dx.inc(sy.dma_start(out=x_tm[:, t * E:(t + 1) * E],
                                  in_=x_full[t * 128:(t + 1) * 128, :]), 16)

    @blk.vector
    def _(v):
        v.memset(eps_t[:], 1e-5)
        v.memset(ones_row[:], 1.0)
        v.memset(onesb_t[:], 1.0)
        v.memset(sc_t[:, 0:1], 0.75)
        v.drain()
        v.tensor_scalar(sc_t[:, 1:2], sc_t[:, 0:1], M23, -M23, ALU.add, ALU.add)
        v.drain()
        v.tensor_sub(sc_t[:, 2:3], sc_t[:, 1:2], sc_t[:, 0:1])
        v.drain()
        v.tensor_scalar(g_t[:], sc_t[:, 2:3], 4.0, -1.0, ALU.mult, ALU.add)
        v.drain()
        v.tensor_single_scalar(h_t[:], g_t[:], 1.0, ALU.add)
        v.memset(c5217[:], M52)
        v.drain()
        s_g.inc(v.tensor_single_scalar(c5217[:], c5217[:], -M17, ALU.add))

    with ExitStack() as wes:
        wsb = lambda n, s, d: wes.enter_context(nc.sbuf_tensor(n, s, d))
        wraw = wsb("wraw", [128, 2 * 4096], F32)
        wt = wsb("wt", [128, 4096], F32)
        wn = wsb("wn", [128, 4096], F32)
        wqt = wsb("wqt", [128, 2 * 4096], BF16)
        asum = wsb("asum", [128, 33], F32)
        swbc = wsb("swbc", [128, 1], F32)
        obf = wsb("obf", [1, E], F32)
        wrow = wsb("wrow", [1, 128], F32)
        ps_wq = wes.enter_context(nc.psum_tensor("ps_wq", [128, 128], F32))

        @blk.sync
        def _(sy):
            s_dc.inc(sy.dma_start(out=obf[:], in_=ob_h), 16)

        @blk.vector
        def _(v):
            s_dc.w(v)
            v.tensor_copy(obhi[:], obf[:])
            v.drain()
            s_ob.inc(v.tensor_tensor(oblo[:], obf[:], obhi[:], ALU.subtract))

        specs = [(wgT_h, wgq_h, 8, 4096, 0), (wuT_h, wuq_h, 8, 4096, 1),
                 (wdT_h, wdq_h, 32, 1024, 2)]
        if stage == 'setup0':
            specs = []
        elif stage == 'wq1':
            specs = specs[:1]

        for si, (src, dst, nt, wdt, wi) in enumerate(specs):
            bdw, bvr = d_w.base(), v_wr.n

            @blk.sync
            def _(sy, src=src, nt=nt, wdt=wdt, bvr=bvr):
                v_wq.w(sy)
                for t in range(nt):
                    if t >= 2:
                        v_wr.w(sy, bvr + t - 1)
                    d_w.inc(sy.dma_start(
                        out=wraw[:, (t % 2) * 4096:(t % 2) * 4096 + wdt],
                        in_=src[t * 128:(t + 1) * 128, :]), t)

            @blk.vector
            def _(v, nt=nt, wdt=wdt, bdw=bdw):
                for t in range(nt):
                    d_w.w(v, t, bdw)
                    v_wr.inc(v.reduce_sum(
                        asum[:, t:t + 1],
                        wraw[:, (t % 2) * 4096:(t % 2) * 4096 + wdt],
                        axis=AX.X, apply_absolute_value=True))
                v.drain()
                v.reduce_sum(asum[:, 32:33], asum[:, 0:nt], axis=AX.X)
                v_wr.inc(v.drain())

            @blk.tensor
            def _(te):
                v_wr.w(te)
                s_dc.w(te)
                t_wt.inc(te.transpose(ps_wq[0:1, 0:128], asum[:, 32:33], id32[:]))

            @blk.vector
            def _(v, nt=nt, wdt=wdt, wi=wi):
                t_wt.w(v)
                g_wb.w(v)
                v.reduce_sum(sc_t[0:1, 4:5], ps_wq[0:1, 0:128], axis=AX.X)
                v.drain()
                v.tensor_scalar(wbar[0:1, wi:wi + 1], sc_t[0:1, 4:5],
                                1.0 / (nt * 128 * wdt), 1e-5, ALU.mult, ALU.max)
                v.drain()
                v_ws.inc(v.reciprocal(sc_t[0:1, 5:6], wbar[0:1, wi:wi + 1]))

            @blk.vector
            def _(v):
                v_ws.w(v)
                v_ws.inc(v.tensor_single_scalar(wrow[0:1, 0:128], ones_row[0:1, :],
                                                sc_t[0:1, 5:6], ALU.mult))

            @blk.tensor
            def _(te):
                v_ws.w(te)
                t_wt.inc(te.transpose(ps_wq[:, 0:1], wrow[0:1, 0:128],
                                      id32[0:1, 0:1]))

            @blk.vector
            def _(v):
                t_wt.w(v)
                g_wb.inc(v.tensor_copy(swbc[:], ps_wq[:, 0:1]))

            bdw2, bvq, boq = d_w.base(), v_wq.n, o_wq.base()

            @blk.sync
            def _(sy, src=src, nt=nt, wdt=wdt, bvq=bvq):
                v_wr.w(sy)
                for t in range(nt):
                    if t >= 2:
                        v_wq.w(sy, bvq + t - 1)
                    d_w.inc(sy.dma_start(
                        out=wraw[:, (t % 2) * 4096:(t % 2) * 4096 + wdt],
                        in_=src[t * 128:(t + 1) * 128, :]), t)

            @blk.vector
            def _(v, nt=nt, wdt=wdt, bdw2=bdw2, boq=boq):
                g_wb.w(v)
                for i, s_ in enumerate(o_wq.s):
                    if boq[i] > 0:
                        v.wait_ge(s_.h, boq[i])
                for t in range(nt):
                    d_w.w(v, t, bdw2)
                    if t >= 2:
                        o_wq.w(v, t - 2, boq)
                    buf = wraw[:, (t % 2) * 4096:(t % 2) * 4096 + wdt]
                    qb = wqt[:, (t % 2) * 4096:(t % 2) * 4096 + wdt]
                    v.tensor_single_scalar(wt[:, 0:wdt], buf, swbc[:, 0:1], ALU.mult)
                    v.drain()
                    v.tensor_single_scalar(wn[:, 0:wdt], wt[:, 0:wdt], -0.5, ALU.is_le)
                    v.drain()
                    v_wq.inc(v.scalar_tensor_tensor(
                        qb, wt[:, 0:wdt], 0.5, wn[:, 0:wdt], ALU.is_ge, ALU.subtract))
                    v.drain()

            @blk.gpsimd
            def _(g, dst=dst, nt=nt, wdt=wdt, bvq=bvq):
                for t in range(nt):
                    v_wq.w(g, bvq + t + 1)
                    o_wq.inc(g.dma_start(
                        out=dst[t * 128:(t + 1) * 128, :],
                        in_=wqt[:, (t % 2) * 4096:(t % 2) * 4096 + wdt]), t)

    def _early_out():
        @blk.gpsimd
        def _(gp):
            for s_ in o_wq.s:
                if s_.n > 0:
                    gp.wait_ge(s_.h, s_.n)
            for t in range(2):
                d_y.inc(gp.dma_start(out=y_h[t * 128:(t + 1) * 128, :],
                                     in_=x_tm[:, t * E:(t + 1) * E]), 16)
            gp.wait_ge(d_y.h, d_y.n)
        es.close()
        return nc

    if stage in ('wqonly', 'setup0', 'wq1'):
        return _early_out()

    with ExitStack() as aes:
        asb = lambda n, s, d: aes.enter_context(nc.sbuf_tensor(n, s, d))
        h_hi = asb("h_hi", [128, 4 * 4096], BF16)
        h_lo = asb("h_lo", [128, 4 * 4096], BF16)
        vA = asb("vA", [128, NRC * 65], F32)
        vB = asb("vB", [128, NRC * 65], F32)
        stat = asb("stat", [128, 16], F32)
        mh_t = asb("mh_t", [128, 4], F32)
        mh_r = asb("mh_r", [4, 2], F32)

        pb_q = aes.enter_context(nc.psum_tensor("pb_q", [128, 512], F32))
        pb_o = aes.enter_context(nc.psum_tensor("pb_o", [128, 512], F32))
        pb_s = [aes.enter_context(nc.psum_tensor(f"pb_s{i}", [128, 512], F32))
                for i in range(2)]
        pb_av = [aes.enter_context(nc.psum_tensor(f"pb_av{i}", [128, 512], F32))
                 for i in range(4)]

        def emit_hbuild(g, hes):
            hsb = lambda n, s, d: hes.enter_context(nc.sbuf_tensor(n + str(g), s, d))
            pw_hi = hsb("pw_hi", [128, 8 * 512], BF16)
            pw_lo = hsb("pw_lo", [128, 8 * 512], BF16)
            pw_f = hsb("pw_f", [128, 2 * 512], F32)
            xa = hsb("xa", [128, 2 * E], F32)
            xsq = hsb("xsq", [128, E], F32)
            lnt = hsb("lnt", [128, E], F32)
            ln_hi = hsb("ln_hi", [128, 4 * E], BF16)
            ln_lo = hsb("ln_lo", [128, 4 * E], BF16)
            bdpw, bvpw = d_pw.base(), v_pw.n
            bdxa, basq, bvst, bars, bvln = d_xa.base(), a_sq.n, v_st.n, a_rs.n, v_ln.n
            btlt, bvls, btpj, bvhe = t_lt.n, v_ls.n, t_pj.n, v_he.n
            prev_res = v_res.n
            prev_nm = v_nm.n
            prev_af_h = v_af.n

            @blk.sync
            def _(sy):
                v_res.w(sy, prev_res)
                for s_ in o_wq.s:
                    if s_.n > 0:
                        sy.wait_ge(s_.h, s_.n)
                v_wq.w(sy)
                for ic in range(8):
                    if ic >= 2:
                        v_pw.w(sy, bvpw + ic - 1)
                    d_pw.inc(sy.dma_start(
                        out=pw_f[:, (ic % 2) * 512:(ic % 2 + 1) * 512],
                        in_=pw_h[ic * 128:(ic + 1) * 128, 512 * g:512 * (g + 1)]), ic)
                for rc in range(NRC):
                    if rc >= 2:
                        v_ln.w(sy, bvln + rc - 1)
                    d_xa.inc(sy.dma_start(
                        out=xa[:, (rc % 2) * E:(rc % 2 + 1) * E],
                        in_=x_full[rc * 128:(rc + 1) * 128, :]), rc)

            @blk.vector
            def _(v):
                t_pj.w(v, btpj)
                for ic in range(8):
                    d_pw.w(v, ic, bdpw)
                    src = pw_f[:, (ic % 2) * 512:(ic % 2 + 1) * 512]
                    v.tensor_copy(pw_hi[:, ic * 512:(ic + 1) * 512], src)
                    v.drain()
                    v_pw.inc(v.tensor_tensor(
                        pw_lo[:, ic * 512:(ic + 1) * 512], src,
                        pw_hi[:, ic * 512:(ic + 1) * 512], ALU.subtract))

            def emit_sq(sc, rc):
                c = (rc % 2) * 8
                d_xa.w(sc, rc, bdxa)
                if rc >= 2:
                    v_st.w(sc, bvst + rc - 1)
                a_sq.inc(sc.activation(
                    xsq[:, 0:E],
                    xa[:, (rc % 2) * E:(rc % 2 + 1) * E], AFT.Square,
                    accum_out=stat[:, c + 0:c + 1]))
                sc.drain()

            def emit_sqrt(sc, rc):
                c = (rc % 2) * 8
                v_st.w(sc, bvst + rc + 1)
                if rc >= 2:
                    v_ln.w(sc, bvln + rc - 1)
                a_rs.inc(sc.activation(
                    stat[:, c + 5:c + 6], stat[:, c + 4:c + 5], AFT.Sqrt,
                    bias=eps_t[:, 0:1], scale=1.0))

            @blk.scalar
            def _(sc):
                s_g.w(sc, 1)
                v_res.w(sc, prev_res)
                for s_ in o_wq.s:
                    if s_.n > 0:
                        sc.wait_ge(s_.h, s_.n)
                v_wq.w(sc)
                for rc in range(NRC):
                    emit_sq(sc, rc)
                    if rc >= 1:
                        emit_sqrt(sc, rc - 1)
                emit_sqrt(sc, NRC - 1)

            def emit_stats2(v, rc):
                c = (rc % 2) * 8
                a_sq.w(v, basq + rc + 1)
                if rc >= 2:
                    a_rs.w(v, bars + rc - 1)
                if rc >= 2:
                    v_ln.w(v, bvln + rc - 1)
                v.reduce_sum(stat[:, c + 1:c + 2],
                             xa[:, (rc % 2) * E:(rc % 2 + 1) * E], axis=AX.X)
                v.drain()
                v.tensor_single_scalar(stat[:, c + 2:c + 3], stat[:, c + 1:c + 2],
                                       1.0 / E, ALU.mult)
                v.drain()
                v.tensor_tensor(stat[:, c + 3:c + 4], stat[:, c + 2:c + 3],
                                stat[:, c + 2:c + 3], ALU.mult)
                v.drain()
                v_st.inc(v.scalar_tensor_tensor(
                    stat[:, c + 4:c + 5], stat[:, c + 0:c + 1], 1.0 / E,
                    stat[:, c + 3:c + 4], ALU.mult, ALU.subtract))
                v.drain()

            def emit_ln(v, rc):
                c = (rc % 2) * 8
                a_rs.w(v, bars + rc + 1)
                if rc >= 1:
                    t_lt.w(v, btlt + 8 * rc)
                v.reciprocal(stat[:, c + 6:c + 7], stat[:, c + 5:c + 6])
                v.drain()
                v_ln.inc(v.tensor_scalar(
                    lnt[:], xa[:, (rc % 2) * E:(rc % 2 + 1) * E],
                    stat[:, c + 2:c + 3], stat[:, c + 6:c + 7],
                    ALU.subtract, ALU.mult))
                v.drain()



            ltp = [pb_q, pb_o, pb_s[0], pb_s[1]]

            def emit_lt(te, rc):
                v_ln.w(te, bvln + rc + 1)
                for ib in range(8):
                    tid = rc * 8 + ib
                    if tid >= 4:
                        v_ls.w(te, bvls + tid - 3)
                    t_lt.inc(te.transpose(
                        ltp[ib % 4][:, 0:128],
                        lnt[:, ib * 128:(ib + 1) * 128], id32[:]))

            def emit_pj(te, rcq):
                v_ls.w(te, bvls + 32 * rcq + 32)
                for lc in range(4):
                    pid = rcq * 4 + lc
                    pp = pb_av[lc]
                    if pid >= 4:
                        v_he.w(te, bvhe + pid - 3)
                    first = True
                    last = None
                    for (wop, rop) in ((pw_hi, ln_hi), (pw_hi, ln_lo), (pw_lo, ln_hi)):
                        for ic in range(8):
                            last = te.matmul(
                                pp[:, 0:512],
                                wop[:, ic * 512 + lc * 128:ic * 512 + (lc + 1) * 128],
                                rop[:, ic * 512:(ic + 1) * 512],
                                start=first, stop=(wop is pw_lo and ic == 7))
                            first = False
                    t_pj.inc(last)

            def emit_ls(v, rc):
                for ib in range(8):
                    tid = rc * 8 + ib
                    t_lt.w(v, btlt + tid + 1)
                    if ib == 0 and rc >= 4:
                        t_pj.w(v, btpj + 4 * (rc // 4))
                    src = ltp[ib % 4][:, 0:128]
                    dst = ib * 512 + (rc % 4) * 128
                    v.tensor_copy(ln_hi[:, dst:dst + 128], src)
                    v.drain()
                    v_ls.inc(v.tensor_tensor(
                        ln_lo[:, dst:dst + 128], src,
                        ln_hi[:, dst:dst + 128], ALU.subtract))
                    v.drain()

            def emit_he(v, rcq):
                for lc in range(4):
                    pid = rcq * 4 + lc
                    t_pj.w(v, btpj + pid + 1)
                    pp = pb_av[lc]
                    dst = lc * 4096 + rcq * 512
                    v.tensor_copy(h_hi[:, dst:dst + 512], pp[:, 0:512])
                    v.drain()
                    v_he.inc(v.tensor_tensor(
                        h_lo[:, dst:dst + 512], pp[:, 0:512],
                        h_hi[:, dst:dst + 512], ALU.subtract))
                    v.drain()

            @blk.tensor
            def _(te):
                s_dc.w(te)
                v_res.w(te, prev_res)
                v_nm.w(te, prev_nm)
                v_mh.w(te)
                g_cb.w(te)
                v_pw.w(te, bvpw + 8)
                v_ve.w(te)
                t_av.w(te)
                v_af.w(te, prev_af_h)
                for rc in range(NRC):
                    if rc >= 4 and rc % 4 == 0:
                        emit_pj(te, rc // 4 - 1)
                    emit_lt(te, rc)
                emit_pj(te, NRC // 4 - 1)

            @blk.vector
            def _(v):
                for rc in range(NRC + 3):
                    if rc < NRC:
                        emit_stats2(v, rc)
                    if 1 <= rc < NRC + 1:
                        emit_ln(v, rc - 1)
                    if 1 <= rc < NRC + 1:
                        emit_ls(v, rc - 1)
                    if rc >= 6 and (rc - 6) % 4 == 0:
                        emit_he(v, (rc - 6) // 4)

            @blk.vector
            def _(v):
                v_he.w(v, bvhe + (NRC // 4) * 4)
                for lc in range(4):
                    v.reduce_max(mh_t[:, lc:lc + 1], h_hi[:, lc * 4096:(lc + 1) * 4096],
                                 axis=AX.X, apply_absolute_value=True)
                v.drain()
                v_mh.inc(v.tensor_single_scalar(mh_t[:], mh_t[:], 1.01, ALU.mult))

            @blk.gpsimd
            def _(gp, g=g):
                v_mh.w(gp)
                if g >= 1:
                    v_mh.w(gp)
                g_mh.inc(gp.dma_start(out=mh_row[0:1, 0:512], in_=mh_t[:, 0:4]), 16)

            @blk.vector
            def _(v, g=g):
                g_mh.w(v)
                mv = mh_row.ap().rearrange("o (p l) -> o l p", l=4)
                for lc in range(4):
                    for j in range(2):
                        p = 4 * g + lc
                        v.reduce_max(
                            mkrow[0:1, 2 * p + j:2 * p + j + 1],
                            mv[0:1, lc:lc + 1, 64 * j:64 * (j + 1)], axis=AX.X)
                v_mh.inc(v.drain())

        def emit_attention(g, lp, T):
            (aw_f, vw_f, ow_f, aw_hi, aw_lo, vw_hi, vw_lo, ow_hi, ow_lo,
             q_hi, q_lo, w32, mq_t, cbias, csc, crow,
             attn_tm, attn_hi, attn_lo, rr_t, rhib, fbA_t, fbB_t, attnB_s) = T
            p = 4 * g + lp
            hofs = lp * 4096
            bdaw, bvaw = d_aw.n, v_aw.n
            btv, bvve = t_v.n, v_ve.n
            hdone = v_he.n
            prev_nm = v_nm.n
            prev_res = v_res.n
            prev_res2 = v_res.n
            prev_op = t_op.n
            prev_at = t_at.n
            prev_af = v_af.n
            prev_s = t_s.n
            prev_e = a_e.n

            @blk.sync
            def _(sy):
                t_pj.w(sy)
                v_aw.w(sy, bvaw)
                d_aw.inc(sy.dma_start(out=aw_f[:], in_=awp_h[p * 128:(p + 1) * 128, :]), 16)
                d_aw.inc(sy.dma_start(out=vw_f[:], in_=vwp_h[p * 128:(p + 1) * 128, :]), 16)
                d_aw.inc(sy.dma_start(out=ow_f[:], in_=owT_h[p * 128:(p + 1) * 128, :]), 16)

            @blk.vector
            def _(v):
                d_aw.w(v, bdaw + 48)
                t_q.w(v)
                t_v.w(v, btv)
                t_op.w(v, prev_op)
                t_av.w(v)
                v.tensor_copy(aw_hi[:], aw_f[:])
                v.tensor_copy(vw_hi[:], vw_f[:])
                v.tensor_copy(ow_hi[:], ow_f[:])
                v.drain()
                v.tensor_tensor(aw_lo[:], aw_f[:], aw_hi[:], ALU.subtract)
                v.tensor_tensor(vw_lo[:], vw_f[:], vw_hi[:], ALU.subtract)
                v.tensor_tensor(ow_lo[:], ow_f[:], ow_hi[:], ALU.subtract)
                v_aw.inc(v.drain())

            @blk.tensor
            def _(te):
                v_aw.w(te, bvaw + 1)
                v_he.w(te, hdone)
                v_af.w(te, prev_af)
                v_res.w(te, prev_res2)
                t_pj.w(te)
                for rc in range(NRC):
                    if rc >= 2:
                        v_ve.w(te, bvve + rc - 1)
                    last = None
                    for (tp, hp, pav) in (((0, 0), 0, pb_av[rc % 2]),
                                          ((64, 0), 64, pb_av[2 + rc % 2])):
                        first = True
                        for (hop, vop) in ((h_hi, vw_hi), (h_hi, vw_lo), (h_lo, vw_hi)):
                            last = te.matmul(
                                pav[:, 0:64],
                                hop[hp:hp + 64, hofs + rc * 128:hofs + (rc + 1) * 128],
                                vop[hp:hp + 64, :], start=first,
                                stop=(hop is h_lo), tile_position=tp)
                            first = False
                    t_v.inc(last)

            @blk.vector
            def _(v):
                for rc in range(NRC):
                    t_v.w(v, btv + rc + 1)
                    o = rc * 65
                    for (hp, vb) in ((0, vA), (64, vB)):
                        pav = pb_av[(0 if hp == 0 else 2) + rc % 2]
                        v.tensor_copy(vb[:, o:o + 64], pav[:, 0:64])
                        v.memset(vb[:, o + 64:o + 65], 1.0)
                    v_ve.inc(v.drain())

            def emit_resid(v, P):
                s_dx.w(v)
                for st in range(4):
                    for eh in range(2):
                        oid = 2 * st + eh
                        t_op.w(v, P['btop0'] + oid + 1)
                        pp = [pb_av[0], pb_av[1]][eh]
                        base = (P['scof'] // 128 + st) * E + eh * 512
                        xd = x_tm[:, base:base + 512]
                        v_res.inc(v.tensor_tensor(xd, xd, pp[:, 0:512], ALU.add))
                        v.drain()

            def emit_outproj(te, P):
                s_ob.w(te, 1)
                v_as.w(te, P['vas_mark'])
                v_af.w(te, P['vaf_mark'])
                for st in range(4):
                    for eh in range(2):
                        oid = 2 * st + eh
                        if oid >= 2:
                            v_res.w(te, P['bvres_r'] + oid - 1)
                        pp = [pb_av[0], pb_av[1]][eh]
                        ops = [(attn_hi, ow_hi, False), (attn_hi, ow_lo, False),
                               (attn_lo, ow_hi, p != 7)]
                        first = True
                        last = None
                        for (a, o, stop) in ops:
                            last = te.matmul(pp[:, 0:512],
                                             a[:, st * 128:(st + 1) * 128],
                                             o[:, eh * 512:(eh + 1) * 512],
                                             start=first, stop=stop)
                            first = False
                        if p == 7:
                            te.matmul(pp[:, 0:512], onesb[0:1, :],
                                      obhi[0:1, eh * 512:(eh + 1) * 512],
                                      start=False, stop=False)
                            last = te.matmul(pp[:, 0:512], onesb[0:1, :],
                                             oblo[0:1, eh * 512:(eh + 1) * 512],
                                             start=False, stop=True)
                        t_op.inc(last)

            def emit_qhead_te(scof_n, gcb_mark):
                hm = dict(btq=t_q.n, bvqe=v_qe.n, btct=t_ct.n, bvcd=v_cd.n,
                          scof=scof_n)

                @blk.tensor
                def _(te):
                    g_cb.w(te, gcb_mark)
                    v_ls.w(te)
                    v_aw.w(te, bvaw + 1)
                    v_he.w(te, hdone)
                    last = None
                    for (tp, hp) in (((0, 0), 0), ((64, 64), 64)):
                        first = True
                        for (a, hh) in ((aw_hi, h_hi), (aw_hi, h_lo), (aw_lo, h_hi)):
                            last = te.matmul(
                                pb_q[hp:hp + 64, 0:512],
                                a[hp:hp + 64, :],
                                hh[hp:hp + 64, hofs + scof_n:hofs + scof_n + 512],
                                start=first, stop=(a is aw_lo), tile_position=tp)
                            first = False
                    t_q.inc(last)
                return hm

            def emit_qhead_dve(hm, bts_gate):
                qo = ((hm['scof'] // 512) % 2) * 512

                @blk.vector
                def _(v):
                    t_q.w(v, hm['btq'] + 1)
                    t_s.w(v, bts_gate)
                    t_ct.w(v)
                    v.tensor_copy(q_hi[:, qo:qo + 512], pb_q[:, 0:512])
                    v.drain()
                    v.tensor_tensor(q_lo[:, qo:qo + 512], pb_q[:, 0:512],
                                    q_hi[:, qo:qo + 512], ALU.subtract)
                    v.reduce_max(mq_t[:], pb_q[:, 0:512], axis=AX.X,
                                 apply_absolute_value=True)
                    v_qe.inc(v.drain())

                @blk.gpsimd
                def _(gp):
                    v_qe.w(gp, hm['bvqe'] + 1)
                    v_cd.w(gp, hm['bvcd'])
                    t_ct.inc(gp.dma_start(out=mq_row[0:1, 0:128],
                                          in_=mq_t[:, 0:1]), 16)

                @blk.vector
                def _(v):
                    t_ct.w(v, hm['btct'] + 16)
                    g_mh.w(v)
                    t_cb.w(v)
                    v.reduce_max(csc[0:1, 0:1], mq_row[0:1, 0:64], axis=AX.X)
                    v.reduce_max(csc[0:1, 1:2], mq_row[0:1, 64:128], axis=AX.X)
                    v.drain()
                    v.tensor_tensor(csc[0:1, 2:3], csc[0:1, 0:1],
                                    mkrow[0:1, 2 * p:2 * p + 1], ALU.mult)
                    v.tensor_tensor(csc[0:1, 3:4], csc[0:1, 1:2],
                                    mkrow[0:1, 2 * p + 1:2 * p + 2], ALU.mult)
                    v.drain()
                    v.tensor_single_scalar(csc[0:1, 4:5], csc[0:1, 2:3], -1.0, ALU.mult)
                    v.tensor_single_scalar(csc[0:1, 5:6], csc[0:1, 3:4], -1.0, ALU.mult)
                    v.drain()
                    v.tensor_single_scalar(crow[0:1, 0:128], ones_row[0:1, :],
                                           csc[0:1, 4:5], ALU.mult)
                    v.tensor_single_scalar(crow[0:1, 128:256], ones_row[0:1, :],
                                           csc[0:1, 5:6], ALU.mult)
                    v_cd.inc(v.drain())

            pend = None
            curhead = emit_qhead_te(0, g_cb.n)
            emit_qhead_dve(curhead, t_s.n)
            for sc_i in range(4):
                scof = sc_i * 512
                scpar = sc_i % 2
                qo = scpar * 512
                hm = curhead
                bts, bae, btav, bvw = t_s.n, a_e.n, t_av.n, v_w.n
                bvnm, btat, bvaf = v_nm.n, t_at.n, v_af.n
                bgcb = g_cb.n

                @blk.tensor
                def _(te):
                    def emit_av(rc):
                        for hidx in range(2):
                            a_e.w(te, bae + 2 * rc + hidx + 1)
                            if rc == 0:
                                v_af.w(te, bvaf)
                            wofs = (2 * (rc % 2) + hidx) * 512
                            vb = [vA, vB][hidx]
                            o = rc * 65
                            last = te.matmul(
                                pb_av[2 + hidx][0:65, 0:512],
                                vb[:, o:o + 65],
                                w32[:, wofs:wofs + 512],
                                start=(rc == 0), stop=(rc == NRC - 1))
                        t_av.inc(last)

                    v_ve.w(te, bvve + NRC)
                    v_qe.w(te, hm['bvqe'] + 1)
                    a_e.w(te, bae)
                    v_ls.w(te)
                    v_mh.w(te)
                    for rc in range(NRC):
                        for hidx in range(2):
                            if rc >= 1:
                                a_e.w(te, bae + 2 * (rc - 1) + hidx + 1)
                            first = True
                            last = None
                            for (kop, qop) in ((h_hi, q_hi), (h_hi, q_lo), (h_lo, q_hi)):
                                last = te.matmul(
                                    pb_s[hidx][:, 0:512],
                                    kop[hidx * 64:hidx * 64 + 64,
                                        hofs + rc * 128:hofs + (rc + 1) * 128],
                                    qop[hidx * 64:hidx * 64 + 64, qo:qo + 512],
                                    start=first, stop=(kop is h_lo),
                                    tile_position=(hidx * 64, 0))
                                first = False
                            t_s.inc(last)
                        if rc == 0:
                            v_cd.w(te, hm['bvcd'] + 1)
                            g_cb.w(te, bgcb)
                            t_cb.inc(te.transpose(pb_o[:, 0:1], crow[0:1, 0:128],
                                                  id32[0:1, 0:1]))
                            t_cb.inc(te.transpose(pb_q[:, 0:1], crow[0:1, 128:256],
                                                  id32[0:1, 0:1]))
                        if rc == 1 and pend is not None:
                            emit_outproj(te, pend)
                        if rc >= 1:
                            emit_av(rc - 1)
                    emit_av(NRC - 1)

                @blk.vector
                def _(v):
                    t_cb.w(v)
                    a_e.w(v, bae)
                    v.tensor_copy(cbias[:, 2 * scpar:2 * scpar + 1], pb_o[:, 0:1])
                    g_cb.inc(v.tensor_copy(cbias[:, 2 * scpar + 1:2 * scpar + 2],
                                           pb_q[:, 0:1]))

                @blk.scalar
                def _(sc):
                    g_cb.w(sc, bgcb + 1)
                    for rc in range(NRC):
                        for hidx in range(2):
                            t_s.w(sc, bts + 2 * rc + hidx + 1)
                            if rc >= 2:
                                t_av.w(sc, btav + rc - 1)
                            elif btav > 0:
                                t_av.w(sc, btav + rc - 1)
                            slot = (2 * (rc % 2) + hidx) * 512
                            a_e.inc(sc.activation(
                                w32[:, slot:slot + 512], pb_s[hidx][:, 0:512], AFT.Exp,
                                bias=cbias[:, 2 * scpar + hidx:2 * scpar + hidx + 1],
                                scale=1.0 / 64))

                if sc_i < 3:
                    nexthead = emit_qhead_te(scof + 512, g_cb.n)

                if pend is not None:
                    @blk.vector
                    def _(v, P=pend):
                        emit_resid(v, P)

                @blk.vector
                def _(v):
                    t_av.w(v, btav + NRC)
                    t_at.w(v, btat)
                    for hidx in range(2):
                        v.reciprocal(rr_t[64:65, hidx * 512:(hidx + 1) * 512],
                                     pb_av[2 + hidx][64:65, 0:512])
                    v.drain()
                    for hidx in range(2):
                        v.tensor_copy(rhib[64:65, hidx * 1024:hidx * 1024 + 512],
                                      rr_t[64:65, hidx * 512:(hidx + 1) * 512])
                    v.drain()
                    for hidx in range(2):
                        v.tensor_tensor(rhib[64:65, hidx * 1024 + 512:hidx * 1024 + 1024],
                                        rr_t[64:65, hidx * 512:(hidx + 1) * 512],
                                        rhib[64:65, hidx * 1024:hidx * 1024 + 512],
                                        ALU.subtract)
                    v_nm.inc(v.drain())

                @blk.tensor
                def _(te):
                    v_nm.w(te, bvnm + 1)
                    v_res.w(te)
                    for hidx in range(2):
                        te.matmul(pb_av[hidx][:, 0:512],
                                  onesb_t[64:65, 0:128],
                                  rhib[64:65, hidx * 1024:hidx * 1024 + 512],
                                  start=True, stop=False, tile_position=(64, 0))
                        t_at.inc(te.matmul(
                            pb_av[hidx][:, 0:512],
                            onesb_t[64:65, 0:128],
                            rhib[64:65, hidx * 1024 + 512:hidx * 1024 + 1024],
                            start=False, stop=True, tile_position=(64, 0)))

                @blk.vector
                def _(v):
                    t_at.w(v, btat + 1)
                    v.tensor_copy(fbA_t[:], pb_av[0][:, 0:512])
                    v_af.inc(v.drain())
                    t_at.w(v, btat + 2)
                    v.tensor_copy(fbB_t[:], pb_av[1][:, 0:512])
                    v_af.inc(v.drain())
                    v.tensor_tensor(attn_tm[0:64, 0:512], pb_av[2][0:64, 0:512],
                                    fbA_t[0:64, 0:512], ALU.mult)
                    v_af.inc(v.drain())
                    v.tensor_tensor(attnB_s[0:64, 0:512], pb_av[3][0:64, 0:512],
                                    fbB_t[0:64, 0:512], ALU.mult)
                    v_af.inc(v.drain())

                if sc_i < 3:
                    emit_qhead_dve(nexthead, bts)
                    curhead = nexthead

                @blk.gpsimd
                def _(gp):
                    v_af.w(gp, bvaf + 4)
                    d_pk.inc(gp.dma_start(out=attn_tm[64:128, 0:512],
                                          in_=attnB_s[0:64, 0:512]), 16)

                @blk.vector
                def _(v):
                    d_pk.w(v)
                    t_op.w(v)
                    v.tensor_copy(attn_hi[:], attn_tm[:])
                    v.drain()
                    v.tensor_tensor(attn_lo[:], attn_tm[:], attn_hi[:], ALU.subtract)
                    v_as.inc(v.drain())

                pend = dict(scof=scof, btop0=t_op.n, vas_mark=v_as.n,
                            vaf_mark=v_af.n - 2, bvres_r=v_res.n)

            if pend is not None:
                @blk.vector
                def _(v, P=pend):
                    emit_resid(v, P)

                @blk.tensor
                def _(te, P=pend):
                    emit_outproj(te, P)
                pend = None

        ngroups = 1 if stage in ('h0', 'a0') else 2
        for g in range(ngroups):
            with ExitStack() as hes:
                emit_hbuild(g, hes)
            if stage == 'h0':
                break
            with ExitStack() as bes:
                bsb = lambda n, s, d: bes.enter_context(
                    nc.sbuf_tensor(n + str(g), s, d))
                T = (bsb("aw_f", [128, 64], F32), bsb("vw_f", [128, 64], F32),
                     bsb("ow_f", [128, E], F32),
                     bsb("aw_hi", [128, 64], BF16), bsb("aw_lo", [128, 64], BF16),
                     bsb("vw_hi", [128, 64], BF16), bsb("vw_lo", [128, 64], BF16),
                     bsb("ow_hi", [128, E], BF16), bsb("ow_lo", [128, E], BF16),
                     bsb("q_hi", [128, 1024], BF16), bsb("q_lo", [128, 1024], BF16),
                     bsb("w32", [128, 4 * 512], F32),
                     bsb("mq_t", [128, 1], F32), bsb("cbias", [128, 4], F32),
                     bsb("csc", [1, 8], F32), bsb("crow", [1, 256], F32),
                     bsb("attn_tm", [128, 512], F32),
                     bsb("attn_hi", [128, 512], BF16),
                     bsb("attn_lo", [128, 512], BF16),
                     bsb("rr_t", [128, 1024], F32),
                     bsb("rhib", [128, 2048], BF16),
                     bsb("fbA_t", [128, 512], F32), bsb("fbB_t", [128, 512], F32),
                     bsb("attnB_s", [128, 512], F32))
                for lp in range(4 if stage != 'a0' else 1):
                    emit_attention(g, lp, T)

    if stage in ('h0', 'a0'):
        with ExitStack() as _tes:
            _dummy = _tes.enter_context(nc.sbuf_tensor("dmy", [128, 8], F32))

            @blk.gpsimd
            def _(gp):
                if stage == 'h0':
                    v_he.w(gp)
                else:
                    v_res.w(gp)
                for t in range(2):
                    d_y.inc(gp.dma_start(out=y_h[t * 128:(t + 1) * 128, :],
                                         in_=x_tm[:, t * E:(t + 1) * E]), 16)
                gp.wait_ge(d_y.h, d_y.n)
        es.close()
        return nc

    if stage == 'attn':
        all_res_a = v_res.n

        @blk.gpsimd
        def _(gp):
            v_res.w(gp, all_res_a)
            for t in range(16):
                d_y.inc(gp.dma_start(out=y_h[t * 128:(t + 1) * 128, :],
                                     in_=x_tm[:, t * E:(t + 1) * E]), 16)
            gp.wait_ge(d_y.h, d_y.n)
        es.close()
        return nc

    with ExitStack() as fes:
        fsb = lambda n, s, d: fes.enter_context(nc.sbuf_tensor(n, s, d))
        f_ln = fsb("f_ln", [128, 2 * E], F32)
        f_xq = fsb("f_xq", [128, 2 * E], F32)
        f_t = fsb("f_t", [128, E], F32)
        f_u = fsb("f_u", [128, E], F32)
        f_v = fsb("f_v", [128, E], F32)
        f_xqf = fsb("f_xqf", [128, 2 * 8 * FSC], BF16)
        f_gu = fsb("f_gu", [128, NOC * FSC], F32)
        f_guq = fsb("f_guq", [128, NOC * FSC], BF16)
        f_wg = fsb("f_wg", [128, 3 * 8 * 128], BF16)
        f_wu = fsb("f_wu", [128, 3 * 8 * 128], BF16)
        f_wd = fsb("f_wd", [128, 2 * 32 * 128], BF16)
        f_do = fsb("f_do", [128, NEC * FSC], F32)
        f_gr = fsb("f_gr", [128, FSC], F32)
        f_si = fsb("f_si", [128, FSC], F32)
        f_fb = fsb("f_fb", [128, FSC], F32)
        f_sb = fsb("f_sb", [128, FSC], F32)
        f_mx = fsb("f_mx", [128, FSC], F32)
        f_mx2 = fsb("f_mx2", [128, FSC], F32)
        f_sc = fsb("f_sc", [128, 16], F32)
        f_st = fsb("f_st", [128, 20], F32)
        f_row = fsb("f_row", [1, 2 * FSC], F32)
        f_rhi = fsb("f_rhi", [1, FSC], BF16)
        f_rlo = fsb("f_rlo", [1, FSC], BF16)
        f_m2r = fsb("f_m2r", [1, 512], F32)
        f_rhi2 = fsb("f_rhi2", [1, FSC], BF16)
        f_rlo2 = fsb("f_rlo2", [1, FSC], BF16)
        f_mgr = fsb("f_mgr", [1, 256], F32)
        f_for = fsb("f_for", [1, 256], F32)
        f_trw = fsb("f_trw", [1, 256], F32)

        pf_g = fes.enter_context(nc.psum_tensor("pf_g", [128, FSC], F32))
        pf_u = fes.enter_context(nc.psum_tensor("pf_u", [128, FSC], F32))
        pf_d0 = fes.enter_context(nc.psum_tensor("pf_d0", [128, FSC], F32))
        pf_d1 = fes.enter_context(nc.psum_tensor("pf_d1", [128, FSC], F32))
        pf_t0 = fes.enter_context(nc.psum_tensor("pf_t0", [128, 128], F32))
        pf_t1 = fes.enter_context(nc.psum_tensor("pf_t1", [128, 128], F32))
        pf_b = fes.enter_context(nc.psum_tensor("pf_b", [128, FSC], F32))
        pf_r = fes.enter_context(nc.psum_tensor("pf_r", [128, 128], F32))

        all_res = v_res.n
        wq_done = o_wq.base()

        def wq_wait(eng):
            for i, s in enumerate(o_wq.s):
                if wq_done[i] > 0:
                    eng.wait_ge(s.h, wq_done[i])

        if stage == 'ffndbg':
            for nm, sh, dt in [("dxq", [128, 2048], F32), ("dsc", [128, 16], F32),
                               ("dst", [128, 2], F32), ("dgu", [128, 8192], F32),
                               ("dguq", [128, 8192], BF16), ("dfb", [128, 256], F32),
                               ("dsb", [128, 256], F32), ("ddo", [128, 2048], F32),
                               ("dxqf", [128, 2048], BF16), ("dln", [128, 2048], F32)]:
                dbg_aps[nm] = nc.dram_tensor(nm, sh, dt, kind="ExternalOutput").ap()

        def snapA():
            return dict(basq=a_f1.n, bvst=v_f1.n, bars=a_f2.n, bvfq=v_fq.n,
                        btxt=t_xt.n, bvxe=v_xe.n, btfr=t_fr.n, bvfr=v_fr.n)

        def A_stats_dve(fcn, fan, fac):
            @blk.vector
            def _(v):
                for st2 in range(2):
                    gst = fcn * 2 + st2
                    cs = st2 * 8
                    a_f1.w(v, fan['basq'] + st2 + 1)
                    if fac is not None:
                        a_f2.w(v, fac['bars'] + 2)
                    v.reduce_sum(f_st[:, cs + 1:cs + 2],
                                 x_tm[:, gst * E:(gst + 1) * E], axis=AX.X)
                    v.drain()
                    v.tensor_single_scalar(f_st[:, cs + 2:cs + 3],
                                           f_st[:, cs + 1:cs + 2], 1.0 / E, ALU.mult)
                    v.drain()
                    v.tensor_tensor(f_st[:, cs + 3:cs + 4], f_st[:, cs + 2:cs + 3],
                                    f_st[:, cs + 2:cs + 3], ALU.mult)
                    v.drain()
                    v_f1.inc(v.scalar_tensor_tensor(
                        f_st[:, cs + 4:cs + 5], f_st[:, cs + 0:cs + 1], 1.0 / E,
                        f_st[:, cs + 3:cs + 4], ALU.mult, ALU.subtract))
                    v.drain()

        def A_quant_dve(fcn, fan, fac):
            @blk.vector
            def _(v):
                s_g.w(v, 1)
                a_f2.w(v, fan['bars'] + 2)
                a_f1.w(v, fan['basq'] + 2)
                if fac is not None:
                    t_fr.w(v, fac['btfr'] + 2)
                    t_xt.w(v, fac['btxt'] + 16)
                for st2 in range(2):
                    gst = fcn * 2 + st2
                    cs = st2 * 8
                    c0 = st2 * 8
                    v.reciprocal(f_st[:, cs + 6:cs + 7], f_st[:, cs + 5:cs + 6])
                    v.drain()
                    v.tensor_scalar(f_ln[:, st2 * E:(st2 + 1) * E],
                                    x_tm[:, gst * E:(gst + 1) * E],
                                    f_st[:, cs + 2:cs + 3], f_st[:, cs + 6:cs + 7],
                                    ALU.subtract, ALU.mult)
                    v_f2.inc(v.drain())
                    v.reduce_max(f_sc[:, c0:c0 + 1], f_ln[:, st2 * E:(st2 + 1) * E],
                                 axis=AX.X, apply_absolute_value=True)
                    v.drain()
                    v.tensor_scalar(f_sc[:, c0 + 1:c0 + 2], f_sc[:, c0:c0 + 1],
                                    1.0, 1e-5, ALU.mult, ALU.max)
                    v.drain()
                    v.reciprocal(f_sc[:, c0 + 2:c0 + 3], f_sc[:, c0 + 1:c0 + 2])
                    v.drain()
                    v.tensor_single_scalar(f_sc[:, c0 + 3:c0 + 4],
                                           f_sc[:, c0 + 2:c0 + 3], 127.0, ALU.mult)
                    v.drain()
                    v.tensor_single_scalar(f_t[:], f_ln[:, st2 * E:(st2 + 1) * E],
                                           f_sc[:, c0 + 3:c0 + 4], ALU.mult)
                    v.drain()
                    v.tensor_scalar(f_u[:], f_t[:], M23, -M23, ALU.add, ALU.add)
                    v.tensor_scalar(f_v[:], f_t[:], M52, -M52, ALU.add, ALU.add)
                    v.drain()
                    v.scalar_tensor_tensor(f_xq[:, st2 * E:(st2 + 1) * E],
                                           f_u[:], h_t[:, 0:1], f_v[:],
                                           ALU.mult, ALU.add)
                    v_fq.inc(v.drain())

        def A_xqT_te(fcn, fan):
            @blk.tensor
            def _(te):
                v_fq.w(te, fan['bvfq'] + 2)
                t_fg.w(te)
                v_dt.w(te)
                for st2 in range(2):
                    for ib in range(8):
                        tid = st2 * 8 + ib
                        if tid >= 2:
                            v_xe.w(te, fan['bvxe'] + tid - 1)
                        t_xt.inc(te.transpose(
                            [pf_t0, pf_t1][tid % 2][:, 0:128],
                            f_xq[:, st2 * E + ib * 128:st2 * E + (ib + 1) * 128],
                            id32[:]))

        def A_xqT_dve(fcn, fan):
            xp = (fcn % 2) * 8 * FSC
            @blk.vector
            def _(v):
                for st2 in range(2):
                    for ib in range(8):
                        tid = st2 * 8 + ib
                        t_xt.w(v, fan['btxt'] + tid + 1)
                        v_xe.inc(v.tensor_copy(
                            f_xqf[:, xp + ib * FSC + st2 * 128:
                                  xp + ib * FSC + (st2 + 1) * 128],
                            [pf_t0, pf_t1][tid % 2][:, 0:128]))

        def A_frT_te(fcn, fan):
            @blk.tensor
            def _(te):
                for st2 in range(2):
                    tid = 16 + st2
                    v_xe.w(te, fan['bvxe'] + tid - 1)
                    t_fr.inc(te.transpose(
                        [pf_t0, pf_t1][tid % 2][0:1, 0:128],
                        f_sc[:, st2 * 8 + 1:st2 * 8 + 2], id32[:]))

        def A_fr_dve(fcn, fan, btbc_cur):
            mp = (fcn % 2) * 256
            @blk.vector
            def _(v):
                if btbc_cur is not None:
                    t_bc.w(v, btbc_cur + 1)
                for st2 in range(2):
                    tid = 16 + st2
                    t_fr.w(v, fan['btfr'] + st2 + 1)
                    v_fr.inc(v.tensor_copy(
                        f_m2r[0:1, mp + st2 * 128:mp + (st2 + 1) * 128],
                        [pf_t0, pf_t1][tid % 2][0:1, 0:128]))
                v.drain()
                v.tensor_scalar(f_row[0:1, 0:FSC], f_m2r[0:1, mp:mp + FSC],
                                wbar[0:1, 0:1], 1.0 / 127, ALU.mult, ALU.mult)
                v.drain()
                v.tensor_copy(f_rhi2[:], f_row[0:1, 0:FSC])
                v.drain()
                v_fr.inc(v.tensor_tensor(f_rlo2[:], f_row[0:1, 0:FSC], f_rhi2[:],
                                         ALU.subtract))

        def emit_gate_prefetch(marks):
            pre = dict(bdfw=d_fw.base(), bdfd=d_fd.base())

            @blk.sync
            def _(sy):
                wq_wait(sy)
                v_res.w(sy, all_res)
                for oc in range(3):
                    if marks is not None:
                        t_fg.w(sy, marks[0] + NOC)
                    buf = (oc % 3) * 1024
                    d_fw.inc(sy.dma_start(
                        out=f_wg[:, buf:buf + 1024].rearrange("p (a c) -> p a c", c=128),
                        in_=wgq_h[:, oc * 128:(oc + 1) * 128].rearrange(
                            "(a p) c -> p a c", p=128)), oc)
                    d_fw.inc(sy.dma_start(
                        out=f_wu[:, buf:buf + 1024].rearrange("p (a c) -> p a c", c=128),
                        in_=wuq_h[:, oc * 128:(oc + 1) * 128].rearrange(
                            "(a p) c -> p a c", p=128)), oc)

            @blk.sync
            def _(sy):
                for ec in range(2):
                    if marks is not None:
                        t_dn.w(sy, marks[1] + 7 + ec)
                    d_fd.inc(sy.dma_start(
                        out=f_wd[:, (ec % 2) * 4096:(ec % 2 + 1) * 4096].rearrange(
                            "p (a c) -> p a c", c=128),
                        in_=wdq_h[:, ec * 128:(ec + 1) * 128].rearrange(
                            "(a p) c -> p a c", p=128)), ec)
            return pre

        fC = snapA()
        wpre = emit_gate_prefetch(None)

        @blk.scalar
        def _(sc):
            v_res.w(sc, all_res)
            wq_wait(sc)
            for st2 in range(2):
                a_f1.inc(sc.activation(
                    f_ln[:, st2 * E:(st2 + 1) * E],
                    x_tm[:, st2 * E:(st2 + 1) * E], AFT.Square,
                    accum_out=f_st[:, st2 * 8:st2 * 8 + 1]))
                sc.drain()

        A_stats_dve(0, fC, None)

        @blk.scalar
        def _(sc):
            for st2 in range(2):
                cs = st2 * 8
                v_f1.w(sc, fC['bvst'] + st2 + 1)
                a_f2.inc(sc.activation(
                    f_st[:, cs + 5:cs + 6], f_st[:, cs + 4:cs + 5], AFT.Sqrt,
                    bias=eps_t[:, 0:1], scale=1.0))
                sc.drain()

        A_quant_dve(0, fC, None)
        A_xqT_te(0, fC)
        A_xqT_dve(0, fC)
        A_frT_te(0, fC)
        A_fr_dve(0, fC, None)

        for fc in range(NFC if stage != 'ffndbg' else 1):
            btbc, bvbc = t_bc.n, v_bc.n
            btfg, bvgr, basi, bvgu = t_fg.n, v_gr.n, a_si.n, v_gu.n
            btmx, bvmx = t_mx.n, v_mx.n
            btsr, bvsr, btsb, bvsb = t_sr.n, v_sr.n, t_sb.n, v_sb2.n
            btfo, bvfo = t_fo.n, v_fo.n
            bvgq, btdn, bvde = v_gq.n, t_dn.n, v_de.n
            bdfw, bdfd = wpre['bdfw'], wpre['bdfd']
            btdt, bvdt = t_dt.n, v_dt.n
            prev_dt = v_dt.n
            xp = (fc % 2) * 8 * FSC
            mp = (fc % 2) * 256
            last_fc = (fc == NFC - 1) or stage == 'ffndbg'
            fN = None if last_fc else snapA()

            @blk.tensor
            def _(te):
                v_fr.w(te, fC['bvfr'] + 3)
                v_sb2.w(te)
                if fc >= 1:
                    v_bc.w(te, bvbc)
                te.matmul(pf_b[:, 0:FSC], onesb[0:1, :], f_rhi2[0:1, :],
                          start=True, stop=False)
                t_bc.inc(te.matmul(pf_b[:, 0:FSC], onesb[0:1, :], f_rlo2[0:1, :],
                                   start=False, stop=True))

            @blk.vector
            def _(v):
                t_bc.w(v, btbc + 1)
                v_bc.inc(v.tensor_copy(f_fb[:], pf_b[:, 0:FSC]))
                t_mx.w(v, btmx)
                v.memset(f_mx[:], 0.0)
                v.memset(f_mx2[:], 0.0)
                v_bc.inc(v.drain())

            @blk.sync
            def _(sy):
                for oc in range(3, NOC):
                    t_fg.w(sy, btfg + oc - 2)
                    buf = (oc % 3) * 1024
                    d_fw.inc(sy.dma_start(
                        out=f_wg[:, buf:buf + 1024].rearrange("p (a c) -> p a c", c=128),
                        in_=wgq_h[:, oc * 128:(oc + 1) * 128].rearrange(
                            "(a p) c -> p a c", p=128)), oc)
                    d_fw.inc(sy.dma_start(
                        out=f_wu[:, buf:buf + 1024].rearrange("p (a c) -> p a c", c=128),
                        in_=wuq_h[:, oc * 128:(oc + 1) * 128].rearrange(
                            "(a p) c -> p a c", p=128)), oc)

            @blk.scalar
            def _(sc):
                if fN is not None:
                    v_res.w(sc, all_res)
                    v_f1.w(sc, fC['bvst'] + 2)
                    v_fq.w(sc, fC['bvfq'] + 2)
                    for st2 in range(2):
                        gst = (fc + 1) * 2 + st2
                        a_f1.inc(sc.activation(
                            f_ln[:, st2 * E:(st2 + 1) * E],
                            x_tm[:, gst * E:(gst + 1) * E], AFT.Square,
                            accum_out=f_st[:, st2 * 8:st2 * 8 + 1]))
                        sc.drain()
                for oc in range(NOC):
                    v_gr.w(sc, bvgr + oc + 1)
                    if oc >= 1:
                        v_gu.w(sc, bvgu + oc)
                    a_si.inc(sc.activation(f_si[:], f_gr[:], AFT.Silu))
                    if oc == 24 and fN is not None:
                        for st2 in range(2):
                            cs = st2 * 8
                            v_f1.w(sc, fN['bvst'] + st2 + 1)
                            a_f2.inc(sc.activation(
                                f_st[:, cs + 5:cs + 6], f_st[:, cs + 4:cs + 5],
                                AFT.Sqrt, bias=eps_t[:, 0:1], scale=1.0))
                            sc.drain()

            @blk.tensor
            def _(te):
                v_xe.w(te, fC['bvxe'] + 16)
                if fc >= 1:
                    v_gu.w(te, bvgu)
                for oc in range(NOC):
                    d_fw.w(te, oc, bdfw, per=32)
                    if oc >= 2:
                        v_gr.w(te, bvgr + oc - 1)
                        v_gu.w(te, bvgu + oc - 1)
                    buf = (oc % 3) * 1024
                    pg = [pf_g, pf_d0][oc % 2]
                    pu = [pf_u, pf_d1][oc % 2]
                    for ic in range(8):
                        te.matmul(pg[:, 0:FSC],
                                  f_wg[:, buf + ic * 128:buf + (ic + 1) * 128],
                                  f_xqf[:, xp + ic * FSC:xp + (ic + 1) * FSC],
                                  start=(ic == 0), stop=(ic == 7))
                    last = None
                    for ic in range(8):
                        last = te.matmul(pu[:, 0:FSC],
                                         f_wu[:, buf + ic * 128:buf + (ic + 1) * 128],
                                         f_xqf[:, xp + ic * FSC:xp + (ic + 1) * FSC],
                                         start=(ic == 0), stop=(ic == 7))
                    t_fg.inc(last)

            def fold_iter(v, oc):
                if oc < NOC:
                    t_fg.w(v, btfg + oc + 1)
                    v_bc.w(v, bvbc + 2)
                    if oc >= 1:
                        a_si.w(v, basi + oc)
                    pg = [pf_g, pf_d0][oc % 2]
                    v_gr.inc(v.tensor_tensor(f_gr[:], pg[:, 0:FSC], f_fb[:], ALU.mult))
                    v.drain()
                if oc >= 1:
                    a_si.w(v, basi + oc)
                    pu = [pf_u, pf_d1][(oc - 1) % 2]
                    v.tensor_tensor(f_gu[:, (oc - 1) * FSC:oc * FSC], f_si[:],
                                    pu[:, 0:FSC], ALU.mult)
                    v.drain()
                    v.tensor_tensor(f_mx[:], f_mx[:],
                                    f_gu[:, (oc - 1) * FSC:oc * FSC], ALU.max)
                    v.scalar_tensor_tensor(f_mx2[:],
                                           f_gu[:, (oc - 1) * FSC:oc * FSC],
                                           -1.0, f_mx2[:], ALU.mult, ALU.max)
                    v_gu.inc(v.drain())

            @blk.vector
            def _(v):
                for oc in range(0, 17):
                    fold_iter(v, oc)

            if fN is not None:
                A_stats_dve(fc + 1, fN, fC)

            @blk.vector
            def _(v):
                for oc in range(17, NOC + 1):
                    fold_iter(v, oc)
                v.tensor_tensor(f_mx[:], f_mx[:], f_mx2[:], ALU.max)
                v_gu.inc(v.drain())

            if fN is not None:
                A_quant_dve(fc + 1, fN, fC)

            @blk.tensor
            def _(te):
                v_gu.w(te, bvgu + NOC + 1)
                v_fr.w(te, fC['bvfr'] + 3)
                v_dt.w(te, prev_dt)
                for st2 in range(2):
                    t_mx.inc(te.transpose([pf_t0, pf_t1][st2][:, 0:128],
                                          f_mx[:, st2 * 128:(st2 + 1) * 128], id32[:]))

            @blk.vector
            def _(v):
                for st2 in range(2):
                    t_mx.w(v, btmx + st2 + 1)
                    c0 = st2 * 8
                    v.reduce_max(f_sc[:, c0 + 6:c0 + 7], [pf_t0, pf_t1][st2][:, 0:128],
                                 axis=AX.X)
                v_mx.inc(v.drain())

            @blk.tensor
            def _(te):
                v_mx.w(te, bvmx + 1)
                if fc >= 1:
                    v_fo.w(te, bvfo)
                for st2 in range(2):
                    if st2 >= 1:
                        v_sr.w(te, bvsr + 1)
                    t_sr.inc(te.transpose(pf_r[0:1, 0:128],
                                          f_sc[:, st2 * 8 + 6:st2 * 8 + 7], id32[:]))

            @blk.vector
            def _(v):
                v_dt.w(v, prev_dt)
                for st2 in range(2):
                    t_sr.w(v, btsr + st2 + 1)
                    v_sr.inc(v.tensor_copy(f_mgr[0:1, st2 * 128:(st2 + 1) * 128],
                                           pf_r[0:1, 0:128]))
                v.drain()
                v.tensor_tensor(f_trw[0:1, 0:FSC], f_mgr[0:1, 0:FSC],
                                f_m2r[0:1, mp:mp + FSC], ALU.mult)
                v.drain()
                v.tensor_scalar(f_trw[0:1, 0:FSC], f_trw[0:1, 0:FSC],
                                wbar[0:1, 1:2], 1.0 / 127, ALU.mult, ALU.mult)
                v.drain()
                v.tensor_scalar(f_trw[0:1, 0:FSC], f_trw[0:1, 0:FSC],
                                1.0, 1e-5, ALU.mult, ALU.max)
                v.drain()
                v.tensor_scalar(f_for[0:1, 0:FSC], f_trw[0:1, 0:FSC],
                                wbar[0:1, 2:3], 1.0 / 127, ALU.mult, ALU.mult)
                v.drain()
                v.reciprocal(f_trw[0:1, 0:FSC], f_trw[0:1, 0:FSC])
                v.drain()
                v.tensor_tensor(f_row[0:1, FSC:2 * FSC], f_m2r[0:1, mp:mp + FSC],
                                f_trw[0:1, 0:FSC], ALU.mult)
                v.drain()
                v.tensor_single_scalar(f_row[0:1, FSC:2 * FSC],
                                       f_row[0:1, FSC:2 * FSC],
                                       wbar[0:1, 1:2], ALU.mult)
                v.drain()
                v.tensor_copy(f_rhi[:], f_row[0:1, FSC:2 * FSC])
                v.drain()
                v_sr.inc(v.tensor_tensor(f_rlo[:], f_row[0:1, FSC:2 * FSC], f_rhi[:],
                                         ALU.subtract))

            @blk.tensor
            def _(te):
                v_sr.w(te, bvsr + 3)
                for st2 in range(2):
                    if st2 >= 1:
                        v_fo.w(te, bvfo + 1)
                    t_fo.inc(te.transpose(pf_r[:, 0:1],
                                          f_for[0:1, st2 * 128:(st2 + 1) * 128],
                                          id32[0:1, 0:1]))

            @blk.vector
            def _(v):
                for st2 in range(2):
                    t_fo.w(v, btfo + st2 + 1)
                    v_fo.inc(v.tensor_copy(f_st[:, 18 + st2:19 + st2],
                                           pf_r[:, 0:1]))

            @blk.tensor
            def _(te):
                v_sr.w(te, bvsr + 3)
                v_bc.w(te)
                te.matmul(pf_b[:, 0:FSC], onesb[0:1, :], f_rhi[0:1, :],
                          start=True, stop=False)
                t_sb.inc(te.matmul(pf_b[:, 0:FSC], onesb[0:1, :], f_rlo[0:1, :],
                                   start=False, stop=True))

            @blk.vector
            def _(v):
                t_sb.w(v, btsb + 1)
                v_sb2.inc(v.tensor_copy(f_sb[:], pf_b[:, 0:FSC]))

            @blk.vector
            def _(v):
                v_sb2.w(v, bvsb + 1)
                for oc in range(NOC):
                    if fc >= 1 and oc == 0:
                        t_dn.w(v, btdn)
                    v.tensor_tensor(f_gr[:], f_gu[:, oc * FSC:(oc + 1) * FSC],
                                    f_sb[:], ALU.mult)
                    v.drain()
                    v.tensor_scalar(f_si[:], f_gr[:], M23, -M23, ALU.add, ALU.add)
                    v.tensor_scalar(f_fb[:], f_gr[:], M17, c5217[:, 0:1],
                                    ALU.add, ALU.add)
                    v.drain()
                    v.tensor_single_scalar(f_fb[:], f_fb[:], -M52, ALU.add)
                    v.drain()
                    v_gq.inc(v.scalar_tensor_tensor(
                        f_guq[:, oc * FSC:(oc + 1) * FSC], f_si[:], h_t[:, 0:1],
                        f_fb[:], ALU.mult, ALU.add))
                    v.drain()

            @blk.sync
            def _(sy):
                for ec in range(2, NEC):
                    t_dn.w(sy, btdn + ec - 1)
                    d_fd.inc(sy.dma_start(
                        out=f_wd[:, (ec % 2) * 4096:(ec % 2 + 1) * 4096].rearrange(
                            "p (a c) -> p a c", c=128),
                        in_=wdq_h[:, ec * 128:(ec + 1) * 128].rearrange(
                            "(a p) c -> p a c", p=128)), ec)

            if not last_fc:
                wpre = emit_gate_prefetch((btfg, btdn))

            @blk.tensor
            def _(te):
                v_gq.w(te, bvgq + NOC)
                for ec in range(NEC):
                    d_fd.w(te, ec, bdfd)
                    if ec >= 2:
                        v_de.w(te, bvde + ec - 1)
                    pd = [pf_d0, pf_d1][ec % 2]
                    last = None
                    for ic in range(NOC):
                        last = te.matmul(
                            pd[:, 0:FSC],
                            f_wd[:, (ec % 2) * 4096 + ic * 128:
                                 (ec % 2) * 4096 + (ic + 1) * 128],
                            f_guq[:, ic * FSC:(ic + 1) * FSC],
                            start=(ic == 0), stop=(ic == NOC - 1))
                    t_dn.inc(last)

            @blk.vector
            def _(v):
                for ec in range(NEC):
                    t_dn.w(v, btdn + ec + 1)
                    pd = [pf_d0, pf_d1][ec % 2]
                    v_de.inc(v.tensor_copy(f_do[:, ec * FSC:(ec + 1) * FSC],
                                           pd[:, 0:FSC]))

            @blk.tensor
            def _(te):
                v_de.w(te, bvde + NEC)
                v_mx.w(te, bvmx + 1)
                for ec in range(NEC):
                    for st2 in range(2):
                        tid = ec * 2 + st2
                        if tid >= 2:
                            v_dt.w(te, bvdt + tid - 1)
                        t_dt.inc(te.transpose(
                            [pf_t0, pf_t1][tid % 2][:, 0:128],
                            f_do[:, ec * FSC + st2 * 128:ec * FSC + (st2 + 1) * 128],
                            id32[:]))

            @blk.vector
            def _(v):
                v_fo.w(v, bvfo + 2)
                for ec in range(NEC):
                    for st2 in range(2):
                        tid = ec * 2 + st2
                        t_dt.w(v, btdt + tid + 1)
                        gst = fc * 2 + st2
                        xd = x_tm[:, gst * E + ec * 128:gst * E + (ec + 1) * 128]
                        v_dt.inc(v.scalar_tensor_tensor(
                            xd, [pf_t0, pf_t1][tid % 2][:, 0:128],
                            f_st[:, 18 + st2:19 + st2], xd, ALU.mult, ALU.add))

            @blk.gpsimd
            def _(gp):
                v_dt.w(gp, bvdt + 16)
                for st2 in range(2):
                    gst = fc * 2 + st2
                    d_y.inc(gp.dma_start(out=y_h[gst * 128:(gst + 1) * 128, :],
                                         in_=x_tm[:, gst * E:(gst + 1) * E]), 16)

            if fN is not None:
                A_xqT_te(fc + 1, fN)
                A_xqT_dve(fc + 1, fN)
                A_frT_te(fc + 1, fN)
                A_fr_dve(fc + 1, fN, btbc)
                fC = fN

        if stage == 'ffndbg':
            @blk.gpsimd
            def _(gp):
                gp.wait_ge(d_y.h, d_y.n)
                for (nm, buf, w) in [("dxq", f_xq, 2048),
                                     ("dst", f_st[:, 18:20], 2), ("dgu", f_gu, 8192),
                                     ("dguq", f_guq, 8192), ("dfb", f_fb, 256),
                                     ("dsb", f_sb, 256), ("ddo", f_do, 2048),
                                     ("dxqf", f_xqf, 2048), ("dln", f_ln, 2048)]:
                    src_ap = buf if not hasattr(buf, "shape") or not hasattr(buf, "ap") else buf
                    try:
                        src_ap = buf[:, 0:w]
                    except Exception:
                        src_ap = buf
                    d_y.inc(gp.dma_start(out=dbg_aps[nm], in_=src_ap), 16)
                gp.wait_ge(d_y.h, d_y.n)

        @blk.gpsimd
        def _(gp):
            gp.wait_ge(d_y.h, d_y.n)

        if dbg:
            @blk.gpsimd
            def _(gp):
                gp.dma_start(out=dbg_h[0:128, 0:4096], in_=x_tm[:, 0:4096]).then_inc(d_y.h, 16)
                gp.wait_ge(d_y.h, d_y.n + 16)

    es.close()
    return nc


def kernel(L0_attn_x, L0_attn_proj_weight, L0_attn_attn_weight,
           L0_attn_value_weight, L0_attn_out_w, L0_attn_out_b,
           L0_ffn_w_gate, L0_ffn_w_up, L0_ffn_w_down):
    import ml_dtypes
    x = np.asarray(L0_attn_x, np.float32)
    pw = np.ascontiguousarray(np.asarray(L0_attn_proj_weight, np.float32)
                              .reshape(E, E))
    aw = np.ascontiguousarray(np.asarray(L0_attn_attn_weight, np.float32)
                              .reshape(8 * 128, 64))
    vw = np.ascontiguousarray(np.asarray(L0_attn_value_weight, np.float32)
                              .reshape(8 * 128, 64))
    owT = np.ascontiguousarray(np.asarray(L0_attn_out_w, np.float32).T)
    ob = np.ascontiguousarray(np.asarray(L0_attn_out_b, np.float32)
                              .reshape(1, E))
    wgT = np.ascontiguousarray(np.asarray(L0_ffn_w_gate, np.float32).T)
    wuT = np.ascontiguousarray(np.asarray(L0_ffn_w_up, np.float32).T)
    wdT = np.ascontiguousarray(np.asarray(L0_ffn_w_down, np.float32).T)
    id32 = np.eye(128, dtype=np.float32)
    idbf = np.eye(128, dtype=ml_dtypes.bfloat16)
    onesb = np.ones((1, 128), dtype=ml_dtypes.bfloat16)

    nc = build_program()
    in_maps = []
    for c in range(8):
        b, sh = c // 2, c % 2
        xr = np.ascontiguousarray(np.roll(x[b], -2048 * sh, axis=0))
        in_maps.append(dict(x_full=xr, pw=pw, awp=aw, vwp=vw, owT=owT, ob=ob,
                            wgT=wgT, wuT=wuT, wdT=wdT, id32=id32, idbf=idbf,
                            onesb=onesb))
    res = run_bass_kernel_spmd(nc, in_maps, core_ids=list(range(8)))
    y = np.empty((4, S, E), np.float32)
    for c in range(8):
        b, sh = c // 2, c % 2
        y[b, 2048 * sh:2048 * (sh + 1), :] = res.results[c]["y"]
    return y


# version 16
# speedup vs baseline: 1.0686x; 1.0153x over previous
import numpy as np
import concourse.bass as bass
import concourse.mybir as mybir
from concourse.bass_utils import run_bass_kernel_spmd
from contextlib import ExitStack

F32 = mybir.dt.float32
BF16 = mybir.dt.bfloat16
ALU = mybir.AluOpType
AFT = mybir.ActivationFunctionType
AX = mybir.AxisListType

M23 = 1.5 * 2.0**23
M52 = 1.5 * 2.0**52
M17 = 1.5 * 2.0**17

S, E = 4096, 1024
SEQ = 2048
INNER = 4096
NRC = 32
FSC = 256
NFC = SEQ // FSC
NOC = INNER // 128
NEC = E // 128


class Sem:
    __slots__ = ("h", "n")

    def __init__(self, es, nc, name):
        self.h = es.enter_context(nc.semaphore(name))
        self.n = 0

    def inc(self, instr, k=1):
        instr.then_inc(self.h, k)
        self.n += k
        return self.n

    def w(self, eng, mark=None):
        m = self.n if mark is None else mark
        if m > 0:
            eng.wait_ge(self.h, m)


class DSem:

    def __init__(self, es, nc, name, k):
        self.s = [Sem(es, nc, f"{name}_{i}") for i in range(k)]
        self.k = k

    def base(self):
        return [s.n for s in self.s]

    def inc(self, instr, idx, amt=16):
        return self.s[idx % self.k].inc(instr, amt)

    def w(self, eng, idx, base, per=16):
        sl = idx % self.k
        eng.wait_ge(self.s[sl].h, base[sl] + per * (idx // self.k + 1))


def build_program(dbg=False, stage='full'):
    nc = bass.Bass("TRN2", target_bir_lowering=False)

    x_full = nc.dram_tensor("x_full", [S, E], F32, kind="ExternalInput").ap()
    pw_h = nc.dram_tensor("pw", [E, E], F32, kind="ExternalInput").ap()
    awp_h = nc.dram_tensor("awp", [8 * 128, 64], F32, kind="ExternalInput").ap()
    vwp_h = nc.dram_tensor("vwp", [8 * 128, 64], F32, kind="ExternalInput").ap()
    owT_h = nc.dram_tensor("owT", [E, E], F32, kind="ExternalInput").ap()
    ob_h = nc.dram_tensor("ob", [1, E], F32, kind="ExternalInput").ap()
    wgT_h = nc.dram_tensor("wgT", [128, 256 * 128], F32, kind="ExternalInput").ap()
    wuT_h = nc.dram_tensor("wuT", [128, 256 * 128], F32, kind="ExternalInput").ap()
    wdT_h = nc.dram_tensor("wdT", [128, 256 * 128], F32, kind="ExternalInput").ap()
    id32_h = nc.dram_tensor("id32", [128, 128], F32, kind="ExternalInput").ap()
    ones_h = nc.dram_tensor("onesb", [1, 128], BF16, kind="ExternalInput").ap()
    idbf_h = nc.dram_tensor("idbf", [128, 128], BF16, kind="ExternalInput").ap()
    y_h = nc.dram_tensor("y", [SEQ, E], F32, kind="ExternalOutput").ap()

    wgq_h = nc.dram_tensor("wgq", [128, 256 * 128], BF16, kind="Internal").ap()
    wuq_h = nc.dram_tensor("wuq", [128, 256 * 128], BF16, kind="Internal").ap()
    wdq_h = nc.dram_tensor("wdq", [128, 256 * 128], BF16, kind="Internal").ap()
    dbg_h = nc.dram_tensor("dbg", [128, 4096], F32,
                           kind="ExternalOutput").ap() if dbg else None
    dbg_aps = {}

    es = ExitStack()
    sb = lambda n, s, d: es.enter_context(nc.sbuf_tensor(n, s, d))
    sem = lambda n: Sem(es, nc, n)

    x_tm = sb("x_tm", [128, 16 * E], F32)
    id32 = sb("id32s", [128, 128], F32)
    idbf = sb("idbfs", [128, 128], BF16)
    onesb = sb("onesbs", [1, 128], BF16)
    eps_t = sb("eps_t", [128, 1], F32)
    g_t = sb("g_t", [128, 1], F32)
    h_t = sb("h_t", [128, 1], F32)
    c5217 = sb("c5217", [128, 1], F32)
    sc_t = sb("sc_t", [128, 8], F32)
    mkrow = sb("mkrow", [1, 32], F32)
    mh_row = sb("mh_row", [1, 512], F32)
    ones_row = sb("ones_row", [1, 128], F32)
    onesb_t = sb("onesb_t", [128, 128], BF16)
    mq_row = sb("mq_row", [1, 128], F32)
    wbar = sb("wbar", [1, 4], F32)
    obhi = sb("obhi", [1, E], BF16)
    oblo = sb("oblo", [1, E], BF16)

    s_dc = sem("s_dc"); s_dx = sem("s_dx"); s_g = sem("s_g"); s_ob = sem("s_ob")
    d_w = DSem(es, nc, "d_w", 2); v_wr = sem("v_wr"); t_wt = sem("t_wt"); v_ws = sem("v_ws")
    g_wb = sem("g_wb"); v_wq = sem("v_wq"); o_wq = DSem(es, nc, "o_wq", 2); s_gp = sem("s_gp")
    d_pw = DSem(es, nc, "d_pw", 2); v_pw = sem("v_pw")
    d_xa = DSem(es, nc, "d_xa", 2); a_sq = sem("a_sq"); v_st = sem("v_st"); a_rs = sem("a_rs")
    v_ln = sem("v_ln"); t_lt = sem("t_lt"); v_ls = sem("v_ls"); t_pj = sem("t_pj")
    v_he = sem("v_he"); v_mh = sem("v_mh"); t_mh = sem("t_mh"); g_mh = sem("g_mh")
    d_aw = sem("d_aw"); v_aw = sem("v_aw")
    t_v = sem("t_v"); v_ve = sem("v_ve")
    t_q = sem("t_q"); v_qe = sem("v_qe"); t_ct = sem("t_ct"); v_cd = sem("v_cd")
    g_cb = sem("g_cb")
    t_s = sem("t_s"); a_e = sem("a_e"); v_w = sem("v_w"); t_av = sem("t_av")
    t_cb = sem("t_cb"); v_as = sem("v_as"); d_pk = Sem(es, nc, "d_pk")
    v_nm = sem("v_nm"); t_at = sem("t_at"); v_af = sem("v_af"); t_op = sem("t_op")
    v_res = sem("v_res")
    a_f1 = sem("a_f1"); v_f1 = sem("v_f1"); a_f2 = sem("a_f2"); v_f2 = sem("v_f2")
    v_fq = sem("v_fq"); t_xt = sem("t_xt"); v_xe = sem("v_xe")
    t_fr = sem("t_fr"); v_fr = sem("v_fr"); t_bc = sem("t_bc"); v_bc = sem("v_bc")
    d_fw = DSem(es, nc, "d_fw", 3); t_fg = sem("t_fg"); v_gr = sem("v_gr"); a_si = sem("a_si")
    v_gu = sem("v_gu"); t_mx = sem("t_mx"); v_mx = sem("v_mx")
    t_sr = sem("t_sr"); v_sr = sem("v_sr"); t_sb = sem("t_sb"); v_sb2 = sem("v_sb2")
    t_fo = sem("t_fo"); v_fo = sem("v_fo")
    v_gq = sem("v_gq"); d_fd = DSem(es, nc, "d_fd", 2); t_dn = sem("t_dn"); v_de = sem("v_de")
    t_dt = sem("t_dt"); v_dt = sem("v_dt"); d_y = sem("d_y")

    blk = es.enter_context(nc.Block())

    @blk.sync
    def _(sy):
        s_dc.inc(sy.dma_start(out=id32[:], in_=id32_h), 16)
        s_dc.inc(sy.dma_start(out=onesb[:], in_=ones_h), 16)
        s_dc.inc(sy.dma_start(out=idbf[:], in_=idbf_h), 16)
        for t in range(16):
            s_dx.inc(sy.dma_start(out=x_tm[:, t * E:(t + 1) * E],
                                  in_=x_full[t * 128:(t + 1) * 128, :]), 16)

    @blk.vector
    def _(v):
        v.memset(eps_t[:], 1e-5)
        v.memset(ones_row[:], 1.0)
        v.memset(onesb_t[:], 1.0)
        v.memset(sc_t[:, 0:1], 0.75)
        v.drain()
        v.tensor_scalar(sc_t[:, 1:2], sc_t[:, 0:1], M23, -M23, ALU.add, ALU.add)
        v.drain()
        v.tensor_sub(sc_t[:, 2:3], sc_t[:, 1:2], sc_t[:, 0:1])
        v.drain()
        v.tensor_scalar(g_t[:], sc_t[:, 2:3], 4.0, -1.0, ALU.mult, ALU.add)
        v.drain()
        v.tensor_single_scalar(h_t[:], g_t[:], 1.0, ALU.add)
        v.memset(c5217[:], M52)
        v.drain()
        s_g.inc(v.tensor_single_scalar(c5217[:], c5217[:], -M17, ALU.add))

    with ExitStack() as wes:
        wsb = lambda n, s, d: wes.enter_context(nc.sbuf_tensor(n, s, d))
        wraw = wsb("wraw", [128, 2 * 4096], F32)
        wt = wsb("wt", [128, 4096], F32)
        wn = wsb("wn", [128, 4096], F32)
        wqt = wsb("wqt", [128, 2 * 4096], BF16)
        asum = wsb("asum", [128, 33], F32)
        swbc = wsb("swbc", [128, 1], F32)
        obf = wsb("obf", [1, E], F32)
        wrow = wsb("wrow", [1, 128], F32)
        ps_wq = wes.enter_context(nc.psum_tensor("ps_wq", [128, 128], F32))

        @blk.sync
        def _(sy):
            s_dc.inc(sy.dma_start(out=obf[:], in_=ob_h), 16)

        @blk.vector
        def _(v):
            s_dc.w(v)
            v.tensor_copy(obhi[:], obf[:])
            v.drain()
            s_ob.inc(v.tensor_tensor(oblo[:], obf[:], obhi[:], ALU.subtract))

        specs = [(wgT_h, wgq_h, 8, 4096, 0), (wuT_h, wuq_h, 8, 4096, 1),
                 (wdT_h, wdq_h, 8, 4096, 2)]
        if stage == 'setup0':
            specs = []
        elif stage == 'wq1':
            specs = specs[:1]

        for si, (src, dst, nt, wdt, wi) in enumerate(specs):
            bdw, bvr = d_w.base(), v_wr.n

            @blk.sync
            def _(sy, src=src, nt=nt, wdt=wdt, bvr=bvr):
                v_wq.w(sy)
                for t in range(nt):
                    if t >= 2:
                        v_wr.w(sy, bvr + t - 1)
                    d_w.inc(sy.dma_start(
                        out=wraw[:, (t % 2) * 4096:(t % 2) * 4096 + wdt],
                        in_=src[:, t * 4096:t * 4096 + wdt]), t)

            @blk.vector
            def _(v, nt=nt, wdt=wdt, bdw=bdw):
                for t in range(nt):
                    d_w.w(v, t, bdw)
                    v_wr.inc(v.reduce_sum(
                        asum[:, t:t + 1],
                        wraw[:, (t % 2) * 4096:(t % 2) * 4096 + wdt],
                        axis=AX.X, apply_absolute_value=True))
                v.drain()
                v.reduce_sum(asum[:, 32:33], asum[:, 0:nt], axis=AX.X)
                v_wr.inc(v.drain())

            @blk.tensor
            def _(te):
                v_wr.w(te)
                s_dc.w(te)
                t_wt.inc(te.transpose(ps_wq[0:1, 0:128], asum[:, 32:33], id32[:]))

            @blk.vector
            def _(v, nt=nt, wdt=wdt, wi=wi):
                t_wt.w(v)
                g_wb.w(v)
                v.reduce_sum(sc_t[0:1, 4:5], ps_wq[0:1, 0:128], axis=AX.X)
                v.drain()
                v.tensor_scalar(wbar[0:1, wi:wi + 1], sc_t[0:1, 4:5],
                                1.0 / (nt * 128 * wdt), 1e-5, ALU.mult, ALU.max)
                v.drain()
                v_ws.inc(v.reciprocal(sc_t[0:1, 5:6], wbar[0:1, wi:wi + 1]))

            @blk.vector
            def _(v):
                v_ws.w(v)
                v_ws.inc(v.tensor_single_scalar(wrow[0:1, 0:128], ones_row[0:1, :],
                                                sc_t[0:1, 5:6], ALU.mult))

            @blk.tensor
            def _(te):
                v_ws.w(te)
                t_wt.inc(te.transpose(ps_wq[:, 0:1], wrow[0:1, 0:128],
                                      id32[0:1, 0:1]))

            @blk.vector
            def _(v):
                t_wt.w(v)
                g_wb.inc(v.tensor_copy(swbc[:], ps_wq[:, 0:1]))

            bdw2, bvq, boq = d_w.base(), v_wq.n, o_wq.base()

            @blk.sync
            def _(sy, src=src, nt=nt, wdt=wdt, bvq=bvq):
                v_wr.w(sy)
                for t in range(nt):
                    if t >= 2:
                        v_wq.w(sy, bvq + t - 1)
                    d_w.inc(sy.dma_start(
                        out=wraw[:, (t % 2) * 4096:(t % 2) * 4096 + wdt],
                        in_=src[:, t * 4096:t * 4096 + wdt]), t)

            @blk.vector
            def _(v, nt=nt, wdt=wdt, bdw2=bdw2, boq=boq):
                g_wb.w(v)
                for i, s_ in enumerate(o_wq.s):
                    if boq[i] > 0:
                        v.wait_ge(s_.h, boq[i])
                for t in range(nt):
                    d_w.w(v, t, bdw2)
                    if t >= 2:
                        o_wq.w(v, t - 2, boq)
                    buf = wraw[:, (t % 2) * 4096:(t % 2) * 4096 + wdt]
                    qb = wqt[:, (t % 2) * 4096:(t % 2) * 4096 + wdt]
                    v.tensor_single_scalar(wt[:, 0:wdt], buf, swbc[:, 0:1], ALU.mult)
                    v.drain()
                    v.tensor_single_scalar(wn[:, 0:wdt], wt[:, 0:wdt], -0.5, ALU.is_le)
                    v.drain()
                    v_wq.inc(v.scalar_tensor_tensor(
                        qb, wt[:, 0:wdt], 0.5, wn[:, 0:wdt], ALU.is_ge, ALU.subtract))
                    v.drain()

            @blk.gpsimd
            def _(g, dst=dst, nt=nt, wdt=wdt, bvq=bvq):
                for t in range(nt):
                    v_wq.w(g, bvq + t + 1)
                    o_wq.inc(g.dma_start(
                        out=dst[:, t * 4096:t * 4096 + wdt],
                        in_=wqt[:, (t % 2) * 4096:(t % 2) * 4096 + wdt]), t)

    def _early_out():
        @blk.gpsimd
        def _(gp):
            for s_ in o_wq.s:
                if s_.n > 0:
                    gp.wait_ge(s_.h, s_.n)
            for t in range(2):
                d_y.inc(gp.dma_start(out=y_h[t * 128:(t + 1) * 128, :],
                                     in_=x_tm[:, t * E:(t + 1) * E]), 16)
            gp.wait_ge(d_y.h, d_y.n)
        es.close()
        return nc

    if stage in ('wqonly', 'setup0', 'wq1'):
        return _early_out()

    with ExitStack() as aes:
        asb = lambda n, s, d: aes.enter_context(nc.sbuf_tensor(n, s, d))
        h_hi = asb("h_hi", [128, 4 * 4096], BF16)
        h_lo = asb("h_lo", [128, 4 * 4096], BF16)
        vA = asb("vA", [128, NRC * 65], F32)
        vB = asb("vB", [128, NRC * 65], F32)
        stat = asb("stat", [128, 16], F32)
        mh_t = asb("mh_t", [128, 4], F32)
        mh_r = asb("mh_r", [4, 2], F32)

        pb_q = aes.enter_context(nc.psum_tensor("pb_q", [128, 512], F32))
        pb_o = aes.enter_context(nc.psum_tensor("pb_o", [128, 512], F32))
        pb_s = [aes.enter_context(nc.psum_tensor(f"pb_s{i}", [128, 512], F32))
                for i in range(2)]
        pb_av = [aes.enter_context(nc.psum_tensor(f"pb_av{i}", [128, 512], F32))
                 for i in range(4)]

        def emit_hbuild(g, hes):
            hsb = lambda n, s, d: hes.enter_context(nc.sbuf_tensor(n + str(g), s, d))
            pw_hi = hsb("pw_hi", [128, 8 * 512], BF16)
            pw_lo = hsb("pw_lo", [128, 8 * 512], BF16)
            pw_f = hsb("pw_f", [128, 2 * 512], F32)
            xa = hsb("xa", [128, 2 * E], F32)
            xsq = hsb("xsq", [128, E], F32)
            lnt = hsb("lnt", [128, E], F32)
            ln_hi = hsb("ln_hi", [128, 4 * E], BF16)
            ln_lo = hsb("ln_lo", [128, 4 * E], BF16)
            bdpw, bvpw = d_pw.base(), v_pw.n
            bdxa, basq, bvst, bars, bvln = d_xa.base(), a_sq.n, v_st.n, a_rs.n, v_ln.n
            btlt, bvls, btpj, bvhe = t_lt.n, v_ls.n, t_pj.n, v_he.n
            prev_res = v_res.n
            prev_nm = v_nm.n
            prev_af_h = v_af.n

            @blk.sync
            def _(sy):
                v_res.w(sy, prev_res)
                for s_ in o_wq.s:
                    if s_.n > 0:
                        sy.wait_ge(s_.h, s_.n)
                v_wq.w(sy)
                for ic in range(8):
                    if ic >= 2:
                        v_pw.w(sy, bvpw + ic - 1)
                    d_pw.inc(sy.dma_start(
                        out=pw_f[:, (ic % 2) * 512:(ic % 2 + 1) * 512],
                        in_=pw_h[ic * 128:(ic + 1) * 128, 512 * g:512 * (g + 1)]), ic)
                for rc in range(NRC):
                    if rc >= 2:
                        v_ln.w(sy, bvln + rc - 1)
                    d_xa.inc(sy.dma_start(
                        out=xa[:, (rc % 2) * E:(rc % 2 + 1) * E],
                        in_=x_full[rc * 128:(rc + 1) * 128, :]), rc)

            @blk.vector
            def _(v):
                t_pj.w(v, btpj)
                for ic in range(8):
                    d_pw.w(v, ic, bdpw)
                    src = pw_f[:, (ic % 2) * 512:(ic % 2 + 1) * 512]
                    v.tensor_copy(pw_hi[:, ic * 512:(ic + 1) * 512], src)
                    v.drain()
                    v_pw.inc(v.tensor_tensor(
                        pw_lo[:, ic * 512:(ic + 1) * 512], src,
                        pw_hi[:, ic * 512:(ic + 1) * 512], ALU.subtract))

            def emit_sq(sc, rc):
                c = (rc % 2) * 8
                d_xa.w(sc, rc, bdxa)
                if rc >= 2:
                    v_st.w(sc, bvst + rc - 1)
                a_sq.inc(sc.activation(
                    xsq[:, 0:E],
                    xa[:, (rc % 2) * E:(rc % 2 + 1) * E], AFT.Square,
                    accum_out=stat[:, c + 0:c + 1]))
                sc.drain()

            def emit_sqrt(sc, rc):
                c = (rc % 2) * 8
                v_st.w(sc, bvst + rc + 1)
                if rc >= 2:
                    v_ln.w(sc, bvln + rc - 1)
                a_rs.inc(sc.activation(
                    stat[:, c + 5:c + 6], stat[:, c + 4:c + 5], AFT.Sqrt,
                    bias=eps_t[:, 0:1], scale=1.0))

            @blk.scalar
            def _(sc):
                s_g.w(sc, 1)
                v_res.w(sc, prev_res)
                for s_ in o_wq.s:
                    if s_.n > 0:
                        sc.wait_ge(s_.h, s_.n)
                v_wq.w(sc)
                for rc in range(NRC):
                    emit_sq(sc, rc)
                    if rc >= 1:
                        emit_sqrt(sc, rc - 1)
                emit_sqrt(sc, NRC - 1)

            def emit_stats2(v, rc):
                c = (rc % 2) * 8
                a_sq.w(v, basq + rc + 1)
                if rc >= 2:
                    a_rs.w(v, bars + rc - 1)
                if rc >= 2:
                    v_ln.w(v, bvln + rc - 1)
                v.reduce_sum(stat[:, c + 1:c + 2],
                             xa[:, (rc % 2) * E:(rc % 2 + 1) * E], axis=AX.X)
                v.drain()
                v.tensor_single_scalar(stat[:, c + 2:c + 3], stat[:, c + 1:c + 2],
                                       1.0 / E, ALU.mult)
                v.drain()
                v.tensor_tensor(stat[:, c + 3:c + 4], stat[:, c + 2:c + 3],
                                stat[:, c + 2:c + 3], ALU.mult)
                v.drain()
                v_st.inc(v.scalar_tensor_tensor(
                    stat[:, c + 4:c + 5], stat[:, c + 0:c + 1], 1.0 / E,
                    stat[:, c + 3:c + 4], ALU.mult, ALU.subtract))
                v.drain()

            def emit_ln(v, rc):
                c = (rc % 2) * 8
                a_rs.w(v, bars + rc + 1)
                if rc >= 1:
                    t_lt.w(v, btlt + 8 * rc)
                v.reciprocal(stat[:, c + 6:c + 7], stat[:, c + 5:c + 6])
                v.drain()
                v_ln.inc(v.tensor_scalar(
                    lnt[:], xa[:, (rc % 2) * E:(rc % 2 + 1) * E],
                    stat[:, c + 2:c + 3], stat[:, c + 6:c + 7],
                    ALU.subtract, ALU.mult))
                v.drain()



            ltp = [pb_q, pb_o, pb_s[0], pb_s[1]]

            def emit_lt(te, rc):
                v_ln.w(te, bvln + rc + 1)
                for ib in range(8):
                    tid = rc * 8 + ib
                    if tid >= 4:
                        v_ls.w(te, bvls + tid - 3)
                    t_lt.inc(te.transpose(
                        ltp[ib % 4][:, 0:128],
                        lnt[:, ib * 128:(ib + 1) * 128], id32[:]))

            def emit_pj(te, rcq):
                v_ls.w(te, bvls + 32 * rcq + 32)
                for lc in range(4):
                    pid = rcq * 4 + lc
                    pp = pb_av[lc]
                    if pid >= 4:
                        v_he.w(te, bvhe + pid - 3)
                    first = True
                    last = None
                    for (wop, rop) in ((pw_hi, ln_hi), (pw_hi, ln_lo), (pw_lo, ln_hi)):
                        for ic in range(8):
                            last = te.matmul(
                                pp[:, 0:512],
                                wop[:, ic * 512 + lc * 128:ic * 512 + (lc + 1) * 128],
                                rop[:, ic * 512:(ic + 1) * 512],
                                start=first, stop=(wop is pw_lo and ic == 7))
                            first = False
                    t_pj.inc(last)

            def emit_ls(v, rc):
                for ib in range(8):
                    tid = rc * 8 + ib
                    t_lt.w(v, btlt + tid + 1)
                    if ib == 0 and rc >= 4:
                        t_pj.w(v, btpj + 4 * (rc // 4))
                    src = ltp[ib % 4][:, 0:128]
                    dst = ib * 512 + (rc % 4) * 128
                    v.tensor_copy(ln_hi[:, dst:dst + 128], src)
                    v.drain()
                    v_ls.inc(v.tensor_tensor(
                        ln_lo[:, dst:dst + 128], src,
                        ln_hi[:, dst:dst + 128], ALU.subtract))
                    v.drain()

            def emit_he(v, rcq):
                for lc in range(4):
                    pid = rcq * 4 + lc
                    t_pj.w(v, btpj + pid + 1)
                    pp = pb_av[lc]
                    dst = lc * 4096 + rcq * 512
                    v.tensor_copy(h_hi[:, dst:dst + 512], pp[:, 0:512])
                    v.drain()
                    v_he.inc(v.tensor_tensor(
                        h_lo[:, dst:dst + 512], pp[:, 0:512],
                        h_hi[:, dst:dst + 512], ALU.subtract))
                    v.drain()

            @blk.tensor
            def _(te):
                s_dc.w(te)
                v_res.w(te, prev_res)
                v_nm.w(te, prev_nm)
                v_mh.w(te)
                g_cb.w(te)
                v_pw.w(te, bvpw + 8)
                v_ve.w(te)
                t_av.w(te)
                v_af.w(te, prev_af_h)
                for rc in range(NRC):
                    if rc >= 4 and rc % 4 == 0:
                        emit_pj(te, rc // 4 - 1)
                    emit_lt(te, rc)
                emit_pj(te, NRC // 4 - 1)

            @blk.vector
            def _(v):
                for rc in range(NRC + 3):
                    if rc < NRC:
                        emit_stats2(v, rc)
                    if 1 <= rc < NRC + 1:
                        emit_ln(v, rc - 1)
                    if 1 <= rc < NRC + 1:
                        emit_ls(v, rc - 1)
                    if rc >= 6 and (rc - 6) % 4 == 0:
                        emit_he(v, (rc - 6) // 4)

            @blk.vector
            def _(v):
                v_he.w(v, bvhe + (NRC // 4) * 4)
                for lc in range(4):
                    v.reduce_max(mh_t[:, lc:lc + 1], h_hi[:, lc * 4096:(lc + 1) * 4096],
                                 axis=AX.X, apply_absolute_value=True)
                v.drain()
                v_mh.inc(v.tensor_single_scalar(mh_t[:], mh_t[:], 1.01, ALU.mult))

            @blk.gpsimd
            def _(gp, g=g):
                v_mh.w(gp)
                if g >= 1:
                    v_mh.w(gp)
                g_mh.inc(gp.dma_start(out=mh_row[0:1, 0:512], in_=mh_t[:, 0:4]), 16)

            @blk.vector
            def _(v, g=g):
                g_mh.w(v)
                mv = mh_row.ap().rearrange("o (p l) -> o l p", l=4)
                for lc in range(4):
                    for j in range(2):
                        p = 4 * g + lc
                        v.reduce_max(
                            mkrow[0:1, 2 * p + j:2 * p + j + 1],
                            mv[0:1, lc:lc + 1, 64 * j:64 * (j + 1)], axis=AX.X)
                v_mh.inc(v.drain())

        def emit_attention(g, lp, T):
            (aw_f, vw_f, ow_f, aw_hi, aw_lo, vw_hi, vw_lo, ow_hi, ow_lo,
             q_hi, q_lo, w32, mq_t, cbias, csc, crow,
             attn_tm, attn_hi, attn_lo, rr_t, rhib, fbA_t, fbB_t, attnB_s) = T
            p = 4 * g + lp
            hofs = lp * 4096
            bdaw, bvaw = d_aw.n, v_aw.n
            btv, bvve = t_v.n, v_ve.n
            hdone = v_he.n
            prev_nm = v_nm.n
            prev_res = v_res.n
            prev_res2 = v_res.n
            prev_op = t_op.n
            prev_at = t_at.n
            prev_af = v_af.n
            prev_s = t_s.n
            prev_e = a_e.n

            @blk.sync
            def _(sy):
                t_pj.w(sy)
                v_aw.w(sy, bvaw)
                d_aw.inc(sy.dma_start(out=aw_f[:], in_=awp_h[p * 128:(p + 1) * 128, :]), 16)
                d_aw.inc(sy.dma_start(out=vw_f[:], in_=vwp_h[p * 128:(p + 1) * 128, :]), 16)
                d_aw.inc(sy.dma_start(out=ow_f[:], in_=owT_h[p * 128:(p + 1) * 128, :]), 16)

            @blk.vector
            def _(v):
                d_aw.w(v, bdaw + 48)
                t_q.w(v)
                t_v.w(v, btv)
                t_op.w(v, prev_op)
                t_av.w(v)
                v.tensor_copy(aw_hi[:], aw_f[:])
                v.tensor_copy(vw_hi[:], vw_f[:])
                v.tensor_copy(ow_hi[:], ow_f[:])
                v.drain()
                v.tensor_tensor(aw_lo[:], aw_f[:], aw_hi[:], ALU.subtract)
                v.tensor_tensor(vw_lo[:], vw_f[:], vw_hi[:], ALU.subtract)
                v.tensor_tensor(ow_lo[:], ow_f[:], ow_hi[:], ALU.subtract)
                v_aw.inc(v.drain())

            @blk.tensor
            def _(te):
                v_aw.w(te, bvaw + 1)
                v_he.w(te, hdone)
                v_af.w(te, prev_af)
                v_res.w(te, prev_res2)
                t_pj.w(te)
                for rc in range(NRC):
                    if rc >= 2:
                        v_ve.w(te, bvve + rc - 1)
                    last = None
                    for (tp, hp, pav) in (((0, 0), 0, pb_av[rc % 2]),
                                          ((64, 0), 64, pb_av[2 + rc % 2])):
                        first = True
                        for (hop, vop) in ((h_hi, vw_hi), (h_hi, vw_lo), (h_lo, vw_hi)):
                            last = te.matmul(
                                pav[:, 0:64],
                                hop[hp:hp + 64, hofs + rc * 128:hofs + (rc + 1) * 128],
                                vop[hp:hp + 64, :], start=first,
                                stop=(hop is h_lo), tile_position=tp)
                            first = False
                    t_v.inc(last)

            @blk.vector
            def _(v):
                for rc in range(NRC):
                    t_v.w(v, btv + rc + 1)
                    o = rc * 65
                    for (hp, vb) in ((0, vA), (64, vB)):
                        pav = pb_av[(0 if hp == 0 else 2) + rc % 2]
                        v.tensor_copy(vb[:, o:o + 64], pav[:, 0:64])
                        v.memset(vb[:, o + 64:o + 65], 1.0)
                    v_ve.inc(v.drain())

            def emit_resid(v, P):
                s_dx.w(v)
                for st in range(4):
                    for eh in range(2):
                        oid = 2 * st + eh
                        t_op.w(v, P['btop0'] + oid + 1)
                        pp = [pb_av[0], pb_av[1]][eh]
                        base = (P['scof'] // 128 + st) * E + eh * 512
                        xd = x_tm[:, base:base + 512]
                        v_res.inc(v.tensor_tensor(xd, xd, pp[:, 0:512], ALU.add))
                        v.drain()

            def emit_outproj(te, P):
                s_ob.w(te, 1)
                v_as.w(te, P['vas_mark'])
                v_af.w(te, P['vaf_mark'])
                for st in range(4):
                    for eh in range(2):
                        oid = 2 * st + eh
                        if oid >= 2:
                            v_res.w(te, P['bvres_r'] + oid - 1)
                        pp = [pb_av[0], pb_av[1]][eh]
                        ops = [(attn_hi, ow_hi, False), (attn_hi, ow_lo, False),
                               (attn_lo, ow_hi, p != 7)]
                        first = True
                        last = None
                        for (a, o, stop) in ops:
                            last = te.matmul(pp[:, 0:512],
                                             a[:, st * 128:(st + 1) * 128],
                                             o[:, eh * 512:(eh + 1) * 512],
                                             start=first, stop=stop)
                            first = False
                        if p == 7:
                            te.matmul(pp[:, 0:512], onesb[0:1, :],
                                      obhi[0:1, eh * 512:(eh + 1) * 512],
                                      start=False, stop=False)
                            last = te.matmul(pp[:, 0:512], onesb[0:1, :],
                                             oblo[0:1, eh * 512:(eh + 1) * 512],
                                             start=False, stop=True)
                        t_op.inc(last)

            def emit_qhead_te(scof_n, gcb_mark):
                hm = dict(btq=t_q.n, bvqe=v_qe.n, btct=t_ct.n, bvcd=v_cd.n,
                          scof=scof_n)

                @blk.tensor
                def _(te):
                    g_cb.w(te, gcb_mark)
                    v_ls.w(te)
                    v_aw.w(te, bvaw + 1)
                    v_he.w(te, hdone)
                    last = None
                    for (tp, hp) in (((0, 0), 0), ((64, 64), 64)):
                        first = True
                        for (a, hh) in ((aw_hi, h_hi), (aw_hi, h_lo), (aw_lo, h_hi)):
                            last = te.matmul(
                                pb_q[hp:hp + 64, 0:512],
                                a[hp:hp + 64, :],
                                hh[hp:hp + 64, hofs + scof_n:hofs + scof_n + 512],
                                start=first, stop=(a is aw_lo), tile_position=tp)
                            first = False
                    t_q.inc(last)
                return hm

            def emit_qhead_dve(hm, bts_gate):
                qo = ((hm['scof'] // 512) % 2) * 512

                @blk.vector
                def _(v):
                    t_q.w(v, hm['btq'] + 1)
                    t_s.w(v, bts_gate)
                    t_ct.w(v)
                    v.tensor_copy(q_hi[:, qo:qo + 512], pb_q[:, 0:512])
                    v.drain()
                    v.tensor_tensor(q_lo[:, qo:qo + 512], pb_q[:, 0:512],
                                    q_hi[:, qo:qo + 512], ALU.subtract)
                    v.reduce_max(mq_t[:], pb_q[:, 0:512], axis=AX.X,
                                 apply_absolute_value=True)
                    v_qe.inc(v.drain())

                @blk.gpsimd
                def _(gp):
                    v_qe.w(gp, hm['bvqe'] + 1)
                    v_cd.w(gp, hm['bvcd'])
                    t_ct.inc(gp.dma_start(out=mq_row[0:1, 0:128],
                                          in_=mq_t[:, 0:1]), 16)

                @blk.vector
                def _(v):
                    t_ct.w(v, hm['btct'] + 16)
                    g_mh.w(v)
                    t_cb.w(v)
                    v.reduce_max(csc[0:1, 0:1], mq_row[0:1, 0:64], axis=AX.X)
                    v.reduce_max(csc[0:1, 1:2], mq_row[0:1, 64:128], axis=AX.X)
                    v.drain()
                    v.tensor_tensor(csc[0:1, 2:3], csc[0:1, 0:1],
                                    mkrow[0:1, 2 * p:2 * p + 1], ALU.mult)
                    v.tensor_tensor(csc[0:1, 3:4], csc[0:1, 1:2],
                                    mkrow[0:1, 2 * p + 1:2 * p + 2], ALU.mult)
                    v.drain()
                    v.tensor_single_scalar(csc[0:1, 4:5], csc[0:1, 2:3], -1.0, ALU.mult)
                    v.tensor_single_scalar(csc[0:1, 5:6], csc[0:1, 3:4], -1.0, ALU.mult)
                    v.drain()
                    v.tensor_single_scalar(crow[0:1, 0:128], ones_row[0:1, :],
                                           csc[0:1, 4:5], ALU.mult)
                    v.tensor_single_scalar(crow[0:1, 128:256], ones_row[0:1, :],
                                           csc[0:1, 5:6], ALU.mult)
                    v_cd.inc(v.drain())

            pend = None
            curhead = emit_qhead_te(0, g_cb.n)
            emit_qhead_dve(curhead, t_s.n)
            for sc_i in range(4):
                scof = sc_i * 512
                scpar = sc_i % 2
                qo = scpar * 512
                hm = curhead
                bts, bae, btav, bvw = t_s.n, a_e.n, t_av.n, v_w.n
                bvnm, btat, bvaf = v_nm.n, t_at.n, v_af.n
                bgcb = g_cb.n

                @blk.tensor
                def _(te):
                    def emit_av(rc):
                        for hidx in range(2):
                            a_e.w(te, bae + 2 * rc + hidx + 1)
                            if rc == 0:
                                v_af.w(te, bvaf)
                            wofs = (2 * (rc % 2) + hidx) * 512
                            vb = [vA, vB][hidx]
                            o = rc * 65
                            last = te.matmul(
                                pb_av[2 + hidx][0:65, 0:512],
                                vb[:, o:o + 65],
                                w32[:, wofs:wofs + 512],
                                start=(rc == 0), stop=(rc == NRC - 1))
                        t_av.inc(last)

                    v_ve.w(te, bvve + NRC)
                    v_qe.w(te, hm['bvqe'] + 1)
                    a_e.w(te, bae)
                    v_ls.w(te)
                    v_mh.w(te)
                    for rc in range(NRC):
                        for hidx in range(2):
                            if rc >= 1:
                                a_e.w(te, bae + 2 * (rc - 1) + hidx + 1)
                            first = True
                            last = None
                            for (kop, qop) in ((h_hi, q_hi), (h_hi, q_lo), (h_lo, q_hi)):
                                last = te.matmul(
                                    pb_s[hidx][:, 0:512],
                                    kop[hidx * 64:hidx * 64 + 64,
                                        hofs + rc * 128:hofs + (rc + 1) * 128],
                                    qop[hidx * 64:hidx * 64 + 64, qo:qo + 512],
                                    start=first, stop=(kop is h_lo),
                                    tile_position=(hidx * 64, 0))
                                first = False
                            t_s.inc(last)
                        if rc == 0:
                            v_cd.w(te, hm['bvcd'] + 1)
                            g_cb.w(te, bgcb)
                            t_cb.inc(te.transpose(pb_o[:, 0:1], crow[0:1, 0:128],
                                                  id32[0:1, 0:1]))
                            t_cb.inc(te.transpose(pb_q[:, 0:1], crow[0:1, 128:256],
                                                  id32[0:1, 0:1]))
                        if rc == 1 and pend is not None:
                            emit_outproj(te, pend)
                        if rc >= 1:
                            emit_av(rc - 1)
                    emit_av(NRC - 1)

                @blk.vector
                def _(v):
                    t_cb.w(v)
                    a_e.w(v, bae)
                    v.tensor_copy(cbias[:, 2 * scpar:2 * scpar + 1], pb_o[:, 0:1])
                    g_cb.inc(v.tensor_copy(cbias[:, 2 * scpar + 1:2 * scpar + 2],
                                           pb_q[:, 0:1]))

                @blk.scalar
                def _(sc):
                    g_cb.w(sc, bgcb + 1)
                    for rc in range(NRC):
                        for hidx in range(2):
                            t_s.w(sc, bts + 2 * rc + hidx + 1)
                            if rc >= 2:
                                t_av.w(sc, btav + rc - 1)
                            elif btav > 0:
                                t_av.w(sc, btav + rc - 1)
                            slot = (2 * (rc % 2) + hidx) * 512
                            a_e.inc(sc.activation(
                                w32[:, slot:slot + 512], pb_s[hidx][:, 0:512], AFT.Exp,
                                bias=cbias[:, 2 * scpar + hidx:2 * scpar + hidx + 1],
                                scale=1.0 / 64))

                if sc_i < 3:
                    nexthead = emit_qhead_te(scof + 512, g_cb.n)

                if pend is not None:
                    @blk.vector
                    def _(v, P=pend):
                        emit_resid(v, P)

                @blk.vector
                def _(v):
                    t_av.w(v, btav + NRC)
                    t_at.w(v, btat)
                    for hidx in range(2):
                        v.reciprocal(rr_t[64:65, hidx * 512:(hidx + 1) * 512],
                                     pb_av[2 + hidx][64:65, 0:512])
                    v.drain()
                    for hidx in range(2):
                        v.tensor_copy(rhib[64:65, hidx * 1024:hidx * 1024 + 512],
                                      rr_t[64:65, hidx * 512:(hidx + 1) * 512])
                    v.drain()
                    for hidx in range(2):
                        v.tensor_tensor(rhib[64:65, hidx * 1024 + 512:hidx * 1024 + 1024],
                                        rr_t[64:65, hidx * 512:(hidx + 1) * 512],
                                        rhib[64:65, hidx * 1024:hidx * 1024 + 512],
                                        ALU.subtract)
                    v_nm.inc(v.drain())

                @blk.tensor
                def _(te):
                    v_nm.w(te, bvnm + 1)
                    v_res.w(te)
                    for hidx in range(2):
                        te.matmul(pb_av[hidx][:, 0:512],
                                  onesb_t[64:65, 0:128],
                                  rhib[64:65, hidx * 1024:hidx * 1024 + 512],
                                  start=True, stop=False, tile_position=(64, 0))
                        t_at.inc(te.matmul(
                            pb_av[hidx][:, 0:512],
                            onesb_t[64:65, 0:128],
                            rhib[64:65, hidx * 1024 + 512:hidx * 1024 + 1024],
                            start=False, stop=True, tile_position=(64, 0)))

                @blk.vector
                def _(v):
                    t_at.w(v, btat + 1)
                    v.tensor_copy(fbA_t[:], pb_av[0][:, 0:512])
                    v_af.inc(v.drain())
                    t_at.w(v, btat + 2)
                    v.tensor_copy(fbB_t[:], pb_av[1][:, 0:512])
                    v_af.inc(v.drain())
                    v.tensor_tensor(attn_tm[0:64, 0:512], pb_av[2][0:64, 0:512],
                                    fbA_t[0:64, 0:512], ALU.mult)
                    v_af.inc(v.drain())
                    v.tensor_tensor(attnB_s[0:64, 0:512], pb_av[3][0:64, 0:512],
                                    fbB_t[0:64, 0:512], ALU.mult)
                    v_af.inc(v.drain())

                if sc_i < 3:
                    emit_qhead_dve(nexthead, bts)
                    curhead = nexthead

                @blk.gpsimd
                def _(gp):
                    v_af.w(gp, bvaf + 4)
                    d_pk.inc(gp.dma_start(out=attn_tm[64:128, 0:512],
                                          in_=attnB_s[0:64, 0:512]), 16)

                @blk.vector
                def _(v):
                    d_pk.w(v)
                    t_op.w(v)
                    v.tensor_copy(attn_hi[:], attn_tm[:])
                    v.drain()
                    v.tensor_tensor(attn_lo[:], attn_tm[:], attn_hi[:], ALU.subtract)
                    v_as.inc(v.drain())

                pend = dict(scof=scof, btop0=t_op.n, vas_mark=v_as.n,
                            vaf_mark=v_af.n - 2, bvres_r=v_res.n)

            if pend is not None:
                @blk.vector
                def _(v, P=pend):
                    emit_resid(v, P)

                @blk.tensor
                def _(te, P=pend):
                    emit_outproj(te, P)
                pend = None

        ngroups = 1 if stage in ('h0', 'a0') else 2
        for g in range(ngroups):
            with ExitStack() as hes:
                emit_hbuild(g, hes)
            if stage == 'h0':
                break
            with ExitStack() as bes:
                bsb = lambda n, s, d: bes.enter_context(
                    nc.sbuf_tensor(n + str(g), s, d))
                T = (bsb("aw_f", [128, 64], F32), bsb("vw_f", [128, 64], F32),
                     bsb("ow_f", [128, E], F32),
                     bsb("aw_hi", [128, 64], BF16), bsb("aw_lo", [128, 64], BF16),
                     bsb("vw_hi", [128, 64], BF16), bsb("vw_lo", [128, 64], BF16),
                     bsb("ow_hi", [128, E], BF16), bsb("ow_lo", [128, E], BF16),
                     bsb("q_hi", [128, 1024], BF16), bsb("q_lo", [128, 1024], BF16),
                     bsb("w32", [128, 4 * 512], F32),
                     bsb("mq_t", [128, 1], F32), bsb("cbias", [128, 4], F32),
                     bsb("csc", [1, 8], F32), bsb("crow", [1, 256], F32),
                     bsb("attn_tm", [128, 512], F32),
                     bsb("attn_hi", [128, 512], BF16),
                     bsb("attn_lo", [128, 512], BF16),
                     bsb("rr_t", [128, 1024], F32),
                     bsb("rhib", [128, 2048], BF16),
                     bsb("fbA_t", [128, 512], F32), bsb("fbB_t", [128, 512], F32),
                     bsb("attnB_s", [128, 512], F32))
                for lp in range(4 if stage != 'a0' else 1):
                    emit_attention(g, lp, T)

    if stage in ('h0', 'a0'):
        with ExitStack() as _tes:
            _dummy = _tes.enter_context(nc.sbuf_tensor("dmy", [128, 8], F32))

            @blk.gpsimd
            def _(gp):
                if stage == 'h0':
                    v_he.w(gp)
                else:
                    v_res.w(gp)
                for t in range(2):
                    d_y.inc(gp.dma_start(out=y_h[t * 128:(t + 1) * 128, :],
                                         in_=x_tm[:, t * E:(t + 1) * E]), 16)
                gp.wait_ge(d_y.h, d_y.n)
        es.close()
        return nc

    if stage == 'attn':
        all_res_a = v_res.n

        @blk.gpsimd
        def _(gp):
            v_res.w(gp, all_res_a)
            for t in range(16):
                d_y.inc(gp.dma_start(out=y_h[t * 128:(t + 1) * 128, :],
                                     in_=x_tm[:, t * E:(t + 1) * E]), 16)
            gp.wait_ge(d_y.h, d_y.n)
        es.close()
        return nc

    with ExitStack() as fes:
        fsb = lambda n, s, d: fes.enter_context(nc.sbuf_tensor(n, s, d))
        f_ln = fsb("f_ln", [128, 2 * E], F32)
        f_xq = fsb("f_xq", [128, 2 * E], F32)
        f_t = fsb("f_t", [128, E], F32)
        f_u = fsb("f_u", [128, E], F32)
        f_v = fsb("f_v", [128, E], F32)
        f_xqf = fsb("f_xqf", [128, 2 * 8 * FSC], BF16)
        f_gu = fsb("f_gu", [128, NOC * FSC], F32)
        f_guq = fsb("f_guq", [128, NOC * FSC], BF16)
        f_wg = fsb("f_wg", [128, 3 * 8 * 128], BF16)
        f_wu = fsb("f_wu", [128, 3 * 8 * 128], BF16)
        f_wd = fsb("f_wd", [128, 2 * 32 * 128], BF16)
        f_do = fsb("f_do", [128, NEC * FSC], F32)
        f_gr = fsb("f_gr", [128, FSC], F32)
        f_si = fsb("f_si", [128, FSC], F32)
        f_fb = fsb("f_fb", [128, FSC], F32)
        f_sb = fsb("f_sb", [128, FSC], F32)
        f_mx = fsb("f_mx", [128, FSC], F32)
        f_mx2 = fsb("f_mx2", [128, FSC], F32)
        f_sc = fsb("f_sc", [128, 16], F32)
        f_st = fsb("f_st", [128, 20], F32)
        f_row = fsb("f_row", [1, 2 * FSC], F32)
        f_rhi = fsb("f_rhi", [1, FSC], BF16)
        f_rlo = fsb("f_rlo", [1, FSC], BF16)
        f_m2r = fsb("f_m2r", [1, 512], F32)
        f_rhi2 = fsb("f_rhi2", [1, FSC], BF16)
        f_rlo2 = fsb("f_rlo2", [1, FSC], BF16)
        f_mgr = fsb("f_mgr", [1, 256], F32)
        f_for = fsb("f_for", [1, 256], F32)
        f_trw = fsb("f_trw", [1, 256], F32)

        pf_g = fes.enter_context(nc.psum_tensor("pf_g", [128, FSC], F32))
        pf_u = fes.enter_context(nc.psum_tensor("pf_u", [128, FSC], F32))
        pf_d0 = fes.enter_context(nc.psum_tensor("pf_d0", [128, FSC], F32))
        pf_d1 = fes.enter_context(nc.psum_tensor("pf_d1", [128, FSC], F32))
        pf_t0 = fes.enter_context(nc.psum_tensor("pf_t0", [128, 128], F32))
        pf_t1 = fes.enter_context(nc.psum_tensor("pf_t1", [128, 128], F32))
        pf_b = fes.enter_context(nc.psum_tensor("pf_b", [128, FSC], F32))
        pf_r = fes.enter_context(nc.psum_tensor("pf_r", [128, 128], F32))

        all_res = v_res.n
        wq_done = o_wq.base()

        def wq_wait(eng):
            for i, s in enumerate(o_wq.s):
                if wq_done[i] > 0:
                    eng.wait_ge(s.h, wq_done[i])

        if stage == 'ffndbg':
            for nm, sh, dt in [("dxq", [128, 2048], F32), ("dsc", [128, 16], F32),
                               ("dst", [128, 2], F32), ("dgu", [128, 8192], F32),
                               ("dguq", [128, 8192], BF16), ("dfb", [128, 256], F32),
                               ("dsb", [128, 256], F32), ("ddo", [128, 2048], F32),
                               ("dxqf", [128, 2048], BF16), ("dln", [128, 2048], F32)]:
                dbg_aps[nm] = nc.dram_tensor(nm, sh, dt, kind="ExternalOutput").ap()

        def snapA():
            return dict(basq=a_f1.n, bvst=v_f1.n, bars=a_f2.n, bvfq=v_fq.n,
                        btxt=t_xt.n, bvxe=v_xe.n, btfr=t_fr.n, bvfr=v_fr.n)

        def A_stats_dve(fcn, fan, fac):
            @blk.vector
            def _(v):
                for st2 in range(2):
                    gst = fcn * 2 + st2
                    cs = st2 * 8
                    a_f1.w(v, fan['basq'] + st2 + 1)
                    if fac is not None:
                        a_f2.w(v, fac['bars'] + 2)
                    v.reduce_sum(f_st[:, cs + 1:cs + 2],
                                 x_tm[:, gst * E:(gst + 1) * E], axis=AX.X)
                    v.drain()
                    v.tensor_single_scalar(f_st[:, cs + 2:cs + 3],
                                           f_st[:, cs + 1:cs + 2], 1.0 / E, ALU.mult)
                    v.drain()
                    v.tensor_tensor(f_st[:, cs + 3:cs + 4], f_st[:, cs + 2:cs + 3],
                                    f_st[:, cs + 2:cs + 3], ALU.mult)
                    v.drain()
                    v_f1.inc(v.scalar_tensor_tensor(
                        f_st[:, cs + 4:cs + 5], f_st[:, cs + 0:cs + 1], 1.0 / E,
                        f_st[:, cs + 3:cs + 4], ALU.mult, ALU.subtract))
                    v.drain()

        def A_quant_dve(fcn, fan, fac):
            @blk.vector
            def _(v):
                s_g.w(v, 1)
                a_f2.w(v, fan['bars'] + 2)
                a_f1.w(v, fan['basq'] + 2)
                if fac is not None:
                    t_fr.w(v, fac['btfr'] + 2)
                    t_xt.w(v, fac['btxt'] + 16)
                for st2 in range(2):
                    gst = fcn * 2 + st2
                    cs = st2 * 8
                    c0 = st2 * 8
                    v.reciprocal(f_st[:, cs + 6:cs + 7], f_st[:, cs + 5:cs + 6])
                    v.drain()
                    v.tensor_scalar(f_ln[:, st2 * E:(st2 + 1) * E],
                                    x_tm[:, gst * E:(gst + 1) * E],
                                    f_st[:, cs + 2:cs + 3], f_st[:, cs + 6:cs + 7],
                                    ALU.subtract, ALU.mult)
                    v_f2.inc(v.drain())
                    v.reduce_max(f_sc[:, c0:c0 + 1], f_ln[:, st2 * E:(st2 + 1) * E],
                                 axis=AX.X, apply_absolute_value=True)
                    v.drain()
                    v.tensor_scalar(f_sc[:, c0 + 1:c0 + 2], f_sc[:, c0:c0 + 1],
                                    1.0, 1e-5, ALU.mult, ALU.max)
                    v.drain()
                    v.reciprocal(f_sc[:, c0 + 2:c0 + 3], f_sc[:, c0 + 1:c0 + 2])
                    v.drain()
                    v.tensor_single_scalar(f_sc[:, c0 + 3:c0 + 4],
                                           f_sc[:, c0 + 2:c0 + 3], 127.0, ALU.mult)
                    v.drain()
                    v.tensor_single_scalar(f_t[:], f_ln[:, st2 * E:(st2 + 1) * E],
                                           f_sc[:, c0 + 3:c0 + 4], ALU.mult)
                    v.drain()
                    v.tensor_scalar(f_u[:], f_t[:], M23, -M23, ALU.add, ALU.add)
                    v.tensor_scalar(f_v[:], f_t[:], M52, -M52, ALU.add, ALU.add)
                    v.drain()
                    v.scalar_tensor_tensor(f_xq[:, st2 * E:(st2 + 1) * E],
                                           f_u[:], h_t[:, 0:1], f_v[:],
                                           ALU.mult, ALU.add)
                    v_fq.inc(v.drain())

        def A_xqT_te(fcn, fan):
            @blk.tensor
            def _(te):
                v_fq.w(te, fan['bvfq'] + 2)
                t_fg.w(te)
                v_dt.w(te)
                for st2 in range(2):
                    for ib in range(8):
                        tid = st2 * 8 + ib
                        if tid >= 2:
                            v_xe.w(te, fan['bvxe'] + tid - 1)
                        t_xt.inc(te.transpose(
                            [pf_t0, pf_t1][tid % 2][:, 0:128],
                            f_xq[:, st2 * E + ib * 128:st2 * E + (ib + 1) * 128],
                            id32[:]))

        def A_xqT_dve(fcn, fan):
            xp = (fcn % 2) * 8 * FSC
            @blk.vector
            def _(v):
                for st2 in range(2):
                    for ib in range(8):
                        tid = st2 * 8 + ib
                        t_xt.w(v, fan['btxt'] + tid + 1)
                        v_xe.inc(v.tensor_copy(
                            f_xqf[:, xp + ib * FSC + st2 * 128:
                                  xp + ib * FSC + (st2 + 1) * 128],
                            [pf_t0, pf_t1][tid % 2][:, 0:128]))

        def A_frT_te(fcn, fan):
            @blk.tensor
            def _(te):
                for st2 in range(2):
                    tid = 16 + st2
                    v_xe.w(te, fan['bvxe'] + tid - 1)
                    t_fr.inc(te.transpose(
                        [pf_t0, pf_t1][tid % 2][0:1, 0:128],
                        f_sc[:, st2 * 8 + 1:st2 * 8 + 2], id32[:]))

        def A_fr_dve(fcn, fan, btbc_cur):
            mp = (fcn % 2) * 256
            @blk.vector
            def _(v):
                if btbc_cur is not None:
                    t_bc.w(v, btbc_cur + 1)
                for st2 in range(2):
                    tid = 16 + st2
                    t_fr.w(v, fan['btfr'] + st2 + 1)
                    v_fr.inc(v.tensor_copy(
                        f_m2r[0:1, mp + st2 * 128:mp + (st2 + 1) * 128],
                        [pf_t0, pf_t1][tid % 2][0:1, 0:128]))
                v.drain()
                v.tensor_scalar(f_row[0:1, 0:FSC], f_m2r[0:1, mp:mp + FSC],
                                wbar[0:1, 0:1], 1.0 / 127, ALU.mult, ALU.mult)
                v.drain()
                v.tensor_copy(f_rhi2[:], f_row[0:1, 0:FSC])
                v.drain()
                v_fr.inc(v.tensor_tensor(f_rlo2[:], f_row[0:1, 0:FSC], f_rhi2[:],
                                         ALU.subtract))

        def emit_gate_prefetch(marks):
            pre = dict(bdfw=d_fw.base())

            @blk.sync
            def _(sy):
                wq_wait(sy)
                v_res.w(sy, all_res)
                for oc in range(3):
                    if marks is not None:
                        t_fg.w(sy, marks[0] + NOC)
                    buf = (oc % 3) * 1024
                    d_fw.inc(sy.dma_start(
                        out=f_wg[:, buf:buf + 1024],
                        in_=wgq_h[:, oc * 1024:(oc + 1) * 1024]), oc)
                    d_fw.inc(sy.dma_start(
                        out=f_wu[:, buf:buf + 1024],
                        in_=wuq_h[:, oc * 1024:(oc + 1) * 1024]), oc)
            return pre

        def emit_down_prefetch(marks):
            pre = dict(bdfd=d_fd.base())

            @blk.sync
            def _(sy):
                for ec in range(2):
                    if marks is not None:
                        t_dn.w(sy, marks[1] + 7 + ec)
                    d_fd.inc(sy.dma_start(
                        out=f_wd[:, (ec % 2) * 4096:(ec % 2 + 1) * 4096],
                        in_=wdq_h[:, ec * 4096:(ec + 1) * 4096]), ec)
            return pre

        fC = snapA()
        wpre_g = emit_gate_prefetch(None)
        wpre_d = emit_down_prefetch(None)

        @blk.scalar
        def _(sc):
            v_res.w(sc, all_res)
            wq_wait(sc)
            for st2 in range(2):
                a_f1.inc(sc.activation(
                    f_ln[:, st2 * E:(st2 + 1) * E],
                    x_tm[:, st2 * E:(st2 + 1) * E], AFT.Square,
                    accum_out=f_st[:, st2 * 8:st2 * 8 + 1]))
                sc.drain()

        A_stats_dve(0, fC, None)

        @blk.scalar
        def _(sc):
            for st2 in range(2):
                cs = st2 * 8
                v_f1.w(sc, fC['bvst'] + st2 + 1)
                a_f2.inc(sc.activation(
                    f_st[:, cs + 5:cs + 6], f_st[:, cs + 4:cs + 5], AFT.Sqrt,
                    bias=eps_t[:, 0:1], scale=1.0))
                sc.drain()

        A_quant_dve(0, fC, None)
        A_xqT_te(0, fC)
        A_xqT_dve(0, fC)
        A_frT_te(0, fC)
        A_fr_dve(0, fC, None)

        for fc in range(NFC if stage != 'ffndbg' else 1):
            btbc, bvbc = t_bc.n, v_bc.n
            btfg, bvgr, basi, bvgu = t_fg.n, v_gr.n, a_si.n, v_gu.n
            btmx, bvmx = t_mx.n, v_mx.n
            btsr, bvsr, btsb, bvsb = t_sr.n, v_sr.n, t_sb.n, v_sb2.n
            btfo, bvfo = t_fo.n, v_fo.n
            bvgq, btdn, bvde = v_gq.n, t_dn.n, v_de.n
            bdfw, bdfd = wpre_g['bdfw'], wpre_d['bdfd']
            btdt, bvdt = t_dt.n, v_dt.n
            prev_dt = v_dt.n
            xp = (fc % 2) * 8 * FSC
            mp = (fc % 2) * 256
            last_fc = (fc == NFC - 1) or stage == 'ffndbg'
            fN = None if last_fc else snapA()

            @blk.tensor
            def _(te):
                v_fr.w(te, fC['bvfr'] + 3)
                v_sb2.w(te)
                if fc >= 1:
                    v_bc.w(te, bvbc)
                te.matmul(pf_b[:, 0:FSC], onesb[0:1, :], f_rhi2[0:1, :],
                          start=True, stop=False)
                t_bc.inc(te.matmul(pf_b[:, 0:FSC], onesb[0:1, :], f_rlo2[0:1, :],
                                   start=False, stop=True))

            @blk.vector
            def _(v):
                t_bc.w(v, btbc + 1)
                v_bc.inc(v.tensor_copy(f_fb[:], pf_b[:, 0:FSC]))
                t_mx.w(v, btmx)
                v.memset(f_mx[:], 0.0)
                v.memset(f_mx2[:], 0.0)
                v_bc.inc(v.drain())

            @blk.sync
            def _(sy):
                for oc in range(3, NOC):
                    t_fg.w(sy, btfg + oc - 2)
                    buf = (oc % 3) * 1024
                    d_fw.inc(sy.dma_start(
                        out=f_wg[:, buf:buf + 1024],
                        in_=wgq_h[:, oc * 1024:(oc + 1) * 1024]), oc)
                    d_fw.inc(sy.dma_start(
                        out=f_wu[:, buf:buf + 1024],
                        in_=wuq_h[:, oc * 1024:(oc + 1) * 1024]), oc)

            if not last_fc:
                wpre_g = emit_gate_prefetch((btfg, btdn))

            @blk.scalar
            def _(sc):
                if fN is not None:
                    v_res.w(sc, all_res)
                    v_f1.w(sc, fC['bvst'] + 2)
                    v_fq.w(sc, fC['bvfq'] + 2)
                    for st2 in range(2):
                        gst = (fc + 1) * 2 + st2
                        a_f1.inc(sc.activation(
                            f_ln[:, st2 * E:(st2 + 1) * E],
                            x_tm[:, gst * E:(gst + 1) * E], AFT.Square,
                            accum_out=f_st[:, st2 * 8:st2 * 8 + 1]))
                        sc.drain()
                for oc in range(NOC):
                    v_gr.w(sc, bvgr + oc + 1)
                    if oc >= 1:
                        v_gu.w(sc, bvgu + oc)
                    a_si.inc(sc.activation(f_si[:], f_gr[:], AFT.Silu))
                    if oc == 24 and fN is not None:
                        for st2 in range(2):
                            cs = st2 * 8
                            v_f1.w(sc, fN['bvst'] + st2 + 1)
                            a_f2.inc(sc.activation(
                                f_st[:, cs + 5:cs + 6], f_st[:, cs + 4:cs + 5],
                                AFT.Sqrt, bias=eps_t[:, 0:1], scale=1.0))
                            sc.drain()

            @blk.tensor
            def _(te):
                v_xe.w(te, fC['bvxe'] + 16)
                if fc >= 1:
                    v_gu.w(te, bvgu)
                for oc in range(NOC):
                    d_fw.w(te, oc, bdfw, per=32)
                    if oc >= 2:
                        v_gr.w(te, bvgr + oc - 1)
                        v_gu.w(te, bvgu + oc - 1)
                    buf = (oc % 3) * 1024
                    pg = [pf_g, pf_d0][oc % 2]
                    pu = [pf_u, pf_d1][oc % 2]
                    for ic in range(8):
                        te.matmul(pg[:, 0:FSC],
                                  f_wg[:, buf + ic * 128:buf + (ic + 1) * 128],
                                  f_xqf[:, xp + ic * FSC:xp + (ic + 1) * FSC],
                                  start=(ic == 0), stop=(ic == 7))
                    last = None
                    for ic in range(8):
                        last = te.matmul(pu[:, 0:FSC],
                                         f_wu[:, buf + ic * 128:buf + (ic + 1) * 128],
                                         f_xqf[:, xp + ic * FSC:xp + (ic + 1) * FSC],
                                         start=(ic == 0), stop=(ic == 7))
                    t_fg.inc(last)

            def fold_iter(v, oc):
                if oc < NOC:
                    t_fg.w(v, btfg + oc + 1)
                    v_bc.w(v, bvbc + 2)
                    if oc >= 1:
                        a_si.w(v, basi + oc)
                    pg = [pf_g, pf_d0][oc % 2]
                    v_gr.inc(v.tensor_tensor(f_gr[:], pg[:, 0:FSC], f_fb[:], ALU.mult))
                    v.drain()
                if oc >= 1:
                    a_si.w(v, basi + oc)
                    pu = [pf_u, pf_d1][(oc - 1) % 2]
                    v.tensor_tensor(f_gu[:, (oc - 1) * FSC:oc * FSC], f_si[:],
                                    pu[:, 0:FSC], ALU.mult)
                    v.drain()
                    v.tensor_tensor(f_mx[:], f_mx[:],
                                    f_gu[:, (oc - 1) * FSC:oc * FSC], ALU.max)
                    v.scalar_tensor_tensor(f_mx2[:],
                                           f_gu[:, (oc - 1) * FSC:oc * FSC],
                                           -1.0, f_mx2[:], ALU.mult, ALU.max)
                    v_gu.inc(v.drain())

            @blk.vector
            def _(v):
                for oc in range(0, 17):
                    fold_iter(v, oc)

            if fN is not None:
                A_stats_dve(fc + 1, fN, fC)

            @blk.vector
            def _(v):
                for oc in range(17, NOC + 1):
                    fold_iter(v, oc)
                v.tensor_tensor(f_mx[:], f_mx[:], f_mx2[:], ALU.max)
                v_gu.inc(v.drain())

            if fN is not None:
                A_quant_dve(fc + 1, fN, fC)

            @blk.tensor
            def _(te):
                v_gu.w(te, bvgu + NOC + 1)
                v_fr.w(te, fC['bvfr'] + 3)
                v_dt.w(te, prev_dt)
                for st2 in range(2):
                    t_mx.inc(te.transpose([pf_t0, pf_t1][st2][:, 0:128],
                                          f_mx[:, st2 * 128:(st2 + 1) * 128], id32[:]))

            @blk.vector
            def _(v):
                for st2 in range(2):
                    t_mx.w(v, btmx + st2 + 1)
                    c0 = st2 * 8
                    v.reduce_max(f_sc[:, c0 + 6:c0 + 7], [pf_t0, pf_t1][st2][:, 0:128],
                                 axis=AX.X)
                v_mx.inc(v.drain())

            @blk.tensor
            def _(te):
                v_mx.w(te, bvmx + 1)
                if fc >= 1:
                    v_fo.w(te, bvfo)
                for st2 in range(2):
                    if st2 >= 1:
                        v_sr.w(te, bvsr + 1)
                    t_sr.inc(te.transpose(pf_r[0:1, 0:128],
                                          f_sc[:, st2 * 8 + 6:st2 * 8 + 7], id32[:]))

            @blk.vector
            def _(v):
                v_dt.w(v, prev_dt)
                for st2 in range(2):
                    t_sr.w(v, btsr + st2 + 1)
                    v_sr.inc(v.tensor_copy(f_mgr[0:1, st2 * 128:(st2 + 1) * 128],
                                           pf_r[0:1, 0:128]))
                v.drain()
                v.tensor_tensor(f_trw[0:1, 0:FSC], f_mgr[0:1, 0:FSC],
                                f_m2r[0:1, mp:mp + FSC], ALU.mult)
                v.drain()
                v.tensor_scalar(f_trw[0:1, 0:FSC], f_trw[0:1, 0:FSC],
                                wbar[0:1, 1:2], 1.0 / 127, ALU.mult, ALU.mult)
                v.drain()
                v.tensor_scalar(f_trw[0:1, 0:FSC], f_trw[0:1, 0:FSC],
                                1.0, 1e-5, ALU.mult, ALU.max)
                v.drain()
                v.tensor_scalar(f_for[0:1, 0:FSC], f_trw[0:1, 0:FSC],
                                wbar[0:1, 2:3], 1.0 / 127, ALU.mult, ALU.mult)
                v.drain()
                v.reciprocal(f_trw[0:1, 0:FSC], f_trw[0:1, 0:FSC])
                v.drain()
                v.tensor_tensor(f_row[0:1, FSC:2 * FSC], f_m2r[0:1, mp:mp + FSC],
                                f_trw[0:1, 0:FSC], ALU.mult)
                v.drain()
                v.tensor_single_scalar(f_row[0:1, FSC:2 * FSC],
                                       f_row[0:1, FSC:2 * FSC],
                                       wbar[0:1, 1:2], ALU.mult)
                v.drain()
                v.tensor_copy(f_rhi[:], f_row[0:1, FSC:2 * FSC])
                v.drain()
                v_sr.inc(v.tensor_tensor(f_rlo[:], f_row[0:1, FSC:2 * FSC], f_rhi[:],
                                         ALU.subtract))

            @blk.tensor
            def _(te):
                v_sr.w(te, bvsr + 3)
                for st2 in range(2):
                    if st2 >= 1:
                        v_fo.w(te, bvfo + 1)
                    t_fo.inc(te.transpose(pf_r[:, 0:1],
                                          f_for[0:1, st2 * 128:(st2 + 1) * 128],
                                          id32[0:1, 0:1]))

            @blk.vector
            def _(v):
                for st2 in range(2):
                    t_fo.w(v, btfo + st2 + 1)
                    v_fo.inc(v.tensor_copy(f_st[:, 18 + st2:19 + st2],
                                           pf_r[:, 0:1]))

            @blk.tensor
            def _(te):
                v_sr.w(te, bvsr + 3)
                v_bc.w(te)
                te.matmul(pf_b[:, 0:FSC], onesb[0:1, :], f_rhi[0:1, :],
                          start=True, stop=False)
                t_sb.inc(te.matmul(pf_b[:, 0:FSC], onesb[0:1, :], f_rlo[0:1, :],
                                   start=False, stop=True))

            @blk.vector
            def _(v):
                t_sb.w(v, btsb + 1)
                v_sb2.inc(v.tensor_copy(f_sb[:], pf_b[:, 0:FSC]))

            @blk.vector
            def _(v):
                v_sb2.w(v, bvsb + 1)
                for oc in range(NOC):
                    if fc >= 1 and oc == 0:
                        t_dn.w(v, btdn)
                    v.tensor_tensor(f_gr[:], f_gu[:, oc * FSC:(oc + 1) * FSC],
                                    f_sb[:], ALU.mult)
                    v.drain()
                    v.tensor_scalar(f_si[:], f_gr[:], M23, -M23, ALU.add, ALU.add)
                    v.tensor_scalar(f_fb[:], f_gr[:], M17, c5217[:, 0:1],
                                    ALU.add, ALU.add)
                    v.drain()
                    v.tensor_single_scalar(f_fb[:], f_fb[:], -M52, ALU.add)
                    v.drain()
                    v_gq.inc(v.scalar_tensor_tensor(
                        f_guq[:, oc * FSC:(oc + 1) * FSC], f_si[:], h_t[:, 0:1],
                        f_fb[:], ALU.mult, ALU.add))
                    v.drain()

            @blk.sync
            def _(sy):
                for ec in range(2, NEC):
                    t_dn.w(sy, btdn + ec - 1)
                    d_fd.inc(sy.dma_start(
                        out=f_wd[:, (ec % 2) * 4096:(ec % 2 + 1) * 4096],
                        in_=wdq_h[:, ec * 4096:(ec + 1) * 4096]), ec)

            @blk.tensor
            def _(te):
                v_gq.w(te, bvgq + NOC)
                for ec in range(NEC):
                    d_fd.w(te, ec, bdfd)
                    if ec >= 2:
                        v_de.w(te, bvde + ec - 1)
                    pd = [pf_d0, pf_d1][ec % 2]
                    last = None
                    for ic in range(NOC):
                        last = te.matmul(
                            pd[:, 0:FSC],
                            f_wd[:, (ec % 2) * 4096 + ic * 128:
                                 (ec % 2) * 4096 + (ic + 1) * 128],
                            f_guq[:, ic * FSC:(ic + 1) * FSC],
                            start=(ic == 0), stop=(ic == NOC - 1))
                    t_dn.inc(last)

            if not last_fc:
                wpre_d = emit_down_prefetch((btfg, btdn))

            @blk.vector
            def _(v):
                for ec in range(NEC):
                    t_dn.w(v, btdn + ec + 1)
                    pd = [pf_d0, pf_d1][ec % 2]
                    v_de.inc(v.tensor_copy(f_do[:, ec * FSC:(ec + 1) * FSC],
                                           pd[:, 0:FSC]))

            @blk.tensor
            def _(te):
                v_de.w(te, bvde + NEC)
                v_mx.w(te, bvmx + 1)
                for ec in range(NEC):
                    for st2 in range(2):
                        tid = ec * 2 + st2
                        if tid >= 2:
                            v_dt.w(te, bvdt + tid - 1)
                        t_dt.inc(te.transpose(
                            [pf_t0, pf_t1][tid % 2][:, 0:128],
                            f_do[:, ec * FSC + st2 * 128:ec * FSC + (st2 + 1) * 128],
                            id32[:]))

            @blk.vector
            def _(v):
                v_fo.w(v, bvfo + 2)
                for ec in range(NEC):
                    for st2 in range(2):
                        tid = ec * 2 + st2
                        t_dt.w(v, btdt + tid + 1)
                        gst = fc * 2 + st2
                        xd = x_tm[:, gst * E + ec * 128:gst * E + (ec + 1) * 128]
                        v_dt.inc(v.scalar_tensor_tensor(
                            xd, [pf_t0, pf_t1][tid % 2][:, 0:128],
                            f_st[:, 18 + st2:19 + st2], xd, ALU.mult, ALU.add))

            @blk.gpsimd
            def _(gp):
                v_dt.w(gp, bvdt + 16)
                for st2 in range(2):
                    gst = fc * 2 + st2
                    d_y.inc(gp.dma_start(out=y_h[gst * 128:(gst + 1) * 128, :],
                                         in_=x_tm[:, gst * E:(gst + 1) * E]), 16)

            if fN is not None:
                A_xqT_te(fc + 1, fN)
                A_xqT_dve(fc + 1, fN)
                A_frT_te(fc + 1, fN)
                A_fr_dve(fc + 1, fN, btbc)
                fC = fN

        if stage == 'ffndbg':
            @blk.gpsimd
            def _(gp):
                gp.wait_ge(d_y.h, d_y.n)
                for (nm, buf, w) in [("dxq", f_xq, 2048),
                                     ("dst", f_st[:, 18:20], 2), ("dgu", f_gu, 8192),
                                     ("dguq", f_guq, 8192), ("dfb", f_fb, 256),
                                     ("dsb", f_sb, 256), ("ddo", f_do, 2048),
                                     ("dxqf", f_xqf, 2048), ("dln", f_ln, 2048)]:
                    src_ap = buf if not hasattr(buf, "shape") or not hasattr(buf, "ap") else buf
                    try:
                        src_ap = buf[:, 0:w]
                    except Exception:
                        src_ap = buf
                    d_y.inc(gp.dma_start(out=dbg_aps[nm], in_=src_ap), 16)
                gp.wait_ge(d_y.h, d_y.n)

        @blk.gpsimd
        def _(gp):
            gp.wait_ge(d_y.h, d_y.n)

        if dbg:
            @blk.gpsimd
            def _(gp):
                gp.dma_start(out=dbg_h[0:128, 0:4096], in_=x_tm[:, 0:4096]).then_inc(d_y.h, 16)
                gp.wait_ge(d_y.h, d_y.n + 16)

    es.close()
    return nc


def kernel(L0_attn_x, L0_attn_proj_weight, L0_attn_attn_weight,
           L0_attn_value_weight, L0_attn_out_w, L0_attn_out_b,
           L0_ffn_w_gate, L0_ffn_w_up, L0_ffn_w_down):
    import ml_dtypes
    x = np.asarray(L0_attn_x, np.float32)
    pw = np.ascontiguousarray(np.asarray(L0_attn_proj_weight, np.float32)
                              .reshape(E, E))
    aw = np.ascontiguousarray(np.asarray(L0_attn_attn_weight, np.float32)
                              .reshape(8 * 128, 64))
    vw = np.ascontiguousarray(np.asarray(L0_attn_value_weight, np.float32)
                              .reshape(8 * 128, 64))
    owT = np.ascontiguousarray(np.asarray(L0_attn_out_w, np.float32).T)
    ob = np.ascontiguousarray(np.asarray(L0_attn_out_b, np.float32)
                              .reshape(1, E))
    def tile_oc(wT, nb):
        K, O = wT.shape
        return np.ascontiguousarray(
            wT.reshape(K // 128, 128, O // 128, 128)
            .transpose(1, 2, 0, 3).reshape(128, (K // 128) * O))
    wgT = tile_oc(np.asarray(L0_ffn_w_gate, np.float32).T, 8)
    wuT = tile_oc(np.asarray(L0_ffn_w_up, np.float32).T, 8)
    wdT = tile_oc(np.asarray(L0_ffn_w_down, np.float32).T, 32)
    id32 = np.eye(128, dtype=np.float32)
    idbf = np.eye(128, dtype=ml_dtypes.bfloat16)
    onesb = np.ones((1, 128), dtype=ml_dtypes.bfloat16)

    nc = build_program()
    in_maps = []
    for c in range(8):
        b, sh = c // 2, c % 2
        xr = np.ascontiguousarray(np.roll(x[b], -2048 * sh, axis=0))
        in_maps.append(dict(x_full=xr, pw=pw, awp=aw, vwp=vw, owT=owT, ob=ob,
                            wgT=wgT, wuT=wuT, wdT=wdT, id32=id32, idbf=idbf,
                            onesb=onesb))
    res = run_bass_kernel_spmd(nc, in_maps, core_ids=list(range(8)))
    y = np.empty((4, S, E), np.float32)
    for c in range(8):
        b, sh = c // 2, c % 2
        y[b, 2048 * sh:2048 * (sh + 1), :] = res.results[c]["y"]
    return y


# version 17 (final)
# speedup vs baseline: 1.0982x; 1.0277x over previous
import numpy as np
import concourse.bass as bass
import concourse.mybir as mybir
from concourse.bass_utils import run_bass_kernel_spmd
from contextlib import ExitStack

F32 = mybir.dt.float32
BF16 = mybir.dt.bfloat16
ALU = mybir.AluOpType
AFT = mybir.ActivationFunctionType
AX = mybir.AxisListType

M23 = 1.5 * 2.0**23
M52 = 1.5 * 2.0**52
M17 = 1.5 * 2.0**17

S, E = 4096, 1024
SEQ = 2048
INNER = 4096
NRC = 32
FSC = 256
NFC = SEQ // FSC
NOC = INNER // 128
NEC = E // 128


class Sem:
    __slots__ = ("h", "n")

    def __init__(self, es, nc, name):
        self.h = es.enter_context(nc.semaphore(name))
        self.n = 0

    def inc(self, instr, k=1):
        instr.then_inc(self.h, k)
        self.n += k
        return self.n

    def w(self, eng, mark=None):
        m = self.n if mark is None else mark
        if m > 0:
            eng.wait_ge(self.h, m)


class DSem:

    def __init__(self, es, nc, name, k):
        self.s = [Sem(es, nc, f"{name}_{i}") for i in range(k)]
        self.k = k

    def base(self):
        return [s.n for s in self.s]

    def inc(self, instr, idx, amt=16):
        return self.s[idx % self.k].inc(instr, amt)

    def w(self, eng, idx, base, per=16):
        sl = idx % self.k
        eng.wait_ge(self.s[sl].h, base[sl] + per * (idx // self.k + 1))


def build_program(dbg=False, stage='full'):
    nc = bass.Bass("TRN2", target_bir_lowering=False)

    x_full = nc.dram_tensor("x_full", [S, E], F32, kind="ExternalInput").ap()
    pw_h = nc.dram_tensor("pw", [E, E], F32, kind="ExternalInput").ap()
    awp_h = nc.dram_tensor("awp", [8 * 128, 64], F32, kind="ExternalInput").ap()
    vwp_h = nc.dram_tensor("vwp", [8 * 128, 64], F32, kind="ExternalInput").ap()
    owT_h = nc.dram_tensor("owT", [E, E], F32, kind="ExternalInput").ap()
    ob_h = nc.dram_tensor("ob", [1, E], F32, kind="ExternalInput").ap()
    wgT_h = nc.dram_tensor("wgT", [128, 256 * 128], F32, kind="ExternalInput").ap()
    wuT_h = nc.dram_tensor("wuT", [128, 256 * 128], F32, kind="ExternalInput").ap()
    wdT_h = nc.dram_tensor("wdT", [128, 256 * 128], F32, kind="ExternalInput").ap()
    id32_h = nc.dram_tensor("id32", [128, 128], F32, kind="ExternalInput").ap()
    ones_h = nc.dram_tensor("onesb", [1, 128], BF16, kind="ExternalInput").ap()
    idbf_h = nc.dram_tensor("idbf", [128, 128], BF16, kind="ExternalInput").ap()
    y_h = nc.dram_tensor("y", [SEQ, E], F32, kind="ExternalOutput").ap()

    wgq_h = nc.dram_tensor("wgq", [128, 256 * 128], BF16, kind="Internal").ap()
    wuq_h = nc.dram_tensor("wuq", [128, 256 * 128], BF16, kind="Internal").ap()
    wdq_h = nc.dram_tensor("wdq", [128, 256 * 128], BF16, kind="Internal").ap()
    dbg_h = nc.dram_tensor("dbg", [128, 4096], F32,
                           kind="ExternalOutput").ap() if dbg else None
    dbg_aps = {}

    es = ExitStack()
    sb = lambda n, s, d: es.enter_context(nc.sbuf_tensor(n, s, d))
    sem = lambda n: Sem(es, nc, n)

    x_tm = sb("x_tm", [128, 16 * E], F32)
    id32 = sb("id32s", [128, 128], F32)
    idbf = sb("idbfs", [128, 128], BF16)
    onesb = sb("onesbs", [1, 128], BF16)
    eps_t = sb("eps_t", [128, 1], F32)
    g_t = sb("g_t", [128, 1], F32)
    h_t = sb("h_t", [128, 1], F32)
    c5217 = sb("c5217", [128, 1], F32)
    sc_t = sb("sc_t", [128, 8], F32)
    mkrow = sb("mkrow", [1, 32], F32)
    mh_row = sb("mh_row", [1, 512], F32)
    ones_row = sb("ones_row", [1, 128], F32)
    onesb_t = sb("onesb_t", [128, 128], BF16)
    mq_row = sb("mq_row", [1, 128], F32)
    wbar = sb("wbar", [1, 4], F32)
    obhi = sb("obhi", [1, E], BF16)
    oblo = sb("oblo", [1, E], BF16)

    s_dc = sem("s_dc"); s_dx = sem("s_dx"); s_g = sem("s_g"); s_ob = sem("s_ob")
    d_w = DSem(es, nc, "d_w", 2); v_wr = sem("v_wr"); t_wt = sem("t_wt"); v_ws = sem("v_ws")
    g_wb = sem("g_wb"); v_wq = sem("v_wq"); o_wq = DSem(es, nc, "o_wq", 2); s_gp = sem("s_gp")
    d_pw = DSem(es, nc, "d_pw", 2); v_pw = sem("v_pw")
    d_xa = DSem(es, nc, "d_xa", 2); a_sq = sem("a_sq"); v_st = sem("v_st"); a_rs = sem("a_rs")
    v_ln = sem("v_ln"); t_lt = sem("t_lt"); v_ls = sem("v_ls"); t_pj = sem("t_pj")
    v_he = sem("v_he"); v_mh = sem("v_mh"); t_mh = sem("t_mh"); g_mh = sem("g_mh")
    d_aw = sem("d_aw"); v_aw = sem("v_aw")
    t_v = sem("t_v"); v_ve = sem("v_ve")
    t_q = sem("t_q"); v_qe = sem("v_qe"); t_ct = sem("t_ct"); v_cd = sem("v_cd")
    g_cb = sem("g_cb")
    t_s = sem("t_s"); a_e = sem("a_e"); v_w = sem("v_w"); t_av = sem("t_av")
    t_cb = sem("t_cb"); v_as = sem("v_as"); d_pk = Sem(es, nc, "d_pk")
    v_nm = sem("v_nm"); t_at = sem("t_at"); v_af = sem("v_af"); t_op = sem("t_op")
    v_res = sem("v_res")
    a_f1 = sem("a_f1"); v_f1 = sem("v_f1"); a_f2 = sem("a_f2"); v_f2 = sem("v_f2")
    v_fq = sem("v_fq"); t_xt = sem("t_xt"); v_xe = sem("v_xe")
    t_fr = sem("t_fr"); v_fr = sem("v_fr"); t_bc = sem("t_bc"); v_bc = sem("v_bc")
    d_fw = DSem(es, nc, "d_fw", 3); t_fg = sem("t_fg"); v_gr = sem("v_gr"); a_si = sem("a_si")
    v_gu = sem("v_gu"); t_mx = sem("t_mx"); v_mx = sem("v_mx")
    t_sr = sem("t_sr"); v_sr = sem("v_sr"); t_sb = sem("t_sb"); v_sb2 = sem("v_sb2")
    t_fo = sem("t_fo"); v_fo = sem("v_fo")
    v_gq = sem("v_gq"); d_fd = DSem(es, nc, "d_fd", 2); t_dn = sem("t_dn"); v_de = sem("v_de")
    t_dt = sem("t_dt"); v_dt = sem("v_dt"); d_y = sem("d_y")

    blk = es.enter_context(nc.Block())

    @blk.sync
    def _(sy):
        s_dc.inc(sy.dma_start(out=id32[:], in_=id32_h), 16)
        s_dc.inc(sy.dma_start(out=onesb[:], in_=ones_h), 16)
        s_dc.inc(sy.dma_start(out=idbf[:], in_=idbf_h), 16)
        for t in range(16):
            s_dx.inc(sy.dma_start(out=x_tm[:, t * E:(t + 1) * E],
                                  in_=x_full[t * 128:(t + 1) * 128, :]), 16)

    @blk.vector
    def _(v):
        v.memset(eps_t[:], 1e-5)
        v.memset(ones_row[:], 1.0)
        v.memset(onesb_t[:], 1.0)
        v.memset(sc_t[:, 0:1], 0.75)
        v.drain()
        v.tensor_scalar(sc_t[:, 1:2], sc_t[:, 0:1], M23, -M23, ALU.add, ALU.add)
        v.drain()
        v.tensor_sub(sc_t[:, 2:3], sc_t[:, 1:2], sc_t[:, 0:1])
        v.drain()
        v.tensor_scalar(g_t[:], sc_t[:, 2:3], 4.0, -1.0, ALU.mult, ALU.add)
        v.drain()
        v.tensor_single_scalar(h_t[:], g_t[:], 1.0, ALU.add)
        v.memset(c5217[:], M52)
        v.drain()
        s_g.inc(v.tensor_single_scalar(c5217[:], c5217[:], -M17, ALU.add))

    with ExitStack() as wes:
        wsb = lambda n, s, d: wes.enter_context(nc.sbuf_tensor(n, s, d))
        wraw = wsb("wraw", [128, 2 * 4096], F32)
        wt = wsb("wt", [128, 4096], F32)
        wn = wsb("wn", [128, 4096], F32)
        wqt = wsb("wqt", [128, 2 * 4096], BF16)
        asum = wsb("asum", [128, 33], F32)
        swbc = wsb("swbc", [128, 1], F32)
        obf = wsb("obf", [1, E], F32)
        wrow = wsb("wrow", [1, 128], F32)
        ps_wq = wes.enter_context(nc.psum_tensor("ps_wq", [128, 128], F32))

        @blk.sync
        def _(sy):
            s_dc.inc(sy.dma_start(out=obf[:], in_=ob_h), 16)

        @blk.vector
        def _(v):
            s_dc.w(v)
            v.tensor_copy(obhi[:], obf[:])
            v.drain()
            s_ob.inc(v.tensor_tensor(oblo[:], obf[:], obhi[:], ALU.subtract))

        specs = [(wgT_h, wgq_h, 8, 4096, 0), (wuT_h, wuq_h, 8, 4096, 1),
                 (wdT_h, wdq_h, 8, 4096, 2)]
        if stage == 'setup0':
            specs = []
        elif stage == 'wq1':
            specs = specs[:1]

        for si, (src, dst, nt, wdt, wi) in enumerate(specs):
            bdw, bvr = d_w.base(), v_wr.n

            @blk.sync
            def _(sy, src=src, nt=nt, wdt=wdt, bvr=bvr):
                v_wq.w(sy)
                for t in range(nt):
                    if t >= 2:
                        v_wr.w(sy, bvr + t - 1)
                    d_w.inc(sy.dma_start(
                        out=wraw[:, (t % 2) * 4096:(t % 2) * 4096 + wdt],
                        in_=src[:, t * 4096:t * 4096 + wdt]), t)

            @blk.vector
            def _(v, nt=nt, wdt=wdt, bdw=bdw):
                for t in range(nt):
                    d_w.w(v, t, bdw)
                    v_wr.inc(v.reduce_sum(
                        asum[:, t:t + 1],
                        wraw[:, (t % 2) * 4096:(t % 2) * 4096 + wdt],
                        axis=AX.X, apply_absolute_value=True))
                v.drain()
                v.reduce_sum(asum[:, 32:33], asum[:, 0:nt], axis=AX.X)
                v_wr.inc(v.drain())

            @blk.tensor
            def _(te):
                v_wr.w(te)
                s_dc.w(te)
                t_wt.inc(te.transpose(ps_wq[0:1, 0:128], asum[:, 32:33], id32[:]))

            @blk.vector
            def _(v, nt=nt, wdt=wdt, wi=wi):
                t_wt.w(v)
                g_wb.w(v)
                v.reduce_sum(sc_t[0:1, 4:5], ps_wq[0:1, 0:128], axis=AX.X)
                v.drain()
                v.tensor_scalar(wbar[0:1, wi:wi + 1], sc_t[0:1, 4:5],
                                1.0 / (nt * 128 * wdt), 1e-5, ALU.mult, ALU.max)
                v.drain()
                v_ws.inc(v.reciprocal(sc_t[0:1, 5:6], wbar[0:1, wi:wi + 1]))

            @blk.vector
            def _(v):
                v_ws.w(v)
                v_ws.inc(v.tensor_single_scalar(wrow[0:1, 0:128], ones_row[0:1, :],
                                                sc_t[0:1, 5:6], ALU.mult))

            @blk.tensor
            def _(te):
                v_ws.w(te)
                t_wt.inc(te.transpose(ps_wq[:, 0:1], wrow[0:1, 0:128],
                                      id32[0:1, 0:1]))

            @blk.vector
            def _(v):
                t_wt.w(v)
                g_wb.inc(v.tensor_copy(swbc[:], ps_wq[:, 0:1]))

            bdw2, bvq, boq = d_w.base(), v_wq.n, o_wq.base()

            @blk.sync
            def _(sy, src=src, nt=nt, wdt=wdt, bvq=bvq):
                v_wr.w(sy)
                for t in range(nt):
                    if t >= 2:
                        v_wq.w(sy, bvq + t - 1)
                    d_w.inc(sy.dma_start(
                        out=wraw[:, (t % 2) * 4096:(t % 2) * 4096 + wdt],
                        in_=src[:, t * 4096:t * 4096 + wdt]), t)

            @blk.vector
            def _(v, nt=nt, wdt=wdt, bdw2=bdw2, boq=boq):
                g_wb.w(v)
                for i, s_ in enumerate(o_wq.s):
                    if boq[i] > 0:
                        v.wait_ge(s_.h, boq[i])
                for t in range(nt):
                    d_w.w(v, t, bdw2)
                    if t >= 2:
                        o_wq.w(v, t - 2, boq)
                    buf = wraw[:, (t % 2) * 4096:(t % 2) * 4096 + wdt]
                    qb = wqt[:, (t % 2) * 4096:(t % 2) * 4096 + wdt]
                    v.tensor_single_scalar(wt[:, 0:wdt], buf, swbc[:, 0:1], ALU.mult)
                    v.drain()
                    v.tensor_single_scalar(wn[:, 0:wdt], wt[:, 0:wdt], -0.5, ALU.is_le)
                    v.drain()
                    v_wq.inc(v.scalar_tensor_tensor(
                        qb, wt[:, 0:wdt], 0.5, wn[:, 0:wdt], ALU.is_ge, ALU.subtract))
                    v.drain()

            @blk.gpsimd
            def _(g, dst=dst, nt=nt, wdt=wdt, bvq=bvq):
                for t in range(nt):
                    v_wq.w(g, bvq + t + 1)
                    o_wq.inc(g.dma_start(
                        out=dst[:, t * 4096:t * 4096 + wdt],
                        in_=wqt[:, (t % 2) * 4096:(t % 2) * 4096 + wdt]), t)

    def _early_out():
        @blk.gpsimd
        def _(gp):
            for s_ in o_wq.s:
                if s_.n > 0:
                    gp.wait_ge(s_.h, s_.n)
            for t in range(2):
                d_y.inc(gp.dma_start(out=y_h[t * 128:(t + 1) * 128, :],
                                     in_=x_tm[:, t * E:(t + 1) * E]), 16)
            gp.wait_ge(d_y.h, d_y.n)
        es.close()
        return nc

    if stage in ('wqonly', 'setup0', 'wq1'):
        return _early_out()

    with ExitStack() as aes:
        asb = lambda n, s, d: aes.enter_context(nc.sbuf_tensor(n, s, d))
        h_hi = asb("h_hi", [128, 4 * 4096], BF16)
        h_lo = asb("h_lo", [128, 4 * 4096], BF16)
        vA = asb("vA", [128, NRC * 65], F32)
        vB = asb("vB", [128, NRC * 65], F32)
        stat = asb("stat", [128, 16], F32)
        mh_t = asb("mh_t", [128, 4], F32)
        mh_r = asb("mh_r", [4, 2], F32)

        pb_q = aes.enter_context(nc.psum_tensor("pb_q", [128, 512], F32))
        pb_o = aes.enter_context(nc.psum_tensor("pb_o", [128, 512], F32))
        pb_s = [aes.enter_context(nc.psum_tensor(f"pb_s{i}", [128, 512], F32))
                for i in range(2)]
        pb_av = [aes.enter_context(nc.psum_tensor(f"pb_av{i}", [128, 512], F32))
                 for i in range(4)]

        def emit_hbuild(g, hes):
            hsb = lambda n, s, d: hes.enter_context(nc.sbuf_tensor(n + str(g), s, d))
            pw_hi = hsb("pw_hi", [128, 8 * 512], BF16)
            pw_lo = hsb("pw_lo", [128, 8 * 512], BF16)
            pw_f = hsb("pw_f", [128, 2 * 512], F32)
            xa = hsb("xa", [128, 2 * E], F32)
            xsq = hsb("xsq", [128, E], F32)
            lnt = hsb("lnt", [128, E], F32)
            ln_hi = hsb("ln_hi", [128, 4 * E], BF16)
            ln_lo = hsb("ln_lo", [128, 4 * E], BF16)
            bdpw, bvpw = d_pw.base(), v_pw.n
            bdxa, basq, bvst, bars, bvln = d_xa.base(), a_sq.n, v_st.n, a_rs.n, v_ln.n
            btlt, bvls, btpj, bvhe = t_lt.n, v_ls.n, t_pj.n, v_he.n
            prev_res = v_res.n
            prev_nm = v_nm.n
            prev_af_h = v_af.n

            @blk.sync
            def _(sy):
                v_res.w(sy, prev_res)
                for s_ in o_wq.s:
                    if s_.n > 0:
                        sy.wait_ge(s_.h, s_.n)
                v_wq.w(sy)
                for ic in range(8):
                    if ic >= 2:
                        v_pw.w(sy, bvpw + ic - 1)
                    d_pw.inc(sy.dma_start(
                        out=pw_f[:, (ic % 2) * 512:(ic % 2 + 1) * 512],
                        in_=pw_h[ic * 128:(ic + 1) * 128, 512 * g:512 * (g + 1)]), ic)
                for rc in range(NRC):
                    if rc >= 2:
                        v_ln.w(sy, bvln + rc - 1)
                    d_xa.inc(sy.dma_start(
                        out=xa[:, (rc % 2) * E:(rc % 2 + 1) * E],
                        in_=x_full[rc * 128:(rc + 1) * 128, :]), rc)

            @blk.vector
            def _(v):
                t_pj.w(v, btpj)
                for ic in range(8):
                    d_pw.w(v, ic, bdpw)
                    src = pw_f[:, (ic % 2) * 512:(ic % 2 + 1) * 512]
                    v.tensor_copy(pw_hi[:, ic * 512:(ic + 1) * 512], src)
                    v.drain()
                    v_pw.inc(v.tensor_tensor(
                        pw_lo[:, ic * 512:(ic + 1) * 512], src,
                        pw_hi[:, ic * 512:(ic + 1) * 512], ALU.subtract))

            def emit_sq(sc, rc):
                c = (rc % 2) * 8
                d_xa.w(sc, rc, bdxa)
                if rc >= 2:
                    v_st.w(sc, bvst + rc - 1)
                a_sq.inc(sc.activation(
                    xsq[:, 0:E],
                    xa[:, (rc % 2) * E:(rc % 2 + 1) * E], AFT.Square,
                    accum_out=stat[:, c + 0:c + 1]))
                sc.drain()

            def emit_sqrt(sc, rc):
                c = (rc % 2) * 8
                v_st.w(sc, bvst + rc + 1)
                if rc >= 2:
                    v_ln.w(sc, bvln + rc - 1)
                a_rs.inc(sc.activation(
                    stat[:, c + 5:c + 6], stat[:, c + 4:c + 5], AFT.Sqrt,
                    bias=eps_t[:, 0:1], scale=1.0))

            @blk.scalar
            def _(sc):
                s_g.w(sc, 1)
                v_res.w(sc, prev_res)
                for s_ in o_wq.s:
                    if s_.n > 0:
                        sc.wait_ge(s_.h, s_.n)
                v_wq.w(sc)
                for rc in range(NRC):
                    emit_sq(sc, rc)
                    if rc >= 1:
                        emit_sqrt(sc, rc - 1)
                emit_sqrt(sc, NRC - 1)

            def emit_stats2(v, rc):
                c = (rc % 2) * 8
                a_sq.w(v, basq + rc + 1)
                if rc >= 2:
                    a_rs.w(v, bars + rc - 1)
                if rc >= 2:
                    v_ln.w(v, bvln + rc - 1)
                v.reduce_sum(stat[:, c + 1:c + 2],
                             xa[:, (rc % 2) * E:(rc % 2 + 1) * E], axis=AX.X)
                v.drain()
                v.tensor_single_scalar(stat[:, c + 2:c + 3], stat[:, c + 1:c + 2],
                                       1.0 / E, ALU.mult)
                v.drain()
                v.tensor_tensor(stat[:, c + 3:c + 4], stat[:, c + 2:c + 3],
                                stat[:, c + 2:c + 3], ALU.mult)
                v.drain()
                v_st.inc(v.scalar_tensor_tensor(
                    stat[:, c + 4:c + 5], stat[:, c + 0:c + 1], 1.0 / E,
                    stat[:, c + 3:c + 4], ALU.mult, ALU.subtract))
                v.drain()

            def emit_ln(v, rc):
                c = (rc % 2) * 8
                a_rs.w(v, bars + rc + 1)
                if rc >= 1:
                    t_lt.w(v, btlt + 8 * rc)
                v.reciprocal(stat[:, c + 6:c + 7], stat[:, c + 5:c + 6])
                v.drain()
                v_ln.inc(v.tensor_scalar(
                    lnt[:], xa[:, (rc % 2) * E:(rc % 2 + 1) * E],
                    stat[:, c + 2:c + 3], stat[:, c + 6:c + 7],
                    ALU.subtract, ALU.mult))
                v.drain()



            ltp = [pb_q, pb_o, pb_s[0], pb_s[1]]

            def emit_lt(te, rc):
                v_ln.w(te, bvln + rc + 1)
                for ib in range(8):
                    tid = rc * 8 + ib
                    if tid >= 4:
                        v_ls.w(te, bvls + tid - 3)
                    t_lt.inc(te.transpose(
                        ltp[ib % 4][:, 0:128],
                        lnt[:, ib * 128:(ib + 1) * 128], id32[:]))

            def emit_pj(te, rcq):
                v_ls.w(te, bvls + 32 * rcq + 32)
                for lc in range(4):
                    pid = rcq * 4 + lc
                    pp = pb_av[lc]
                    if pid >= 4:
                        v_he.w(te, bvhe + pid - 3)
                    first = True
                    last = None
                    for (wop, rop) in ((pw_hi, ln_hi), (pw_hi, ln_lo), (pw_lo, ln_hi)):
                        for ic in range(8):
                            last = te.matmul(
                                pp[:, 0:512],
                                wop[:, ic * 512 + lc * 128:ic * 512 + (lc + 1) * 128],
                                rop[:, ic * 512:(ic + 1) * 512],
                                start=first, stop=(wop is pw_lo and ic == 7))
                            first = False
                    t_pj.inc(last)

            def emit_ls(v, rc):
                for ib in range(8):
                    tid = rc * 8 + ib
                    t_lt.w(v, btlt + tid + 1)
                    if ib == 0 and rc >= 4:
                        t_pj.w(v, btpj + 4 * (rc // 4))
                    src = ltp[ib % 4][:, 0:128]
                    dst = ib * 512 + (rc % 4) * 128
                    v.tensor_copy(ln_hi[:, dst:dst + 128], src)
                    v.drain()
                    v_ls.inc(v.tensor_tensor(
                        ln_lo[:, dst:dst + 128], src,
                        ln_hi[:, dst:dst + 128], ALU.subtract))
                    v.drain()

            def emit_he(v, rcq):
                for lc in range(4):
                    pid = rcq * 4 + lc
                    t_pj.w(v, btpj + pid + 1)
                    pp = pb_av[lc]
                    dst = lc * 4096 + rcq * 512
                    v.tensor_copy(h_hi[:, dst:dst + 512], pp[:, 0:512])
                    v.drain()
                    v_he.inc(v.tensor_tensor(
                        h_lo[:, dst:dst + 512], pp[:, 0:512],
                        h_hi[:, dst:dst + 512], ALU.subtract))
                    v.drain()

            @blk.tensor
            def _(te):
                s_dc.w(te)
                v_res.w(te, prev_res)
                v_nm.w(te, prev_nm)
                v_mh.w(te)
                g_cb.w(te)
                v_pw.w(te, bvpw + 8)
                v_ve.w(te)
                t_av.w(te)
                v_af.w(te, prev_af_h)
                for rc in range(NRC):
                    if rc >= 4 and rc % 4 == 0:
                        emit_pj(te, rc // 4 - 1)
                    emit_lt(te, rc)
                emit_pj(te, NRC // 4 - 1)

            @blk.vector
            def _(v):
                for rc in range(NRC + 3):
                    if rc < NRC:
                        emit_stats2(v, rc)
                    if 1 <= rc < NRC + 1:
                        emit_ln(v, rc - 1)
                    if 1 <= rc < NRC + 1:
                        emit_ls(v, rc - 1)
                    if rc >= 6 and (rc - 6) % 4 == 0:
                        emit_he(v, (rc - 6) // 4)

            @blk.vector
            def _(v):
                v_he.w(v, bvhe + (NRC // 4) * 4)
                for lc in range(4):
                    v.reduce_max(mh_t[:, lc:lc + 1], h_hi[:, lc * 4096:(lc + 1) * 4096],
                                 axis=AX.X, apply_absolute_value=True)
                v.drain()
                v_mh.inc(v.tensor_single_scalar(mh_t[:], mh_t[:], 1.01, ALU.mult))

            @blk.gpsimd
            def _(gp, g=g):
                v_mh.w(gp)
                if g >= 1:
                    v_mh.w(gp)
                g_mh.inc(gp.dma_start(out=mh_row[0:1, 0:512], in_=mh_t[:, 0:4]), 16)

            @blk.vector
            def _(v, g=g):
                g_mh.w(v)
                mv = mh_row.ap().rearrange("o (p l) -> o l p", l=4)
                for lc in range(4):
                    for j in range(2):
                        p = 4 * g + lc
                        v.reduce_max(
                            mkrow[0:1, 2 * p + j:2 * p + j + 1],
                            mv[0:1, lc:lc + 1, 64 * j:64 * (j + 1)], axis=AX.X)
                v_mh.inc(v.drain())

        def emit_attention(g, lp, T):
            (aw_f, vw_f, ow_f, aw_hi, aw_lo, vw_hi, vw_lo, ow_hi, ow_lo,
             q_hi, q_lo, w32, mq_t, cbias, csc, crow,
             attn_tm, attn_hi, attn_lo, rr_t, rhib, fbA_t, fbB_t, attnB_s) = T
            p = 4 * g + lp
            hofs = lp * 4096
            bdaw, bvaw = d_aw.n, v_aw.n
            btv, bvve = t_v.n, v_ve.n
            hdone = v_he.n
            prev_nm = v_nm.n
            prev_res = v_res.n
            prev_res2 = v_res.n
            prev_op = t_op.n
            prev_at = t_at.n
            prev_af = v_af.n
            prev_s = t_s.n
            prev_e = a_e.n

            @blk.sync
            def _(sy):
                t_pj.w(sy)
                v_aw.w(sy, bvaw)
                d_aw.inc(sy.dma_start(out=aw_f[:], in_=awp_h[p * 128:(p + 1) * 128, :]), 16)
                d_aw.inc(sy.dma_start(out=vw_f[:], in_=vwp_h[p * 128:(p + 1) * 128, :]), 16)
                d_aw.inc(sy.dma_start(out=ow_f[:], in_=owT_h[p * 128:(p + 1) * 128, :]), 16)

            @blk.vector
            def _(v):
                d_aw.w(v, bdaw + 48)
                t_q.w(v)
                t_v.w(v, btv)
                t_op.w(v, prev_op)
                t_av.w(v)
                v.tensor_copy(aw_hi[:], aw_f[:])
                v.tensor_copy(vw_hi[:], vw_f[:])
                v.tensor_copy(ow_hi[:], ow_f[:])
                v.drain()
                v.tensor_tensor(aw_lo[:], aw_f[:], aw_hi[:], ALU.subtract)
                v.tensor_tensor(vw_lo[:], vw_f[:], vw_hi[:], ALU.subtract)
                v.tensor_tensor(ow_lo[:], ow_f[:], ow_hi[:], ALU.subtract)
                v_aw.inc(v.drain())

            @blk.tensor
            def _(te):
                v_aw.w(te, bvaw + 1)
                v_he.w(te, hdone)
                v_af.w(te, prev_af)
                v_res.w(te, prev_res2)
                t_pj.w(te)
                for rc in range(NRC):
                    if rc >= 2:
                        v_ve.w(te, bvve + rc - 1)
                    last = None
                    for (tp, hp, pav) in (((0, 0), 0, pb_av[rc % 2]),
                                          ((64, 0), 64, pb_av[2 + rc % 2])):
                        first = True
                        for (hop, vop) in ((h_hi, vw_hi), (h_hi, vw_lo), (h_lo, vw_hi)):
                            last = te.matmul(
                                pav[:, 0:64],
                                hop[hp:hp + 64, hofs + rc * 128:hofs + (rc + 1) * 128],
                                vop[hp:hp + 64, :], start=first,
                                stop=(hop is h_lo), tile_position=tp)
                            first = False
                    t_v.inc(last)

            @blk.vector
            def _(v):
                for rc in range(NRC):
                    t_v.w(v, btv + rc + 1)
                    o = rc * 65
                    for (hp, vb) in ((0, vA), (64, vB)):
                        pav = pb_av[(0 if hp == 0 else 2) + rc % 2]
                        v.tensor_copy(vb[:, o:o + 64], pav[:, 0:64])
                        v.memset(vb[:, o + 64:o + 65], 1.0)
                    v_ve.inc(v.drain())

            def emit_resid(v, P):
                s_dx.w(v)
                for st in range(4):
                    for eh in range(2):
                        oid = 2 * st + eh
                        t_op.w(v, P['btop0'] + oid + 1)
                        pp = [pb_av[0], pb_av[1]][eh]
                        base = (P['scof'] // 128 + st) * E + eh * 512
                        xd = x_tm[:, base:base + 512]
                        v_res.inc(v.tensor_tensor(xd, xd, pp[:, 0:512], ALU.add))
                        v.drain()

            def emit_outproj(te, P):
                s_ob.w(te, 1)
                v_as.w(te, P['vas_mark'])
                v_af.w(te, P['vaf_mark'])
                for st in range(4):
                    for eh in range(2):
                        oid = 2 * st + eh
                        if oid >= 2:
                            v_res.w(te, P['bvres_r'] + oid - 1)
                        pp = [pb_av[0], pb_av[1]][eh]
                        ops = [(attn_hi, ow_hi, False), (attn_hi, ow_lo, False),
                               (attn_lo, ow_hi, p != 7)]
                        first = True
                        last = None
                        for (a, o, stop) in ops:
                            last = te.matmul(pp[:, 0:512],
                                             a[:, st * 128:(st + 1) * 128],
                                             o[:, eh * 512:(eh + 1) * 512],
                                             start=first, stop=stop)
                            first = False
                        if p == 7:
                            te.matmul(pp[:, 0:512], onesb[0:1, :],
                                      obhi[0:1, eh * 512:(eh + 1) * 512],
                                      start=False, stop=False)
                            last = te.matmul(pp[:, 0:512], onesb[0:1, :],
                                             oblo[0:1, eh * 512:(eh + 1) * 512],
                                             start=False, stop=True)
                        t_op.inc(last)

            def emit_qhead_te(scof_n, gcb_mark):
                hm = dict(btq=t_q.n, bvqe=v_qe.n, btct=t_ct.n, bvcd=v_cd.n,
                          scof=scof_n)

                @blk.tensor
                def _(te):
                    g_cb.w(te, gcb_mark)
                    v_ls.w(te)
                    v_aw.w(te, bvaw + 1)
                    v_he.w(te, hdone)
                    last = None
                    for (tp, hp) in (((0, 0), 0), ((64, 64), 64)):
                        first = True
                        for (a, hh) in ((aw_hi, h_hi), (aw_hi, h_lo), (aw_lo, h_hi)):
                            last = te.matmul(
                                pb_q[hp:hp + 64, 0:512],
                                a[hp:hp + 64, :],
                                hh[hp:hp + 64, hofs + scof_n:hofs + scof_n + 512],
                                start=first, stop=(a is aw_lo), tile_position=tp)
                            first = False
                    t_q.inc(last)
                return hm

            def emit_qhead_dve(hm, bts_gate):
                qo = ((hm['scof'] // 512) % 2) * 512

                @blk.vector
                def _(v):
                    t_q.w(v, hm['btq'] + 1)
                    t_s.w(v, bts_gate)
                    t_ct.w(v)
                    v.tensor_copy(q_hi[:, qo:qo + 512], pb_q[:, 0:512])
                    v.drain()
                    v.tensor_tensor(q_lo[:, qo:qo + 512], pb_q[:, 0:512],
                                    q_hi[:, qo:qo + 512], ALU.subtract)
                    v.reduce_max(mq_t[:], pb_q[:, 0:512], axis=AX.X,
                                 apply_absolute_value=True)
                    v_qe.inc(v.drain())

                @blk.gpsimd
                def _(gp):
                    v_qe.w(gp, hm['bvqe'] + 1)
                    v_cd.w(gp, hm['bvcd'])
                    t_ct.inc(gp.dma_start(out=mq_row[0:1, 0:128],
                                          in_=mq_t[:, 0:1]), 16)

                @blk.vector
                def _(v):
                    t_ct.w(v, hm['btct'] + 16)
                    g_mh.w(v)
                    t_cb.w(v)
                    v.reduce_max(csc[0:1, 0:1], mq_row[0:1, 0:64], axis=AX.X)
                    v.reduce_max(csc[0:1, 1:2], mq_row[0:1, 64:128], axis=AX.X)
                    v.drain()
                    v.tensor_tensor(csc[0:1, 2:3], csc[0:1, 0:1],
                                    mkrow[0:1, 2 * p:2 * p + 1], ALU.mult)
                    v.tensor_tensor(csc[0:1, 3:4], csc[0:1, 1:2],
                                    mkrow[0:1, 2 * p + 1:2 * p + 2], ALU.mult)
                    v.drain()
                    v.tensor_single_scalar(csc[0:1, 4:5], csc[0:1, 2:3], -1.0, ALU.mult)
                    v.tensor_single_scalar(csc[0:1, 5:6], csc[0:1, 3:4], -1.0, ALU.mult)
                    v.drain()
                    v.tensor_single_scalar(crow[0:1, 0:128], ones_row[0:1, :],
                                           csc[0:1, 4:5], ALU.mult)
                    v.tensor_single_scalar(crow[0:1, 128:256], ones_row[0:1, :],
                                           csc[0:1, 5:6], ALU.mult)
                    v_cd.inc(v.drain())

            pend = None
            curhead = emit_qhead_te(0, g_cb.n)
            emit_qhead_dve(curhead, t_s.n)
            for sc_i in range(4):
                scof = sc_i * 512
                scpar = sc_i % 2
                qo = scpar * 512
                hm = curhead
                bts, bae, btav, bvw = t_s.n, a_e.n, t_av.n, v_w.n
                bvnm, btat, bvaf = v_nm.n, t_at.n, v_af.n
                bgcb = g_cb.n

                @blk.tensor
                def _(te):
                    def emit_av(rc):
                        for hidx in range(2):
                            a_e.w(te, bae + 2 * rc + hidx + 1)
                            if rc == 0:
                                v_af.w(te, bvaf)
                            wofs = (2 * (rc % 2) + hidx) * 512
                            vb = [vA, vB][hidx]
                            o = rc * 65
                            last = te.matmul(
                                pb_av[2 + hidx][0:65, 0:512],
                                vb[:, o:o + 65],
                                w32[:, wofs:wofs + 512],
                                start=(rc == 0), stop=(rc == NRC - 1))
                        t_av.inc(last)

                    v_ve.w(te, bvve + NRC)
                    v_qe.w(te, hm['bvqe'] + 1)
                    a_e.w(te, bae)
                    v_ls.w(te)
                    v_mh.w(te)
                    for rc in range(NRC):
                        for hidx in range(2):
                            if rc >= 1:
                                a_e.w(te, bae + 2 * (rc - 1) + hidx + 1)
                            first = True
                            last = None
                            for (kop, qop) in ((h_hi, q_hi), (h_hi, q_lo), (h_lo, q_hi)):
                                last = te.matmul(
                                    pb_s[hidx][:, 0:512],
                                    kop[hidx * 64:hidx * 64 + 64,
                                        hofs + rc * 128:hofs + (rc + 1) * 128],
                                    qop[hidx * 64:hidx * 64 + 64, qo:qo + 512],
                                    start=first, stop=(kop is h_lo),
                                    tile_position=(hidx * 64, 0))
                                first = False
                            t_s.inc(last)
                        if rc == 0:
                            v_cd.w(te, hm['bvcd'] + 1)
                            g_cb.w(te, bgcb)
                            t_cb.inc(te.transpose(pb_o[:, 0:1], crow[0:1, 0:128],
                                                  id32[0:1, 0:1]))
                            t_cb.inc(te.transpose(pb_q[:, 0:1], crow[0:1, 128:256],
                                                  id32[0:1, 0:1]))
                        if rc == 1 and pend is not None:
                            emit_outproj(te, pend)
                        if rc >= 1:
                            emit_av(rc - 1)
                    emit_av(NRC - 1)

                @blk.vector
                def _(v):
                    t_cb.w(v)
                    a_e.w(v, bae)
                    v.tensor_copy(cbias[:, 2 * scpar:2 * scpar + 1], pb_o[:, 0:1])
                    g_cb.inc(v.tensor_copy(cbias[:, 2 * scpar + 1:2 * scpar + 2],
                                           pb_q[:, 0:1]))

                @blk.scalar
                def _(sc):
                    g_cb.w(sc, bgcb + 1)
                    for rc in range(NRC):
                        for hidx in range(2):
                            t_s.w(sc, bts + 2 * rc + hidx + 1)
                            if rc >= 2:
                                t_av.w(sc, btav + rc - 1)
                            elif btav > 0:
                                t_av.w(sc, btav + rc - 1)
                            slot = (2 * (rc % 2) + hidx) * 512
                            a_e.inc(sc.activation(
                                w32[:, slot:slot + 512], pb_s[hidx][:, 0:512], AFT.Exp,
                                bias=cbias[:, 2 * scpar + hidx:2 * scpar + hidx + 1],
                                scale=1.0 / 64))

                if sc_i < 3:
                    nexthead = emit_qhead_te(scof + 512, g_cb.n)

                if pend is not None:
                    @blk.vector
                    def _(v, P=pend):
                        emit_resid(v, P)

                @blk.vector
                def _(v):
                    t_av.w(v, btav + NRC)
                    t_at.w(v, btat)
                    for hidx in range(2):
                        v.reciprocal(rr_t[64:65, hidx * 512:(hidx + 1) * 512],
                                     pb_av[2 + hidx][64:65, 0:512])
                    v.drain()
                    for hidx in range(2):
                        v.tensor_copy(rhib[64:65, hidx * 1024:hidx * 1024 + 512],
                                      rr_t[64:65, hidx * 512:(hidx + 1) * 512])
                    v.drain()
                    for hidx in range(2):
                        v.tensor_tensor(rhib[64:65, hidx * 1024 + 512:hidx * 1024 + 1024],
                                        rr_t[64:65, hidx * 512:(hidx + 1) * 512],
                                        rhib[64:65, hidx * 1024:hidx * 1024 + 512],
                                        ALU.subtract)
                    v_nm.inc(v.drain())

                @blk.tensor
                def _(te):
                    v_nm.w(te, bvnm + 1)
                    v_res.w(te)
                    for hidx in range(2):
                        te.matmul(pb_av[hidx][:, 0:512],
                                  onesb_t[64:65, 0:128],
                                  rhib[64:65, hidx * 1024:hidx * 1024 + 512],
                                  start=True, stop=False, tile_position=(64, 0))
                        t_at.inc(te.matmul(
                            pb_av[hidx][:, 0:512],
                            onesb_t[64:65, 0:128],
                            rhib[64:65, hidx * 1024 + 512:hidx * 1024 + 1024],
                            start=False, stop=True, tile_position=(64, 0)))

                @blk.vector
                def _(v):
                    t_at.w(v, btat + 1)
                    v.tensor_copy(fbA_t[:], pb_av[0][:, 0:512])
                    v_af.inc(v.drain())
                    t_at.w(v, btat + 2)
                    v.tensor_copy(fbB_t[:], pb_av[1][:, 0:512])
                    v_af.inc(v.drain())
                    v.tensor_tensor(attn_tm[0:64, 0:512], pb_av[2][0:64, 0:512],
                                    fbA_t[0:64, 0:512], ALU.mult)
                    v_af.inc(v.drain())
                    v.tensor_tensor(attnB_s[0:64, 0:512], pb_av[3][0:64, 0:512],
                                    fbB_t[0:64, 0:512], ALU.mult)
                    v_af.inc(v.drain())

                if sc_i < 3:
                    emit_qhead_dve(nexthead, bts)
                    curhead = nexthead

                @blk.gpsimd
                def _(gp):
                    v_af.w(gp, bvaf + 4)
                    d_pk.inc(gp.dma_start(out=attn_tm[64:128, 0:512],
                                          in_=attnB_s[0:64, 0:512]), 16)

                @blk.vector
                def _(v):
                    d_pk.w(v)
                    t_op.w(v)
                    v.tensor_copy(attn_hi[:], attn_tm[:])
                    v.drain()
                    v.tensor_tensor(attn_lo[:], attn_tm[:], attn_hi[:], ALU.subtract)
                    v_as.inc(v.drain())

                pend = dict(scof=scof, btop0=t_op.n, vas_mark=v_as.n,
                            vaf_mark=v_af.n - 2, bvres_r=v_res.n)

            if pend is not None:
                @blk.vector
                def _(v, P=pend):
                    emit_resid(v, P)

                @blk.tensor
                def _(te, P=pend):
                    emit_outproj(te, P)
                pend = None

        ngroups = 1 if stage in ('h0', 'a0') else 2
        for g in range(ngroups):
            with ExitStack() as hes:
                emit_hbuild(g, hes)
            if stage == 'h0':
                break
            with ExitStack() as bes:
                bsb = lambda n, s, d: bes.enter_context(
                    nc.sbuf_tensor(n + str(g), s, d))
                T = (bsb("aw_f", [128, 64], F32), bsb("vw_f", [128, 64], F32),
                     bsb("ow_f", [128, E], F32),
                     bsb("aw_hi", [128, 64], BF16), bsb("aw_lo", [128, 64], BF16),
                     bsb("vw_hi", [128, 64], BF16), bsb("vw_lo", [128, 64], BF16),
                     bsb("ow_hi", [128, E], BF16), bsb("ow_lo", [128, E], BF16),
                     bsb("q_hi", [128, 1024], BF16), bsb("q_lo", [128, 1024], BF16),
                     bsb("w32", [128, 4 * 512], F32),
                     bsb("mq_t", [128, 1], F32), bsb("cbias", [128, 4], F32),
                     bsb("csc", [1, 8], F32), bsb("crow", [1, 256], F32),
                     bsb("attn_tm", [128, 512], F32),
                     bsb("attn_hi", [128, 512], BF16),
                     bsb("attn_lo", [128, 512], BF16),
                     bsb("rr_t", [128, 1024], F32),
                     bsb("rhib", [128, 2048], BF16),
                     bsb("fbA_t", [128, 512], F32), bsb("fbB_t", [128, 512], F32),
                     bsb("attnB_s", [128, 512], F32))
                for lp in range(4 if stage != 'a0' else 1):
                    emit_attention(g, lp, T)

    if stage in ('h0', 'a0'):
        with ExitStack() as _tes:
            _dummy = _tes.enter_context(nc.sbuf_tensor("dmy", [128, 8], F32))

            @blk.gpsimd
            def _(gp):
                if stage == 'h0':
                    v_he.w(gp)
                else:
                    v_res.w(gp)
                for t in range(2):
                    d_y.inc(gp.dma_start(out=y_h[t * 128:(t + 1) * 128, :],
                                         in_=x_tm[:, t * E:(t + 1) * E]), 16)
                gp.wait_ge(d_y.h, d_y.n)
        es.close()
        return nc

    if stage == 'attn':
        all_res_a = v_res.n

        @blk.gpsimd
        def _(gp):
            v_res.w(gp, all_res_a)
            for t in range(16):
                d_y.inc(gp.dma_start(out=y_h[t * 128:(t + 1) * 128, :],
                                     in_=x_tm[:, t * E:(t + 1) * E]), 16)
            gp.wait_ge(d_y.h, d_y.n)
        es.close()
        return nc

    with ExitStack() as fes:
        fsb = lambda n, s, d: fes.enter_context(nc.sbuf_tensor(n, s, d))
        f_ln = fsb("f_ln", [128, 2 * E], F32)
        f_xq = fsb("f_xq", [128, 2 * E], F32)
        f_t = fsb("f_t", [128, E], F32)
        f_u = fsb("f_u", [128, E], F32)
        f_v = fsb("f_v", [128, E], F32)
        f_xqf = fsb("f_xqf", [128, 2 * 8 * FSC], BF16)
        f_gu = fsb("f_gu", [128, NOC * FSC], F32)
        f_guq = fsb("f_guq", [128, NOC * FSC], BF16)
        f_wg = fsb("f_wg", [128, 3 * 8 * 128], BF16)
        f_wu = fsb("f_wu", [128, 3 * 8 * 128], BF16)
        f_wd = fsb("f_wd", [128, 2 * 32 * 128], BF16)
        f_do = fsb("f_do", [128, NEC * FSC], F32)
        f_gr = fsb("f_gr", [128, FSC], F32)
        f_si = fsb("f_si", [128, FSC], F32)
        f_fb = fsb("f_fb", [128, FSC], F32)
        f_sb = fsb("f_sb", [128, FSC], F32)
        f_mx = fsb("f_mx", [128, FSC], F32)
        f_mx2 = fsb("f_mx2", [128, FSC], F32)
        f_sc = fsb("f_sc", [128, 16], F32)
        f_st = fsb("f_st", [128, 20], F32)
        f_row = fsb("f_row", [1, 2 * FSC], F32)
        f_rhi = fsb("f_rhi", [1, FSC], BF16)
        f_rlo = fsb("f_rlo", [1, FSC], BF16)
        f_m2r = fsb("f_m2r", [1, 512], F32)
        f_rhi2 = fsb("f_rhi2", [1, FSC], BF16)
        f_rlo2 = fsb("f_rlo2", [1, FSC], BF16)
        f_mgr = fsb("f_mgr", [1, 256], F32)
        f_for = fsb("f_for", [1, 256], F32)
        f_trw = fsb("f_trw", [1, 256], F32)

        pf_g = fes.enter_context(nc.psum_tensor("pf_g", [128, FSC], F32))
        pf_u = fes.enter_context(nc.psum_tensor("pf_u", [128, FSC], F32))
        pf_d0 = fes.enter_context(nc.psum_tensor("pf_d0", [128, FSC], F32))
        pf_d1 = fes.enter_context(nc.psum_tensor("pf_d1", [128, FSC], F32))
        pf_t0 = fes.enter_context(nc.psum_tensor("pf_t0", [128, 128], F32))
        pf_t1 = fes.enter_context(nc.psum_tensor("pf_t1", [128, 128], F32))
        pf_b = fes.enter_context(nc.psum_tensor("pf_b", [128, FSC], F32))
        pf_r = fes.enter_context(nc.psum_tensor("pf_r", [128, 128], F32))

        all_res = v_res.n
        wq_done = o_wq.base()

        def wq_wait(eng):
            for i, s in enumerate(o_wq.s):
                if wq_done[i] > 0:
                    eng.wait_ge(s.h, wq_done[i])

        if stage == 'ffndbg':
            for nm, sh, dt in [("dxq", [128, 2048], F32), ("dsc", [128, 16], F32),
                               ("dst", [128, 2], F32), ("dgu", [128, 8192], F32),
                               ("dguq", [128, 8192], BF16), ("dfb", [128, 256], F32),
                               ("dsb", [128, 256], F32), ("ddo", [128, 2048], F32),
                               ("dxqf", [128, 2048], BF16), ("dln", [128, 2048], F32)]:
                dbg_aps[nm] = nc.dram_tensor(nm, sh, dt, kind="ExternalOutput").ap()

        def snapA():
            return dict(basq=a_f1.n, bvst=v_f1.n, bars=a_f2.n, bvfq=v_fq.n,
                        btxt=t_xt.n, bvxe=v_xe.n, btfr=t_fr.n, bvfr=v_fr.n)

        def A_stats_dve(fcn, fan, fac):
            @blk.vector
            def _(v):
                for st2 in range(2):
                    gst = fcn * 2 + st2
                    cs = st2 * 8
                    a_f1.w(v, fan['basq'] + st2 + 1)
                    if fac is not None:
                        a_f2.w(v, fac['bars'] + 2)
                    v.reduce_sum(f_st[:, cs + 1:cs + 2],
                                 x_tm[:, gst * E:(gst + 1) * E], axis=AX.X)
                    v.drain()
                    v.tensor_single_scalar(f_st[:, cs + 2:cs + 3],
                                           f_st[:, cs + 1:cs + 2], 1.0 / E, ALU.mult)
                    v.drain()
                    v.tensor_tensor(f_st[:, cs + 3:cs + 4], f_st[:, cs + 2:cs + 3],
                                    f_st[:, cs + 2:cs + 3], ALU.mult)
                    v.drain()
                    v_f1.inc(v.scalar_tensor_tensor(
                        f_st[:, cs + 4:cs + 5], f_st[:, cs + 0:cs + 1], 1.0 / E,
                        f_st[:, cs + 3:cs + 4], ALU.mult, ALU.subtract))
                    v.drain()

        def A_quant_dve(fcn, fan, fac):
            @blk.vector
            def _(v):
                s_g.w(v, 1)
                a_f2.w(v, fan['bars'] + 2)
                a_f1.w(v, fan['basq'] + 2)
                if fac is not None:
                    t_fr.w(v, fac['btfr'] + 2)
                    t_xt.w(v, fac['btxt'] + 16)
                for st2 in range(2):
                    gst = fcn * 2 + st2
                    cs = st2 * 8
                    c0 = st2 * 8
                    v.reciprocal(f_st[:, cs + 6:cs + 7], f_st[:, cs + 5:cs + 6])
                    v.drain()
                    v.tensor_scalar(f_ln[:, st2 * E:(st2 + 1) * E],
                                    x_tm[:, gst * E:(gst + 1) * E],
                                    f_st[:, cs + 2:cs + 3], f_st[:, cs + 6:cs + 7],
                                    ALU.subtract, ALU.mult)
                    v_f2.inc(v.drain())
                    v.reduce_max(f_sc[:, c0:c0 + 1], f_ln[:, st2 * E:(st2 + 1) * E],
                                 axis=AX.X, apply_absolute_value=True)
                    v.drain()
                    v.tensor_scalar(f_sc[:, c0 + 1:c0 + 2], f_sc[:, c0:c0 + 1],
                                    1.0, 1e-5, ALU.mult, ALU.max)
                    v.drain()
                    v.reciprocal(f_sc[:, c0 + 2:c0 + 3], f_sc[:, c0 + 1:c0 + 2])
                    v.drain()
                    v.tensor_single_scalar(f_sc[:, c0 + 3:c0 + 4],
                                           f_sc[:, c0 + 2:c0 + 3], 127.0, ALU.mult)
                    v.drain()
                    v.tensor_single_scalar(f_t[:], f_ln[:, st2 * E:(st2 + 1) * E],
                                           f_sc[:, c0 + 3:c0 + 4], ALU.mult)
                    v.drain()
                    v.tensor_scalar(f_u[:], f_t[:], M23, -M23, ALU.add, ALU.add)
                    v.tensor_scalar(f_v[:], f_t[:], M52, -M52, ALU.add, ALU.add)
                    v.drain()
                    v.scalar_tensor_tensor(f_xq[:, st2 * E:(st2 + 1) * E],
                                           f_u[:], h_t[:, 0:1], f_v[:],
                                           ALU.mult, ALU.add)
                    v_fq.inc(v.drain())

        def A_xqT_te(fcn, fan):
            @blk.tensor
            def _(te):
                v_fq.w(te, fan['bvfq'] + 2)
                t_fg.w(te)
                v_dt.w(te)
                for st2 in range(2):
                    for ib in range(8):
                        tid = st2 * 8 + ib
                        if tid >= 2:
                            v_xe.w(te, fan['bvxe'] + tid - 1)
                        t_xt.inc(te.transpose(
                            [pf_t0, pf_t1][tid % 2][:, 0:128],
                            f_xq[:, st2 * E + ib * 128:st2 * E + (ib + 1) * 128],
                            id32[:]))

        def A_xqT_dve(fcn, fan):
            xp = (fcn % 2) * 8 * FSC
            @blk.vector
            def _(v):
                for st2 in range(2):
                    for ib in range(8):
                        tid = st2 * 8 + ib
                        t_xt.w(v, fan['btxt'] + tid + 1)
                        v_xe.inc(v.tensor_copy(
                            f_xqf[:, xp + ib * FSC + st2 * 128:
                                  xp + ib * FSC + (st2 + 1) * 128],
                            [pf_t0, pf_t1][tid % 2][:, 0:128]))

        def A_frT_te(fcn, fan):
            @blk.tensor
            def _(te):
                for st2 in range(2):
                    tid = 16 + st2
                    v_xe.w(te, fan['bvxe'] + tid - 1)
                    t_fr.inc(te.transpose(
                        [pf_t0, pf_t1][tid % 2][0:1, 0:128],
                        f_sc[:, st2 * 8 + 1:st2 * 8 + 2], id32[:]))

        def A_fr_dve(fcn, fan, btbc_cur):
            mp = (fcn % 2) * 256
            @blk.vector
            def _(v):
                if btbc_cur is not None:
                    t_bc.w(v, btbc_cur + 1)
                for st2 in range(2):
                    tid = 16 + st2
                    t_fr.w(v, fan['btfr'] + st2 + 1)
                    v_fr.inc(v.tensor_copy(
                        f_m2r[0:1, mp + st2 * 128:mp + (st2 + 1) * 128],
                        [pf_t0, pf_t1][tid % 2][0:1, 0:128]))
                v.drain()
                v.tensor_scalar(f_row[0:1, 0:FSC], f_m2r[0:1, mp:mp + FSC],
                                wbar[0:1, 0:1], 1.0 / 127, ALU.mult, ALU.mult)
                v.drain()
                v.tensor_copy(f_rhi2[:], f_row[0:1, 0:FSC])
                v.drain()
                v_fr.inc(v.tensor_tensor(f_rlo2[:], f_row[0:1, 0:FSC], f_rhi2[:],
                                         ALU.subtract))

        def emit_gate_prefetch(marks):
            pre = dict(bdfw=d_fw.base())

            @blk.sync
            def _(sy):
                wq_wait(sy)
                v_res.w(sy, all_res)
                for oc in range(3):
                    if marks is not None:
                        t_fg.w(sy, marks[0] + NOC)
                    buf = (oc % 3) * 1024
                    d_fw.inc(sy.dma_start(
                        out=f_wg[:, buf:buf + 1024],
                        in_=wgq_h[:, oc * 1024:(oc + 1) * 1024]), oc)
                    d_fw.inc(sy.dma_start(
                        out=f_wu[:, buf:buf + 1024],
                        in_=wuq_h[:, oc * 1024:(oc + 1) * 1024]), oc)
            return pre

        def emit_down_prefetch(marks):
            pre = dict(bdfd=d_fd.base())

            @blk.sync
            def _(sy):
                for ec in range(2):
                    if marks is not None:
                        t_dn.w(sy, marks[1] + 7 + ec)
                    d_fd.inc(sy.dma_start(
                        out=f_wd[:, (ec % 2) * 4096:(ec % 2 + 1) * 4096],
                        in_=wdq_h[:, ec * 4096:(ec + 1) * 4096]), ec)
            return pre

        fC = snapA()
        wpre_g = emit_gate_prefetch(None)
        wpre_d = emit_down_prefetch(None)

        @blk.scalar
        def _(sc):
            v_res.w(sc, all_res)
            wq_wait(sc)
            for st2 in range(2):
                a_f1.inc(sc.activation(
                    f_ln[:, st2 * E:(st2 + 1) * E],
                    x_tm[:, st2 * E:(st2 + 1) * E], AFT.Square,
                    accum_out=f_st[:, st2 * 8:st2 * 8 + 1]))
                sc.drain()

        A_stats_dve(0, fC, None)

        @blk.scalar
        def _(sc):
            for st2 in range(2):
                cs = st2 * 8
                v_f1.w(sc, fC['bvst'] + st2 + 1)
                a_f2.inc(sc.activation(
                    f_st[:, cs + 5:cs + 6], f_st[:, cs + 4:cs + 5], AFT.Sqrt,
                    bias=eps_t[:, 0:1], scale=1.0))
                sc.drain()

        A_quant_dve(0, fC, None)
        A_xqT_te(0, fC)
        A_xqT_dve(0, fC)
        A_frT_te(0, fC)
        A_fr_dve(0, fC, None)

        for fc in range(NFC if stage != 'ffndbg' else 1):
            btbc, bvbc = t_bc.n, v_bc.n
            btfg, bvgr, basi, bvgu = t_fg.n, v_gr.n, a_si.n, v_gu.n
            btmx, bvmx = t_mx.n, v_mx.n
            btsr, bvsr, btsb, bvsb = t_sr.n, v_sr.n, t_sb.n, v_sb2.n
            btfo, bvfo = t_fo.n, v_fo.n
            bvgq, btdn, bvde = v_gq.n, t_dn.n, v_de.n
            bdfw, bdfd = wpre_g['bdfw'], wpre_d['bdfd']
            btdt, bvdt = t_dt.n, v_dt.n
            prev_dt = v_dt.n
            xp = (fc % 2) * 8 * FSC
            mp = (fc % 2) * 256
            last_fc = (fc == NFC - 1) or stage == 'ffndbg'
            fN = None if last_fc else snapA()

            @blk.tensor
            def _(te):
                v_fr.w(te, fC['bvfr'] + 3)
                v_sb2.w(te)
                if fc >= 1:
                    v_bc.w(te, bvbc)
                te.matmul(pf_b[:, 0:FSC], onesb[0:1, :], f_rhi2[0:1, :],
                          start=True, stop=False)
                t_bc.inc(te.matmul(pf_b[:, 0:FSC], onesb[0:1, :], f_rlo2[0:1, :],
                                   start=False, stop=True))

            @blk.vector
            def _(v):
                t_bc.w(v, btbc + 1)
                v_bc.inc(v.tensor_copy(f_fb[:], pf_b[:, 0:FSC]))
                t_mx.w(v, btmx)
                v.memset(f_mx[:], 0.0)
                v.memset(f_mx2[:], 0.0)
                v_bc.inc(v.drain())

            @blk.sync
            def _(sy):
                for oc in range(3, NOC):
                    t_fg.w(sy, btfg + oc - 2)
                    buf = (oc % 3) * 1024
                    d_fw.inc(sy.dma_start(
                        out=f_wg[:, buf:buf + 1024],
                        in_=wgq_h[:, oc * 1024:(oc + 1) * 1024]), oc)
                    d_fw.inc(sy.dma_start(
                        out=f_wu[:, buf:buf + 1024],
                        in_=wuq_h[:, oc * 1024:(oc + 1) * 1024]), oc)

            if not last_fc:
                wpre_g = emit_gate_prefetch((btfg, btdn))

            @blk.scalar
            def _(sc):
                if fN is not None:
                    v_res.w(sc, all_res)
                    v_f1.w(sc, fC['bvst'] + 2)
                    v_fq.w(sc, fC['bvfq'] + 2)
                    for st2 in range(2):
                        gst = (fc + 1) * 2 + st2
                        a_f1.inc(sc.activation(
                            f_ln[:, st2 * E:(st2 + 1) * E],
                            x_tm[:, gst * E:(gst + 1) * E], AFT.Square,
                            accum_out=f_st[:, st2 * 8:st2 * 8 + 1]))
                        sc.drain()
                for oc in range(NOC):
                    v_gr.w(sc, bvgr + oc + 1)
                    if oc >= 1:
                        v_gu.w(sc, bvgu + oc)
                    a_si.inc(sc.activation(f_si[:], f_gr[:], AFT.Silu))
                    if oc == 24 and fN is not None:
                        for st2 in range(2):
                            cs = st2 * 8
                            v_f1.w(sc, fN['bvst'] + st2 + 1)
                            a_f2.inc(sc.activation(
                                f_st[:, cs + 5:cs + 6], f_st[:, cs + 4:cs + 5],
                                AFT.Sqrt, bias=eps_t[:, 0:1], scale=1.0))
                            sc.drain()

            @blk.tensor
            def _(te):
                v_xe.w(te, fC['bvxe'] + 16)
                if fc >= 1:
                    v_gu.w(te, bvgu)
                for oc in range(NOC):
                    d_fw.w(te, oc, bdfw, per=32)
                    if oc >= 2:
                        v_gr.w(te, bvgr + oc - 1)
                        v_gu.w(te, bvgu + oc - 1)
                    buf = (oc % 3) * 1024
                    pg = [pf_g, pf_d0][oc % 2]
                    pu = [pf_u, pf_d1][oc % 2]
                    for ic in range(8):
                        te.matmul(pg[:, 0:FSC],
                                  f_wg[:, buf + ic * 128:buf + (ic + 1) * 128],
                                  f_xqf[:, xp + ic * FSC:xp + (ic + 1) * FSC],
                                  start=(ic == 0), stop=(ic == 7))
                    last = None
                    for ic in range(8):
                        last = te.matmul(pu[:, 0:FSC],
                                         f_wu[:, buf + ic * 128:buf + (ic + 1) * 128],
                                         f_xqf[:, xp + ic * FSC:xp + (ic + 1) * FSC],
                                         start=(ic == 0), stop=(ic == 7))
                    t_fg.inc(last)

            def fold_iter(v, oc):
                if oc < NOC:
                    t_fg.w(v, btfg + oc + 1)
                    v_bc.w(v, bvbc + 2)
                    if oc >= 1:
                        a_si.w(v, basi + oc)
                    pg = [pf_g, pf_d0][oc % 2]
                    v_gr.inc(v.tensor_tensor(f_gr[:], pg[:, 0:FSC], f_fb[:], ALU.mult))
                    v.drain()
                if oc >= 1:
                    a_si.w(v, basi + oc)
                    pu = [pf_u, pf_d1][(oc - 1) % 2]
                    v.tensor_tensor(f_gu[:, (oc - 1) * FSC:oc * FSC], f_si[:],
                                    pu[:, 0:FSC], ALU.mult)
                    v.drain()
                    v.tensor_tensor(f_mx[:], f_mx[:],
                                    f_gu[:, (oc - 1) * FSC:oc * FSC], ALU.max)
                    v.scalar_tensor_tensor(f_mx2[:],
                                           f_gu[:, (oc - 1) * FSC:oc * FSC],
                                           -1.0, f_mx2[:], ALU.mult, ALU.max)
                    v_gu.inc(v.drain())

            @blk.vector
            def _(v):
                for oc in range(0, 17):
                    fold_iter(v, oc)

            if fN is not None:
                A_stats_dve(fc + 1, fN, fC)

            @blk.vector
            def _(v):
                for oc in range(17, NOC + 1):
                    fold_iter(v, oc)
                v.tensor_tensor(f_mx[:], f_mx[:], f_mx2[:], ALU.max)
                v_gu.inc(v.drain())

            if fN is not None:
                A_quant_dve(fc + 1, fN, fC)

            @blk.tensor
            def _(te):
                v_gu.w(te, bvgu + NOC + 1)
                v_fr.w(te, fC['bvfr'] + 3)
                v_dt.w(te, prev_dt)
                for st2 in range(2):
                    t_mx.inc(te.transpose([pf_t0, pf_t1][st2][:, 0:128],
                                          f_mx[:, st2 * 128:(st2 + 1) * 128], id32[:]))

            @blk.vector
            def _(v):
                for st2 in range(2):
                    t_mx.w(v, btmx + st2 + 1)
                    c0 = st2 * 8
                    v.reduce_max(f_sc[:, c0 + 6:c0 + 7], [pf_t0, pf_t1][st2][:, 0:128],
                                 axis=AX.X)
                v_mx.inc(v.drain())

            @blk.tensor
            def _(te):
                v_mx.w(te, bvmx + 1)
                if fc >= 1:
                    v_fo.w(te, bvfo)
                for st2 in range(2):
                    if st2 >= 1:
                        v_sr.w(te, bvsr + 1)
                    t_sr.inc(te.transpose(pf_r[0:1, 0:128],
                                          f_sc[:, st2 * 8 + 6:st2 * 8 + 7], id32[:]))

            @blk.vector
            def _(v):
                v_dt.w(v, prev_dt)
                for st2 in range(2):
                    t_sr.w(v, btsr + st2 + 1)
                    v_sr.inc(v.tensor_copy(f_mgr[0:1, st2 * 128:(st2 + 1) * 128],
                                           pf_r[0:1, 0:128]))
                v.drain()
                v.tensor_tensor(f_trw[0:1, 0:FSC], f_mgr[0:1, 0:FSC],
                                f_m2r[0:1, mp:mp + FSC], ALU.mult)
                v.drain()
                v.tensor_scalar(f_trw[0:1, 0:FSC], f_trw[0:1, 0:FSC],
                                wbar[0:1, 1:2], 1.0 / 127, ALU.mult, ALU.mult)
                v.drain()
                v.tensor_scalar(f_trw[0:1, 0:FSC], f_trw[0:1, 0:FSC],
                                1.0, 1e-5, ALU.mult, ALU.max)
                v.drain()
                v.tensor_scalar(f_for[0:1, 0:FSC], f_trw[0:1, 0:FSC],
                                wbar[0:1, 2:3], 1.0 / 127, ALU.mult, ALU.mult)
                v.drain()
                v.reciprocal(f_trw[0:1, 0:FSC], f_trw[0:1, 0:FSC])
                v.drain()
                v.tensor_tensor(f_row[0:1, FSC:2 * FSC], f_m2r[0:1, mp:mp + FSC],
                                f_trw[0:1, 0:FSC], ALU.mult)
                v.drain()
                v.tensor_single_scalar(f_row[0:1, FSC:2 * FSC],
                                       f_row[0:1, FSC:2 * FSC],
                                       wbar[0:1, 1:2], ALU.mult)
                v.drain()
                v.tensor_copy(f_rhi[:], f_row[0:1, FSC:2 * FSC])
                v.drain()
                v_sr.inc(v.tensor_tensor(f_rlo[:], f_row[0:1, FSC:2 * FSC], f_rhi[:],
                                         ALU.subtract))

            @blk.tensor
            def _(te):
                v_sr.w(te, bvsr + 3)
                for st2 in range(2):
                    if st2 >= 1:
                        v_fo.w(te, bvfo + 1)
                    t_fo.inc(te.transpose(pf_r[:, 0:1],
                                          f_for[0:1, st2 * 128:(st2 + 1) * 128],
                                          id32[0:1, 0:1]))

            @blk.vector
            def _(v):
                for st2 in range(2):
                    t_fo.w(v, btfo + st2 + 1)
                    v_fo.inc(v.tensor_copy(f_st[:, 18 + st2:19 + st2],
                                           pf_r[:, 0:1]))

            @blk.tensor
            def _(te):
                v_sr.w(te, bvsr + 3)
                v_bc.w(te)
                te.matmul(pf_b[:, 0:FSC], onesb[0:1, :], f_rhi[0:1, :],
                          start=True, stop=False)
                t_sb.inc(te.matmul(pf_b[:, 0:FSC], onesb[0:1, :], f_rlo[0:1, :],
                                   start=False, stop=True))

            @blk.vector
            def _(v):
                t_sb.w(v, btsb + 1)
                v_sb2.inc(v.tensor_copy(f_sb[:], pf_b[:, 0:FSC]))

            @blk.vector
            def _(v):
                v_sb2.w(v, bvsb + 1)
                t_dt.w(v, btdt)
                if fc >= 1:
                    t_dn.w(v, btdn)
                gr2 = f_do[:, 0:512]
                si2 = f_do[:, 512:1024]
                fb2 = f_do[:, 1024:1536]
                sb2 = f_do[:, 1536:2048]
                v.tensor_copy(sb2[:, 0:256], f_sb[:])
                v.tensor_copy(sb2[:, 256:512], f_sb[:])
                v.drain()
                for op in range(NOC // 2):
                    v.tensor_tensor(gr2, f_gu[:, op * 512:(op + 1) * 512],
                                    sb2, ALU.mult)
                    v.drain()
                    v.tensor_scalar(si2, gr2, M23, -M23, ALU.add, ALU.add)
                    v.tensor_scalar(fb2, gr2, M17, c5217[:, 0:1],
                                    ALU.add, ALU.add)
                    v.drain()
                    v.tensor_single_scalar(fb2, fb2, -M52, ALU.add)
                    v.drain()
                    v_gq.inc(v.scalar_tensor_tensor(
                        f_guq[:, op * 512:(op + 1) * 512], si2, h_t[:, 0:1],
                        fb2, ALU.mult, ALU.add))
                    v.drain()

            @blk.sync
            def _(sy):
                for ec in range(2, NEC):
                    t_dn.w(sy, btdn + ec - 1)
                    d_fd.inc(sy.dma_start(
                        out=f_wd[:, (ec % 2) * 4096:(ec % 2 + 1) * 4096],
                        in_=wdq_h[:, ec * 4096:(ec + 1) * 4096]), ec)

            @blk.tensor
            def _(te):
                for ec in range(NEC):
                    d_fd.w(te, ec, bdfd)
                    if ec >= 2:
                        v_de.w(te, bvde + ec - 1)
                    pd = [pf_d0, pf_d1][ec % 2]
                    last = None
                    for ic in range(NOC):
                        if ec == 0:
                            v_gq.w(te, bvgq + ic // 2 + 1)
                        last = te.matmul(
                            pd[:, 0:FSC],
                            f_wd[:, (ec % 2) * 4096 + ic * 128:
                                 (ec % 2) * 4096 + (ic + 1) * 128],
                            f_guq[:, ic * FSC:(ic + 1) * FSC],
                            start=(ic == 0), stop=(ic == NOC - 1))
                    t_dn.inc(last)

            if not last_fc:
                wpre_d = emit_down_prefetch((btfg, btdn))

            @blk.vector
            def _(v):
                for ec in range(NEC):
                    t_dn.w(v, btdn + ec + 1)
                    pd = [pf_d0, pf_d1][ec % 2]
                    v_de.inc(v.tensor_copy(f_do[:, ec * FSC:(ec + 1) * FSC],
                                           pd[:, 0:FSC]))

            @blk.tensor
            def _(te):
                v_de.w(te, bvde + NEC)
                v_mx.w(te, bvmx + 1)
                for ec in range(NEC):
                    for st2 in range(2):
                        tid = ec * 2 + st2
                        if tid >= 2:
                            v_dt.w(te, bvdt + tid - 1)
                        t_dt.inc(te.transpose(
                            [pf_t0, pf_t1][tid % 2][:, 0:128],
                            f_do[:, ec * FSC + st2 * 128:ec * FSC + (st2 + 1) * 128],
                            id32[:]))

            @blk.vector
            def _(v):
                v_fo.w(v, bvfo + 2)
                for ec in range(NEC):
                    for st2 in range(2):
                        tid = ec * 2 + st2
                        t_dt.w(v, btdt + tid + 1)
                        gst = fc * 2 + st2
                        xd = x_tm[:, gst * E + ec * 128:gst * E + (ec + 1) * 128]
                        v_dt.inc(v.scalar_tensor_tensor(
                            xd, [pf_t0, pf_t1][tid % 2][:, 0:128],
                            f_st[:, 18 + st2:19 + st2], xd, ALU.mult, ALU.add))

            @blk.gpsimd
            def _(gp):
                v_dt.w(gp, bvdt + 16)
                for st2 in range(2):
                    gst = fc * 2 + st2
                    d_y.inc(gp.dma_start(out=y_h[gst * 128:(gst + 1) * 128, :],
                                         in_=x_tm[:, gst * E:(gst + 1) * E]), 16)

            if fN is not None:
                A_xqT_te(fc + 1, fN)
                A_xqT_dve(fc + 1, fN)
                A_frT_te(fc + 1, fN)
                A_fr_dve(fc + 1, fN, btbc)
                fC = fN

        if stage == 'ffndbg':
            @blk.gpsimd
            def _(gp):
                gp.wait_ge(d_y.h, d_y.n)
                for (nm, buf, w) in [("dxq", f_xq, 2048),
                                     ("dst", f_st[:, 18:20], 2), ("dgu", f_gu, 8192),
                                     ("dguq", f_guq, 8192), ("dfb", f_fb, 256),
                                     ("dsb", f_sb, 256), ("ddo", f_do, 2048),
                                     ("dxqf", f_xqf, 2048), ("dln", f_ln, 2048)]:
                    src_ap = buf if not hasattr(buf, "shape") or not hasattr(buf, "ap") else buf
                    try:
                        src_ap = buf[:, 0:w]
                    except Exception:
                        src_ap = buf
                    d_y.inc(gp.dma_start(out=dbg_aps[nm], in_=src_ap), 16)
                gp.wait_ge(d_y.h, d_y.n)

        @blk.gpsimd
        def _(gp):
            gp.wait_ge(d_y.h, d_y.n)

        if dbg:
            @blk.gpsimd
            def _(gp):
                gp.dma_start(out=dbg_h[0:128, 0:4096], in_=x_tm[:, 0:4096]).then_inc(d_y.h, 16)
                gp.wait_ge(d_y.h, d_y.n + 16)

    es.close()
    return nc


def kernel(L0_attn_x, L0_attn_proj_weight, L0_attn_attn_weight,
           L0_attn_value_weight, L0_attn_out_w, L0_attn_out_b,
           L0_ffn_w_gate, L0_ffn_w_up, L0_ffn_w_down):
    import ml_dtypes
    x = np.asarray(L0_attn_x, np.float32)
    pw = np.ascontiguousarray(np.asarray(L0_attn_proj_weight, np.float32)
                              .reshape(E, E))
    aw = np.ascontiguousarray(np.asarray(L0_attn_attn_weight, np.float32)
                              .reshape(8 * 128, 64))
    vw = np.ascontiguousarray(np.asarray(L0_attn_value_weight, np.float32)
                              .reshape(8 * 128, 64))
    owT = np.ascontiguousarray(np.asarray(L0_attn_out_w, np.float32).T)
    ob = np.ascontiguousarray(np.asarray(L0_attn_out_b, np.float32)
                              .reshape(1, E))
    def tile_oc(wT, nb):
        K, O = wT.shape
        return np.ascontiguousarray(
            wT.reshape(K // 128, 128, O // 128, 128)
            .transpose(1, 2, 0, 3).reshape(128, (K // 128) * O))
    wgT = tile_oc(np.asarray(L0_ffn_w_gate, np.float32).T, 8)
    wuT = tile_oc(np.asarray(L0_ffn_w_up, np.float32).T, 8)
    wdT = tile_oc(np.asarray(L0_ffn_w_down, np.float32).T, 32)
    id32 = np.eye(128, dtype=np.float32)
    idbf = np.eye(128, dtype=ml_dtypes.bfloat16)
    onesb = np.ones((1, 128), dtype=ml_dtypes.bfloat16)

    nc = build_program()
    in_maps = []
    for c in range(8):
        b, sh = c // 2, c % 2
        xr = np.ascontiguousarray(np.roll(x[b], -2048 * sh, axis=0))
        in_maps.append(dict(x_full=xr, pw=pw, awp=aw, vwp=vw, owT=owT, ob=ob,
                            wgT=wgT, wuT=wuT, wdT=wdT, id32=id32, idbf=idbf,
                            onesb=onesb))
    res = run_bass_kernel_spmd(nc, in_maps, core_ids=list(range(8)))
    y = np.empty((4, S, E), np.float32)
    for c in range(8):
        b, sh = c // 2, c % 2
        y[b, 2048 * sh:2048 * (sh + 1), :] = res.results[c]["y"]
    return y
